# version 01
# baseline (speedup 1.0000x reference)
import numpy as np
import ml_dtypes
from contextlib import ExitStack

import concourse.bass as bass
import concourse.mybir as mybir
from concourse.bass_utils import run_bass_kernel_spmd

F32 = mybir.dt.float32
BF16 = mybir.dt.bfloat16
AF = mybir.ActivationFunctionType
ALU = mybir.AluOpType
AX = mybir.AxisListType.X

NCORES = 8
BATCH = 32
NSEQ = BATCH // NCORES
S = 2048
D = 1024
NT = S // 128
ROWS = NSEQ * S
NH = 16
E = 8
H0 = 2048
I1 = 1024
EPS = 1e-5

CONSTS = [0.0, 1.0, -1.0, EPS, 0.125, 1.702, 7.0, -7.0]


class Sched:

    ENGINES = ["sync", "gpsimd", "tensor", "vector", "scalar"]
    SEMS = ENGINES + ["gdma"]

    def __init__(self, nc, stack):
        self.nc = nc
        self.sems = {n: stack.enter_context(nc.semaphore(f"qsem_{n}"))
                     for n in self.SEMS}
        self.totals = {n: 0 for n in self.SEMS}
        self.prev = None
        self.pending = []
        self.n_items = 0

    def add(self, eng, fn, dma_n=0):
        sem = "gdma" if (eng == "gpsimd" and dma_n) else eng
        self.pending.append((eng, sem, fn, dma_n, self.prev))
        inc = 16 * dma_n if dma_n else 1
        self.totals[sem] += inc
        self.prev = (sem, self.totals[sem])
        self.n_items += 1

    def flush(self):
        if not self.pending:
            return
        per_engine = {n: [] for n in self.ENGINES}
        for it in self.pending:
            per_engine[it[0]].append(it)
        self.pending = []
        sems = self.sems

        def make(name):
            def emitter(e):
                for (_, sem, fn, dma_n, mark) in per_engine[name]:
                    if mark is not None:
                        e.wait_ge(sems[mark[0]], mark[1])
                    ins = fn(e)
                    if dma_n:
                        assert isinstance(ins, (list, tuple)) and len(ins) == dma_n
                        for i in ins:
                            i.then_inc(sems[sem], 16)
                    else:
                        if isinstance(ins, (list, tuple)):
                            ins = ins[-1]
                        ins.then_inc(sems[sem], 1)
            return emitter

        with self.nc.Block() as block:
            for name in self.ENGINES:
                if per_engine[name]:
                    getattr(block, name)(make(name))


class Ctx:
    def __init__(self, nc, stack):
        self.nc = nc
        self.stack = stack
        self.s = Sched(nc, stack)
        sb = lambda n, shp, dt: stack.enter_context(nc.sbuf_tensor(n, shp, dt))
        ps = lambda n, shp, dt: stack.enter_context(nc.psum_tensor(n, shp, dt))
        self.cpool = sb("cpool", [128, len(CONSTS)], F32)
        self.idn = sb("idn_sb", [128, 128], BF16)
        self.onesb = sb("onesb", [1, 128], BF16)
        self.onesf = sb("onesf", [1, 128], F32)
        self.p2048 = ps("p2048", [128, 2048], F32)
        self.pa = ps("pa", [128, 1024], F32)
        self.pb1 = ps("pb1", [128, 512], F32)
        self.ptr = ps("ptr", [128, 512], BF16)
        for i, v in enumerate(CONSTS):
            nc.const_aps.register(F32, v, self.cpool[:, i:i + 1])

    def init_items(self, idn_dram):
        s = self.s
        for i, v in enumerate(CONSTS):
            s.add("vector", lambda e, i=i, v=v: e.memset(self.cpool[:, i:i + 1], v))
        s.add("vector", lambda e: e.memset(self.onesb[:, :], 1.0))
        s.add("vector", lambda e: e.memset(self.onesf[:, :], 1.0))
        s.add("sync", lambda e: [e.dma_start(out=self.idn[:, :], in_=idn_dram)], dma_n=1)

    def sb(self, n, shp, dt, stack=None):
        return (stack or self.stack).enter_context(self.nc.sbuf_tensor(n, shp, dt))


def emit_transposes(ctx, src_sb, src_col0, nchunk, dst_sb, dst_cols):
    s = ctx.s
    for g0 in range(0, nchunk, 4):
        gn = min(4, nchunk - g0)

        def tfn(e, g0=g0, gn=gn):
            out = None
            for j in range(gn):
                c = g0 + j
                out = e.transpose(ctx.ptr[:, j * 128:(j + 1) * 128],
                                  src_sb[:, src_col0 + c * 128: src_col0 + (c + 1) * 128],
                                  ctx.idn[:, :])
            return out
        s.add("tensor", tfn)

        def cfn(e, g0=g0, gn=gn):
            ins = None
            for j in range(gn):
                c = g0 + j
                ins = e.tensor_copy(out=dst_sb[:, dst_cols[c]:dst_cols[c] + 128],
                                    in_=ctx.ptr[:, j * 128:(j + 1) * 128])
            return ins
        s.add("vector", cfn)


def emit_matmul_group(ctx, psum_ap_fn, lhsT_fn, rhs_fn, nks, bias_rhs_fn=None,
                      bias_lhsT=None):
    def fn(e):
        ins = None
        for k in range(nks):
            ins = e.matmul(psum_ap_fn(), lhsT_fn(k), rhs_fn(k),
                           start=(k == 0), stop=(k == nks - 1 and bias_rhs_fn is None))
        if bias_rhs_fn is not None:
            ins = e.matmul(psum_ap_fn(), bias_lhsT, bias_rhs_fn(),
                           start=False, stop=True)
        return ins
    ctx.s.add("tensor", fn)


def emit_ln_stats(ctx, src_ap, sq_scratch, mu, var, rstd, nmr, inv_w):
    s = ctx.s
    s.add("vector", lambda e: e.reduce_sum(mu[:, :], src_ap, axis=AX))
    s.add("scalar", lambda e: e.activation(sq_scratch, src_ap, AF.Square,
                                           accum_out=var[:, :]))
    s.add("vector", lambda e: e.tensor_scalar_mul(mu[:, :], mu[:, :], inv_w))
    s.add("vector", lambda e: e.tensor_scalar_mul(var[:, :], var[:, :], inv_w))
    s.add("vector", lambda e: e.tensor_mul(nmr[:, :], mu[:, :], mu[:, :]))
    s.add("vector", lambda e: e.tensor_sub(var[:, :], var[:, :], nmr[:, :]))
    s.add("scalar", lambda e: e.activation(var[:, :], var[:, :], AF.Sqrt, bias=EPS))
    s.add("vector", lambda e: e.reciprocal(rstd[:, :], var[:, :]))
    s.add("vector", lambda e: e.tensor_mul(nmr[:, :], mu[:, :], rstd[:, :]))
    s.add("vector", lambda e: e.tensor_scalar_mul(nmr[:, :], nmr[:, :], -1.0))


def emit_row_broadcast(ctx, row_ap_fn, dst_sb, width):
    s = ctx.s
    for nb in range(0, width, 512):
        w = min(512, width - nb)
        s.add("tensor", lambda e, nb=nb, w=w: e.matmul(
            ctx.pb1[:, 0:w], ctx.onesf[:, :], row_ap_fn(nb, w), start=True, stop=True))
        s.add("scalar", lambda e, nb=nb, w=w: e.activation(
            dst_sb[:, nb:nb + w], ctx.pb1[:, 0:w], AF.Identity))



def stage_attn(ctx, xin, xout, wdr, has_bias, ch_dram, sh_dram, tag):
    nc, s = ctx.nc, ctx.s
    with ExitStack() as st:
        sb = lambda n, shp, dt: ctx.sb(f"{tag}s_{n}", shp, dt, st)
        wq_sb = sb("wq", [128, 8 * 1024], BF16)
        wk_sb = sb("wk", [128, 8 * 1024], BF16)
        wv_sb = sb("wv", [128, 8 * 1024], BF16)
        QT = sb("QT", [128, 8 * 2048], BF16)
        KT = sb("KT", [128, 8 * 2048], BF16)
        Vg = sb("Vg", [128, 16 * (16 * 65)], BF16)
        Pb = sb("Pb", [128, 2048], BF16)
        ct = sb("ct", [128, 512], F32)
        sth = sb("sth", [128, 512], F32)
        xs = sb("xs", [128, 1024], F32)
        lnb = sb("lnb", [128, 1024], BF16)
        lnT = sb("lnT", [128, 1024], BF16)
        qbf = sb("qbf", [128, 1024], BF16)
        scr = sb("scr", [128, 1024], F32)
        outs = sb("outs", [128, 1024], F32)
        mu = sb("mu", [128, 1], F32)
        var = sb("var", [128, 1], F32)
        rstd = sb("rstd", [128, 1], F32)
        nmr = sb("nmr", [128, 1], F32)
        invs = sb("invs", [1, 512], F32)
        bcs = sb("bcs", [128, 512], F32)
        bob = sb("bob", [128, 1024], F32)
        brow = sb("brow", [1, 3 * 1024], BF16)
        borow = sb("borow", [1, 1024], F32)

        def wload(dst, src):
            s.add("gpsimd", lambda e, dst=dst, src=src: [e.dma_start(
                out=dst.rearrange("p (c j) -> p c j", j=1024),
                in_=src.rearrange("(c p) j -> p c j", p=128))], dma_n=1)
        wload(wq_sb.ap(), wdr["wq"])
        wload(wk_sb.ap(), wdr["wk"])
        wload(wv_sb.ap(), wdr["wv"])
        if has_bias:
            s.add("gpsimd", lambda e: [
                e.dma_start(out=brow[:, 0:1024], in_=wdr["bq"]),
                e.dma_start(out=brow[:, 1024:2048], in_=wdr["bk"]),
                e.dma_start(out=brow[:, 2048:3072], in_=wdr["bv"]),
            ], dma_n=3)
            s.add("sync", lambda e: [e.dma_start(out=borow[:, :], in_=wdr["bo"])], dma_n=1)
            emit_row_broadcast(ctx, lambda nb, w: borow[0:1, nb:nb + w], bob, 1024)
        else:
            s.add("vector", lambda e: e.memset(bob[:, :], 0.0))

        Vg4 = Vg.ap().rearrange("p (kc h j) -> p kc h j", h=16, j=65)

        for sq in range(NSEQ):
            r0s = sq * S
            for t in range(NT):
                r0 = r0s + t * 128
                s.add("sync", lambda e, r0=r0: [e.dma_start(out=xs[:, :], in_=xin[r0:r0 + 128, :])], dma_n=1)
                s.add("sync", lambda e, t=t: [
                    e.dma_start(out=ct[:, :], in_=ch_dram[t * 128:(t + 1) * 128, :]),
                    e.dma_start(out=sth[:, :], in_=sh_dram[t * 128:(t + 1) * 128, :]),
                ], dma_n=2)
                emit_ln_stats(ctx, xs[:, :], scr[:, :], mu, var, rstd, nmr, 1.0 / 1024.0)
                s.add("vector", lambda e: e.tensor_scalar(
                    out=lnb[:, :], in0=xs[:, :], scalar1=mu[:, :], scalar2=rstd[:, :],
                    op0=ALU.subtract, op1=ALU.mult))
                emit_transposes(ctx, lnb, 0, 8, lnT, [c * 128 for c in range(8)])
                for (dst0, w_sb, boff) in ((0, wq_sb, 0), (1024, wk_sb, 1024)):
                    for nb in range(2):
                        emit_matmul_group(
                            ctx,
                            lambda dst0=dst0, nb=nb: ctx.p2048[:, dst0 + nb * 512: dst0 + (nb + 1) * 512],
                            lambda c: lnT[:, c * 128:(c + 1) * 128],
                            lambda c, w_sb=w_sb, nb=nb: w_sb[:, c * 1024 + nb * 512: c * 1024 + (nb + 1) * 512],
                            8,
                            bias_rhs_fn=(lambda boff=boff, nb=nb: brow[0:1, boff + nb * 512: boff + (nb + 1) * 512]) if has_bias else None,
                            bias_lhsT=ctx.onesb[:, :] if has_bias else None)
                for nb in range(2):
                    emit_matmul_group(
                        ctx,
                        lambda nb=nb: ctx.pa[:, nb * 512:(nb + 1) * 512],
                        lambda c: lnT[:, c * 128:(c + 1) * 128],
                        lambda c, nb=nb: wv_sb[:, c * 1024 + nb * 512: c * 1024 + (nb + 1) * 512],
                        8,
                        bias_rhs_fn=(lambda nb=nb: brow[0:1, 2048 + nb * 512: 2048 + (nb + 1) * 512]) if has_bias else None,
                        bias_lhsT=ctx.onesb[:, :] if has_bias else None)
                for (half, dstT) in ((0, QT), (1, KT)):
                    pe = ctx.p2048.ap().rearrange("p (c j two) -> p c j two", c=2, two=2)[:, half, :, 0]
                    po = ctx.p2048.ap().rearrange("p (c j two) -> p c j two", c=2, two=2)[:, half, :, 1]
                    de = qbf.ap().rearrange("p (j two) -> p j two", two=2)[:, :, 0]
                    do = qbf.ap().rearrange("p (j two) -> p j two", two=2)[:, :, 1]
                    s.add("vector", lambda e, pe=pe: e.tensor_mul(scr[:, 0:512], pe, ct[:, :]))
                    s.add("vector", lambda e, po=po: e.tensor_mul(scr[:, 512:1024], po, sth[:, :]))
                    s.add("gpsimd", lambda e, de=de: e.tensor_tensor(
                        out=de, in0=scr[:, 0:512], in1=scr[:, 512:1024], op=ALU.subtract))
                    s.add("vector", lambda e, po=po: e.tensor_mul(scr[:, 0:512], po, ct[:, :]))
                    s.add("vector", lambda e, pe=pe: e.tensor_mul(scr[:, 512:1024], pe, sth[:, :]))
                    s.add("gpsimd", lambda e, do=do: e.tensor_tensor(
                        out=do, in0=scr[:, 0:512], in1=scr[:, 512:1024], op=ALU.add))
                    emit_transposes(ctx, qbf, 0, 8, dstT, [c * 2048 + t * 128 for c in range(8)])
                s.add("vector", lambda e, t=t: e.memset(Vg4[:, t, :, 64:65], 1.0))
                s.add("vector", lambda e, t=t: e.tensor_copy(
                    out=Vg4[:, t, :, 0:64],
                    in_=ctx.pa.ap().rearrange("p (h j) -> p h j", j=64)))

            for qb in range(4):
                for h in range(NH):
                    c = h // 2
                    r = 64 * (h % 2)
                    for g in range(4):
                        def sfn(e, g=g, c=c, r=r, qb=qb):
                            ins = None
                            for j in range(4):
                                kc = g * 4 + j
                                ins = e.matmul(
                                    ctx.p2048[:, j * 512:(j + 1) * 512],
                                    KT[r:r + 64, c * 2048 + kc * 128: c * 2048 + (kc + 1) * 128],
                                    QT[r:r + 64, c * 2048 + qb * 512: c * 2048 + (qb + 1) * 512],
                                    start=True, stop=True)
                            return ins
                        s.add("tensor", sfn)
                        s.add("scalar", lambda e: e.activation(
                            Pb[:, :], ctx.p2048[:, :], AF.Exp, scale=0.125))
                        def pvfn(e, g=g, h=h):
                            ins = None
                            for j in range(4):
                                kc = g * 4 + j
                                ins = e.matmul(
                                    ctx.pb1[0:65, :],
                                    Vg[:, kc * 1040 + h * 65: kc * 1040 + (h + 1) * 65],
                                    Pb[:, j * 512:(j + 1) * 512],
                                    start=(kc == 0), stop=(kc == 15))
                            return ins
                        s.add("tensor", pvfn)
                    s.add("vector", lambda e: e.reciprocal(invs[:, :], ctx.pb1[64:65, :]))
                    s.add("tensor", lambda e: e.matmul(
                        ctx.pa[:, 0:512], ctx.onesf[:, :], invs[:, :], start=True, stop=True))
                    s.add("scalar", lambda e: e.activation(bcs[:, :], ctx.pa[:, 0:512], AF.Identity))
                    s.add("vector", lambda e, c=c, r=r, qb=qb: e.tensor_tensor(
                        out=QT[r:r + 64, c * 2048 + qb * 512: c * 2048 + (qb + 1) * 512],
                        in0=ctx.pb1[0:64, :], in1=bcs[0:64, :], op=ALU.mult))

            s.add("gpsimd", lambda e: [e.dma_start(
                out=Vg.ap()[:, 0:8192].rearrange("p (c j) -> p c j", j=1024),
                in_=wdr["wo"].rearrange("(c p) j -> p c j", p=128))], dma_n=1)
            for t in range(NT):
                r0 = r0s + t * 128
                s.add("sync", lambda e, r0=r0: [e.dma_start(out=xs[:, :], in_=xin[r0:r0 + 128, :])], dma_n=1)
                for nb in range(2):
                    emit_matmul_group(
                        ctx,
                        lambda nb=nb: ctx.pa[:, nb * 512:(nb + 1) * 512],
                        lambda c, t=t: QT[:, c * 2048 + t * 128: c * 2048 + (t + 1) * 128],
                        lambda c, nb=nb: Vg[:, c * 1024 + nb * 512: c * 1024 + (nb + 1) * 512],
                        8)
                s.add("vector", lambda e: e.scalar_tensor_tensor(
                    out=scr[:, :], in0=ctx.pa[:, :], scalar=1.0, in1=bob[:, :],
                    op0=ALU.mult, op1=ALU.add))
                s.add("vector", lambda e: e.tensor_add(outs[:, :], scr[:, :], xs[:, :]))
                s.add("gpsimd", lambda e, r0=r0: [e.dma_start(out=xout[r0:r0 + 128, :], in_=outs[:, :])], dma_n=1)
        s.flush()



def stage_ffn(ctx, xin, xout, dr, kind, tag):
    nc, s = ctx.nc, ctx.s
    TB = 8
    NB = (NSEQ * NT) // TB
    NC2 = 16 if kind == 0 else 8
    with ExitStack() as st:
        sb = lambda n, shp, dt: ctx.sb(f"{tag}s_{n}", shp, dt, st)
        w1_sb = sb("w1", [128, 8 * H0], BF16)
        w2_sb = sb("w2", [128, NC2 * 1024], BF16)
        uT = sb("uT", [128, 8 * (TB * 128)], BF16)
        acc = sb("acc", [128, TB * 1024], F32)
        cw = sb("cw", [128, TB * 8], F32)
        gwrow = sb("gwrow", [128, 64], BF16)
        grbias = sb("grbias", [1, 8], BF16)
        b1e = sb("b1e", [1, H0], BF16)
        b2e = sb("b2e", [1, 1024], BF16)
        grow_a = sb("grow_a", [1, H0], F32)
        grow_b = sb("grow_b", [1, H0], F32)
        gb = sb("gb", [128, H0], F32)
        bb = sb("bb", [128, H0], F32)
        lngb = sb("lngb", [128, 1024], F32) if kind == 0 else None
        lnbb = sb("lnbb", [128, 1024], F32) if kind == 0 else None
        xs = sb("xs", [128, 1024], F32)
        ub = sb("ub", [128, 1024], BF16)
        ar1 = sb("ar1", [128, 2048], F32)
        ar2 = sb("ar2", [128, 2048], F32)
        hb = sb("hb", [128, 2048], BF16)
        hT = sb("hT", [128, 2048], BF16)
        mu = sb("mu", [128, 1], F32)
        var = sb("var", [128, 1], F32)
        rstd = sb("rstd", [128, 1], F32)
        nmr = sb("nmr", [128, 1], F32)
        m1 = sb("m1", [128, 1], F32)
        m2 = sb("m2", [128, 1], F32)
        gsum = sb("gsum", [128, 1], F32)
        gates = sb("gates", [128, 8], F32)
        msk = sb("msk", [128, 8], F32)
        gm = sb("gm", [128, 8], F32)

        wbf1 = nc.dram_tensor(f"{tag}_wbf1", [E * 1024, H0], BF16).ap()
        w2rows = 2048 if kind == 0 else 1024
        wbf2 = nc.dram_tensor(f"{tag}_wbf2", [E * w2rows, 1024], BF16).ap()

        def preconv(src, dst, rows, cols):
            for r in range(0, rows, 128):
                s.add("sync", lambda e, r=r: [e.dma_start(out=ar1[:, 0:cols], in_=src[r:r + 128, :])], dma_n=1)
                s.add("vector", lambda e: e.tensor_copy(out=hb[:, 0:cols], in_=ar1[:, 0:cols]))
                s.add("gpsimd", lambda e, r=r: [e.dma_start(out=dst[r:r + 128, :], in_=hb[:, 0:cols])], dma_n=1)
        preconv(dr["w1"], wbf1, E * 1024, H0)
        preconv(dr["w2"], wbf2, E * w2rows, 1024)

        s.add("gpsimd", lambda e: [e.dma_start(
            out=gwrow.ap().rearrange("p (c j) -> p c j", j=8),
            in_=dr["gate_w"].rearrange("(c p) j -> p c j", p=128))], dma_n=1)
        if kind == 0:
            s.add("sync", lambda e: [
                e.dma_start(out=grow_a[:, 0:1024], in_=dr["ln_g"]),
                e.dma_start(out=grow_b[:, 0:1024], in_=dr["ln_b"])], dma_n=2)
            emit_row_broadcast(ctx, lambda nb, w: grow_a[0:1, nb:nb + w], lngb, 1024)
            emit_row_broadcast(ctx, lambda nb, w: grow_b[0:1, nb:nb + w], lnbb, 1024)
        else:
            s.add("gpsimd", lambda e: [e.dma_start(out=grbias[:, :], in_=dr["router_bias"])], dma_n=1)

        for b in range(NB):
            t0 = b * TB
            for t in range(TB):
                r0 = (t0 + t) * 128
                s.add("sync", lambda e, r0=r0: [e.dma_start(out=xs[:, :], in_=xin[r0:r0 + 128, :])], dma_n=1)
                emit_ln_stats(ctx, xs[:, :], ar1[:, 0:1024], mu, var, rstd, nmr, 1.0 / 1024.0)
                s.add("vector", lambda e: e.tensor_scalar(
                    out=ub[:, :], in0=xs[:, :], scalar1=mu[:, :], scalar2=rstd[:, :],
                    op0=ALU.subtract, op1=ALU.mult))
                emit_transposes(ctx, ub, 0, 8, uT, [c * TB * 128 + t * 128 for c in range(8)])
                s.add("vector", lambda e, t=t: e.memset(acc[:, t * 1024:(t + 1) * 1024], 0.0))

                def glog(e, t=t):
                    ins = None
                    for c in range(8):
                        ins = e.matmul(ctx.pb1[:, 0:8],
                                       uT[:, c * TB * 128 + t * 128: c * TB * 128 + (t + 1) * 128],
                                       gwrow[:, c * 8:(c + 1) * 8],
                                       start=(c == 0), stop=(c == 7 and kind == 0))
                    if kind == 1:
                        ins = e.matmul(ctx.pb1[:, 0:8], ctx.onesb[:, :], grbias[:, :],
                                       start=False, stop=True)
                    return ins
                s.add("tensor", glog)
                s.add("vector", lambda e: e.reduce_max(m1[:, :], ctx.pb1[:, 0:8], axis=AX))
                s.add("vector", lambda e: e.tensor_scalar_mul(m2[:, :], m1[:, :], -1.0))
                s.add("scalar", lambda e: e.activation(gates[:, :], ctx.pb1[:, 0:8], AF.Exp,
                                                       bias=m2[:, :], accum_out=gsum[:, :]))
                s.add("vector", lambda e: e.reciprocal(gsum[:, :], gsum[:, :]))
                s.add("vector", lambda e: e.tensor_scalar_mul(gates[:, :], gates[:, :], gsum[:, :]))
                s.add("vector", lambda e: e.reduce_max(m1[:, :], gates[:, :], axis=AX))
                s.add("vector", lambda e: e.tensor_scalar(
                    out=msk[:, :], in0=gates[:, :], scalar1=m1[:, :], scalar2=None, op0=ALU.is_ge))
                s.add("vector", lambda e: e.scalar_tensor_tensor(
                    out=gm[:, :], in0=msk[:, :], scalar=-1e30, in1=gates[:, :],
                    op0=ALU.mult, op1=ALU.add))
                s.add("vector", lambda e: e.reduce_max(m2[:, :], gm[:, :], axis=AX))
                s.add("vector", lambda e: e.tensor_scalar(
                    out=msk[:, :], in0=gates[:, :], scalar1=m2[:, :], scalar2=None, op0=ALU.is_ge))
                s.add("vector", lambda e, t=t: e.tensor_mul(cw[:, t * 8:(t + 1) * 8], gates[:, :], msk[:, :]))
                if kind == 0:
                    s.add("vector", lambda e: e.tensor_add(m1[:, :], m1[:, :], m2[:, :]))
                    s.add("vector", lambda e: e.reciprocal(m1[:, :], m1[:, :]))
                    s.add("vector", lambda e, t=t: e.tensor_scalar_mul(
                        cw[:, t * 8:(t + 1) * 8], cw[:, t * 8:(t + 1) * 8], m1[:, :]))

            for ex in range(E):
                s.add("sync", lambda e, ex=ex: [
                    e.dma_start(out=w1_sb.ap().rearrange("p (c j) -> p c j", j=H0),
                                in_=wbf1.rearrange("(ee c p) j -> ee p c j", ee=E, p=128)[ex]),
                    e.dma_start(out=w2_sb.ap().rearrange("p (c j) -> p c j", j=1024),
                                in_=wbf2.rearrange("(ee c p) j -> ee p c j", ee=E, p=128)[ex]),
                ], dma_n=2)
                s.add("gpsimd", lambda e, ex=ex: [
                    e.dma_start(out=b1e[:, :], in_=dr["b1"][ex:ex + 1, :]),
                    e.dma_start(out=b2e[:, :], in_=dr["b2"][ex:ex + 1, :]),
                ], dma_n=2)
                if kind == 0:
                    s.add("sync", lambda e, ex=ex: [
                        e.dma_start(out=grow_a[:, :], in_=dr["ln1_g"][ex:ex + 1, :]),
                        e.dma_start(out=grow_b[:, :], in_=dr["ln1_b"][ex:ex + 1, :])], dma_n=2)
                    emit_row_broadcast(ctx, lambda nb, w: grow_a[0:1, nb:nb + w], gb, H0)
                    emit_row_broadcast(ctx, lambda nb, w: grow_b[0:1, nb:nb + w], bb, H0)
                for t in range(TB):
                    for nb in range(4):
                        emit_matmul_group(
                            ctx,
                            lambda nb=nb: ctx.p2048[:, nb * 512:(nb + 1) * 512],
                            lambda c, t=t: uT[:, c * TB * 128 + t * 128: c * TB * 128 + (t + 1) * 128],
                            lambda c, nb=nb: w1_sb[:, c * H0 + nb * 512: c * H0 + (nb + 1) * 512],
                            8,
                            bias_rhs_fn=lambda nb=nb: b1e[0:1, nb * 512:(nb + 1) * 512],
                            bias_lhsT=ctx.onesb[:, :])
                    if kind == 0:
                        emit_ln_stats(ctx, ctx.p2048[:, :], ar1[:, :], mu, var, rstd, nmr, 1.0 / 2048.0)
                        s.add("scalar", lambda e: e.activation(ar1[:, :], ctx.p2048[:, :], AF.Identity,
                                                               bias=nmr[:, :], scale=rstd[:, :]))
                        s.add("vector", lambda e: e.tensor_mul(ar2[:, :], ar1[:, :], gb[:, :]))
                        s.add("gpsimd", lambda e: e.tensor_add(ar1[:, :], ar2[:, :], bb[:, :]))
                        s.add("scalar", lambda e: e.activation(hb[:, :], ar1[:, :], AF.Gelu))
                        emit_transposes(ctx, hb, 0, 16, hT, [c * 128 for c in range(16)])
                    else:
                        s.add("vector", lambda e: e.tensor_scalar(
                            out=ar1[:, 0:1024], in0=ctx.p2048[:, 0:1024], scalar1=7.0,
                            scalar2=None, op0=ALU.min))
                        s.add("vector", lambda e: e.tensor_scalar(
                            out=ar1[:, 1024:2048], in0=ctx.p2048[:, 1024:2048], scalar1=-7.0,
                            scalar2=7.0, op0=ALU.max, op1=ALU.min))
                        s.add("scalar", lambda e: e.activation(ar2[:, 0:1024], ar1[:, 0:1024],
                                                               AF.Sigmoid, scale=1.702))
                        s.add("vector", lambda e: e.tensor_mul(ar2[:, 1024:2048], ar1[:, 0:1024], ar2[:, 0:1024]))
                        s.add("vector", lambda e: e.scalar_tensor_tensor(
                            out=hb[:, 0:1024], in0=ar1[:, 1024:2048], scalar=1.0,
                            in1=ar2[:, 1024:2048], op0=ALU.add, op1=ALU.mult))
                        emit_transposes(ctx, hb, 0, 8, hT, [c * 128 for c in range(8)])
                    for nb in range(2):
                        emit_matmul_group(
                            ctx,
                            lambda nb=nb: ctx.pa[:, nb * 512:(nb + 1) * 512],
                            lambda c: hT[:, c * 128:(c + 1) * 128],
                            lambda c, nb=nb: w2_sb[:, c * 1024 + nb * 512: c * 1024 + (nb + 1) * 512],
                            NC2,
                            bias_rhs_fn=lambda nb=nb: b2e[0:1, nb * 512:(nb + 1) * 512],
                            bias_lhsT=ctx.onesb[:, :])
                    s.add("vector", lambda e, t=t, ex=ex: e.scalar_tensor_tensor(
                        out=acc[:, t * 1024:(t + 1) * 1024], in0=ctx.pa[:, :],
                        scalar=cw[:, t * 8 + ex: t * 8 + ex + 1],
                        in1=acc[:, t * 1024:(t + 1) * 1024], op0=ALU.mult, op1=ALU.add))

            for t in range(TB):
                r0 = (t0 + t) * 128
                s.add("sync", lambda e, r0=r0: [e.dma_start(out=xs[:, :], in_=xin[r0:r0 + 128, :])], dma_n=1)
                if kind == 0:
                    emit_ln_stats(ctx, xs[:, :], ar1[:, 0:1024], mu, var, rstd, nmr, 1.0 / 1024.0)
                    s.add("vector", lambda e: e.tensor_scalar(
                        out=ar1[:, 0:1024], in0=xs[:, :], scalar1=mu[:, :], scalar2=rstd[:, :],
                        op0=ALU.subtract, op1=ALU.mult))
                    s.add("vector", lambda e, t=t: e.tensor_add(
                        ar1[:, 1024:2048], acc[:, t * 1024:(t + 1) * 1024], ar1[:, 0:1024]))
                    emit_ln_stats(ctx, ar1[:, 1024:2048], ar2[:, 0:1024], mu, var, rstd, nmr, 1.0 / 1024.0)
                    s.add("scalar", lambda e: e.activation(ar2[:, 0:1024], ar1[:, 1024:2048], AF.Identity,
                                                           bias=nmr[:, :], scale=rstd[:, :]))
                    s.add("vector", lambda e: e.tensor_mul(ar2[:, 1024:2048], ar2[:, 0:1024], lngb[:, :]))
                    s.add("gpsimd", lambda e: e.tensor_add(ar1[:, 0:1024], ar2[:, 1024:2048], lnbb[:, :]))
                    s.add("vector", lambda e: e.tensor_add(ar1[:, 1024:2048], ar1[:, 0:1024], xs[:, :]))
                else:
                    s.add("vector", lambda e, t=t: e.tensor_add(
                        ar1[:, 1024:2048], acc[:, t * 1024:(t + 1) * 1024], xs[:, :]))
                s.add("gpsimd", lambda e, r0=r0: [e.dma_start(out=xout[r0:r0 + 128, :], in_=ar1[:, 1024:2048])], dma_n=1)
        s.flush()



def host_tables():
    pos = np.arange(S, dtype=np.float64)[:, None]
    i = np.arange(32, dtype=np.float64)
    f0 = 10000.0 ** (2.0 * i / 64.0)
    a0 = pos / f0[None, :]
    ch0 = np.tile(np.cos(a0), (1, 16)).astype(np.float32)
    sh0 = np.tile(np.sin(a0), (1, 16)).astype(np.float32)
    cs, ss = [], []
    for sc in (1.0, 4.0, 16.0):
        f = 1.0 / (10000.0 * sc) ** (2.0 * i / 64.0)
        a = pos * f[None, :]
        cs.append(np.cos(a))
        ss.append(np.sin(a))
    ch1 = np.tile(np.mean(cs, axis=0), (1, 16)).astype(np.float32)
    sh1 = np.tile(np.mean(ss, axis=0), (1, 16)).astype(np.float32)
    return ch0, sh0, ch1, sh1


def build_stage_program(which):
    nc = bass.Bass("TRN2", target_bir_lowering=False)
    xin = nc.dram_tensor("xin", [ROWS, D], F32, kind="ExternalInput").ap()
    xout = nc.dram_tensor("xout", [ROWS, D], F32, kind="ExternalOutput").ap()
    specs = [("idn", (128, 128), BF16)]
    if which == "a0":
        specs += [(n, (D, D), F32) for n in ("a0_wq", "a0_wk", "a0_wv", "a0_wo")]
        specs += [(n, (1, D), F32) for n in ("a0_bq", "a0_bk", "a0_bv", "a0_bo")]
        specs += [("ch0", (S, 512), F32), ("sh0", (S, 512), F32)]
    if which == "f0":
        specs += [("f0_gate_w", (D, E), F32), ("f0_w1", (E * D, H0), F32),
                  ("f0_b1", (E, H0), F32), ("f0_ln1_g", (E, H0), F32),
                  ("f0_ln1_b", (E, H0), F32), ("f0_w2", (E * H0, D), F32),
                  ("f0_b2", (E, D), F32), ("f0_ln_g", (1, D), F32), ("f0_ln_b", (1, D), F32)]
    if which == "a1":
        specs += [(n, (D, D), F32) for n in ("a1_wq", "a1_wk", "a1_wv", "a1_wo")]
        specs += [("ch1", (S, 512), F32), ("sh1", (S, 512), F32)]
    if which == "f1":
        specs += [("f1_gate_w", (D, E), F32), ("f1_rbias", (1, E), F32),
                  ("f1_w1", (E * D, H0), F32), ("f1_b1", (E, H0), F32),
                  ("f1_w2", (E * I1, D), F32), ("f1_b2", (E, D), F32)]
    dr = {}
    for name, shape, dt in specs:
        dr[name] = nc.dram_tensor(name, list(shape), dt, kind="ExternalInput").ap()

    with ExitStack() as stack:
        ctx = Ctx(nc, stack)
        ctx.init_items(dr["idn"])
        ctx.s.flush()
        if which == "a0":
            stage_attn(ctx, xin, xout,
                       dict(wq=dr["a0_wq"], wk=dr["a0_wk"], wv=dr["a0_wv"], wo=dr["a0_wo"],
                            bq=dr["a0_bq"], bk=dr["a0_bk"], bv=dr["a0_bv"], bo=dr["a0_bo"]),
                       True, dr["ch0"], dr["sh0"], "a0")
        if which == "f0":
            stage_ffn(ctx, xin, xout,
                      dict(gate_w=dr["f0_gate_w"], w1=dr["f0_w1"], b1=dr["f0_b1"],
                           ln1_g=dr["f0_ln1_g"], ln1_b=dr["f0_ln1_b"], w2=dr["f0_w2"],
                           b2=dr["f0_b2"], ln_g=dr["f0_ln_g"], ln_b=dr["f0_ln_b"]),
                      0, "f0")
        if which == "a1":
            stage_attn(ctx, xin, xout,
                       dict(wq=dr["a1_wq"], wk=dr["a1_wk"], wv=dr["a1_wv"], wo=dr["a1_wo"]),
                       False, dr["ch1"], dr["sh1"], "a1")
        if which == "f1":
            stage_ffn(ctx, xin, xout,
                      dict(gate_w=dr["f1_gate_w"], router_bias=dr["f1_rbias"],
                           w1=dr["f1_w1"], b1=dr["f1_b1"], w2=dr["f1_w2"], b2=dr["f1_b2"]),
                      1, "f1")
        ctx.s.add("vector", lambda e: e.engine_nop())
        ctx.s.flush()
    return nc


def _prep_host(inputs):
    ch0, sh0, ch1, sh1 = host_tables()
    c = lambda a: np.ascontiguousarray(np.asarray(a), dtype=np.float32) if not isinstance(a, np.ndarray) or a.dtype != np.float32 or not a.flags.c_contiguous else a
    def cc(a):
        return np.ascontiguousarray(np.asarray(a, dtype=np.float32))
    g = {
        "idn": np.eye(128, dtype=ml_dtypes.bfloat16),
        "ch0": ch0, "sh0": sh0, "ch1": ch1, "sh1": sh1,
        "a0_wq": cc(np.asarray(inputs["L0_attn_wq"]).T),
        "a0_wk": cc(np.asarray(inputs["L0_attn_wk"]).T),
        "a0_wv": cc(np.asarray(inputs["L0_attn_wv"]).T),
        "a0_wo": cc(np.asarray(inputs["L0_attn_wo"]).T),
        "a0_bq": cc(inputs["L0_attn_bq"]).reshape(1, D),
        "a0_bk": cc(inputs["L0_attn_bk"]).reshape(1, D),
        "a0_bv": cc(inputs["L0_attn_bv"]).reshape(1, D),
        "a0_bo": cc(inputs["L0_attn_bo"]).reshape(1, D),
        "f0_gate_w": cc(inputs["L0_ffn_gate_w"]),
        "f0_w1": cc(inputs["L0_ffn_w1"]).reshape(E * D, H0),
        "f0_b1": cc(inputs["L0_ffn_b1"]),
        "f0_ln1_g": cc(inputs["L0_ffn_ln1_g"]),
        "f0_ln1_b": cc(inputs["L0_ffn_ln1_b"]),
        "f0_w2": cc(inputs["L0_ffn_w2"]).reshape(E * H0, D),
        "f0_b2": cc(inputs["L0_ffn_b2"]),
        "f0_ln_g": cc(inputs["L0_ffn_ln_g"]).reshape(1, D),
        "f0_ln_b": cc(inputs["L0_ffn_ln_b"]).reshape(1, D),
        "a1_wq": cc(inputs["L1_attn_Wq"]),
        "a1_wk": cc(inputs["L1_attn_Wk"]),
        "a1_wv": cc(inputs["L1_attn_Wv"]),
        "a1_wo": cc(inputs["L1_attn_Wo"]),
        "f1_gate_w": cc(np.asarray(inputs["L1_ffn_router_weight"]).T),
        "f1_rbias": cc(inputs["L1_ffn_router_bias"]).reshape(1, E),
        "f1_w1": cc(inputs["L1_ffn_gate_up_proj"]).reshape(E * D, 2 * I1),
        "f1_b1": cc(inputs["L1_ffn_gate_up_bias"]),
        "f1_w2": cc(inputs["L1_ffn_down_proj"]).reshape(E * I1, D),
        "f1_b2": cc(inputs["L1_ffn_down_bias"]),
    }
    return g


_STAGE_KEYS = {
    "a0": ["idn", "a0_wq", "a0_wk", "a0_wv", "a0_wo", "a0_bq", "a0_bk", "a0_bv",
           "a0_bo", "ch0", "sh0"],
    "f0": ["idn", "f0_gate_w", "f0_w1", "f0_b1", "f0_ln1_g", "f0_ln1_b", "f0_w2",
           "f0_b2", "f0_ln_g", "f0_ln_b"],
    "a1": ["idn", "a1_wq", "a1_wk", "a1_wv", "a1_wo", "ch1", "sh1"],
    "f1": ["idn", "f1_gate_w", "f1_rbias", "f1_w1", "f1_b1", "f1_w2", "f1_b2"],
}


def kernel(**inputs):
    g = _prep_host(inputs)
    x = np.ascontiguousarray(np.asarray(inputs["L0_attn_x"], dtype=np.float32)
                             ).reshape(BATCH * S, D)
    shards = [np.ascontiguousarray(x[c * ROWS:(c + 1) * ROWS]) for c in range(NCORES)]

    for which in ("a0", "f0", "a1", "f1"):
        nc = build_stage_program(which)
        in_maps = []
        for c in range(NCORES):
            m = {k: g[k] for k in _STAGE_KEYS[which]}
            m["xin"] = shards[c]
            in_maps.append(m)
        res = run_bass_kernel_spmd(nc, in_maps, core_ids=list(range(NCORES)))
        shards = [np.ascontiguousarray(np.asarray(r["xout"], dtype=np.float32))
                  for r in res.results]

    out = np.concatenate(shards, axis=0).reshape(BATCH, S, D)
    return out.astype(np.float32)


# version 02
# speedup vs baseline: 1.0016x; 1.0016x over previous
import numpy as np
import ml_dtypes
from contextlib import ExitStack

import concourse.bass as bass
import concourse.mybir as mybir
from concourse.bass_utils import run_bass_kernel_spmd

F32 = mybir.dt.float32
BF16 = mybir.dt.bfloat16
AF = mybir.ActivationFunctionType
ALU = mybir.AluOpType
AX = mybir.AxisListType.X

NCORES = 8
BATCH = 32
NSEQ = BATCH // NCORES
S = 2048
D = 1024
NT = S // 128
ROWS = NSEQ * S
NH = 16
E = 8
H0 = 2048
I1 = 1024
EPS = 1e-5

CONSTS = [0.0, 1.0, -1.0, EPS, 0.125, 1.702, 7.0, -7.0]


class Sched:

    ENGINES = ["sync", "gpsimd", "tensor", "vector", "scalar"]
    SEMS = ENGINES + ["gdma"]

    def __init__(self, nc, stack):
        self.nc = nc
        self.sems = {n: stack.enter_context(nc.semaphore(f"qsem_{n}"))
                     for n in self.SEMS}
        self.totals = {n: 0 for n in self.SEMS}
        self.prev = None
        self.pending = []
        self.n_items = 0

    def add(self, eng, fn, dma_n=0):
        sem = "gdma" if (eng == "gpsimd" and dma_n) else eng
        self.pending.append((eng, sem, fn, dma_n, self.prev))
        inc = 16 * dma_n if dma_n else 1
        self.totals[sem] += inc
        self.prev = (sem, self.totals[sem])
        self.n_items += 1

    def flush(self):
        if not self.pending:
            return
        per_engine = {n: [] for n in self.ENGINES}
        for it in self.pending:
            per_engine[it[0]].append(it)
        self.pending = []
        sems = self.sems

        def make(name):
            def emitter(e):
                for (_, sem, fn, dma_n, mark) in per_engine[name]:
                    if mark is not None:
                        e.wait_ge(sems[mark[0]], mark[1])
                    ins = fn(e)
                    if dma_n:
                        assert isinstance(ins, (list, tuple)) and len(ins) == dma_n
                        for i in ins:
                            i.then_inc(sems[sem], 16)
                    else:
                        if isinstance(ins, (list, tuple)):
                            ins = ins[-1]
                        ins.then_inc(sems[sem], 1)
            return emitter

        with self.nc.Block() as block:
            for name in self.ENGINES:
                if per_engine[name]:
                    getattr(block, name)(make(name))


class Ctx:
    def __init__(self, nc, stack):
        self.nc = nc
        self.stack = stack
        self.s = Sched(nc, stack)
        sb = lambda n, shp, dt: stack.enter_context(nc.sbuf_tensor(n, shp, dt))
        ps = lambda n, shp, dt: stack.enter_context(nc.psum_tensor(n, shp, dt))
        self.cpool = sb("cpool", [128, len(CONSTS)], F32)
        self.idn = sb("idn_sb", [128, 128], BF16)
        self.onesb = sb("onesb", [1, 128], BF16)
        self.onesf = sb("onesf", [1, 128], F32)
        self.p2048 = ps("p2048", [128, 2048], F32)
        self.pa = ps("pa", [128, 1024], F32)
        self.pb1 = ps("pb1", [128, 512], F32)
        self.ptr = ps("ptr", [128, 512], BF16)
        for i, v in enumerate(CONSTS):
            nc.const_aps.register(F32, v, self.cpool[:, i:i + 1])

    def init_items(self, idn_dram):
        s = self.s
        for i, v in enumerate(CONSTS):
            s.add("vector", lambda e, i=i, v=v: e.memset(self.cpool[:, i:i + 1], v))
        s.add("vector", lambda e: e.memset(self.onesb[:, :], 1.0))
        s.add("vector", lambda e: e.memset(self.onesf[:, :], 1.0))
        s.add("sync", lambda e: [e.dma_start(out=self.idn[:, :], in_=idn_dram)], dma_n=1)

    def sb(self, n, shp, dt, stack=None):
        return (stack or self.stack).enter_context(self.nc.sbuf_tensor(n, shp, dt))


def emit_transposes(ctx, src_sb, src_col0, nchunk, dst_sb, dst_cols):
    s = ctx.s
    for g0 in range(0, nchunk, 4):
        gn = min(4, nchunk - g0)

        def tfn(e, g0=g0, gn=gn):
            out = None
            for j in range(gn):
                c = g0 + j
                out = e.transpose(ctx.ptr[:, j * 128:(j + 1) * 128],
                                  src_sb[:, src_col0 + c * 128: src_col0 + (c + 1) * 128],
                                  ctx.idn[:, :])
            return out
        s.add("tensor", tfn)

        def cfn(e, g0=g0, gn=gn):
            ins = None
            for j in range(gn):
                c = g0 + j
                ins = e.tensor_copy(out=dst_sb[:, dst_cols[c]:dst_cols[c] + 128],
                                    in_=ctx.ptr[:, j * 128:(j + 1) * 128])
            return ins
        s.add("vector", cfn)


def emit_matmul_group(ctx, psum_ap_fn, lhsT_fn, rhs_fn, nks, bias_rhs_fn=None,
                      bias_lhsT=None):
    def fn(e):
        ins = None
        for k in range(nks):
            ins = e.matmul(psum_ap_fn(), lhsT_fn(k), rhs_fn(k),
                           start=(k == 0), stop=(k == nks - 1 and bias_rhs_fn is None))
        if bias_rhs_fn is not None:
            ins = e.matmul(psum_ap_fn(), bias_lhsT, bias_rhs_fn(),
                           start=False, stop=True)
        return ins
    ctx.s.add("tensor", fn)


def emit_ln_stats(ctx, src_ap, sq_scratch, mu, var, rstd, nmr, inv_w):
    s = ctx.s
    s.add("vector", lambda e: e.reduce_sum(mu[:, :], src_ap, axis=AX))
    s.add("scalar", lambda e: e.activation(sq_scratch, src_ap, AF.Square,
                                           accum_out=var[:, :]))
    s.add("vector", lambda e: e.tensor_scalar_mul(mu[:, :], mu[:, :], inv_w))
    s.add("vector", lambda e: e.tensor_scalar_mul(var[:, :], var[:, :], inv_w))
    s.add("vector", lambda e: e.tensor_mul(nmr[:, :], mu[:, :], mu[:, :]))
    s.add("vector", lambda e: e.tensor_sub(var[:, :], var[:, :], nmr[:, :]))
    s.add("scalar", lambda e: e.activation(var[:, :], var[:, :], AF.Sqrt, bias=EPS))
    s.add("vector", lambda e: e.reciprocal(rstd[:, :], var[:, :]))
    s.add("vector", lambda e: e.tensor_mul(nmr[:, :], mu[:, :], rstd[:, :]))
    s.add("vector", lambda e: e.tensor_scalar_mul(nmr[:, :], nmr[:, :], -1.0))


def emit_row_broadcast(ctx, row_ap_fn, dst_sb, width):
    s = ctx.s
    for nb in range(0, width, 512):
        w = min(512, width - nb)
        s.add("tensor", lambda e, nb=nb, w=w: e.matmul(
            ctx.pb1[:, 0:w], ctx.onesf[:, :], row_ap_fn(nb, w), start=True, stop=True))
        s.add("scalar", lambda e, nb=nb, w=w: e.activation(
            dst_sb[:, nb:nb + w], ctx.pb1[:, 0:w], AF.Identity))



def stage_attn(ctx, xin, xout, wdr, has_bias, ch_dram, sh_dram, tag):
    nc, s = ctx.nc, ctx.s
    with ExitStack() as st:
        sb = lambda n, shp, dt: ctx.sb(f"{tag}s_{n}", shp, dt, st)
        wq_sb = sb("wq", [128, 8 * 1024], BF16)
        wk_sb = sb("wk", [128, 8 * 1024], BF16)
        wv_sb = sb("wv", [128, 8 * 1024], BF16)
        QT = sb("QT", [128, 8 * 2048], BF16)
        KT = sb("KT", [128, 8 * 2048], BF16)
        Vg = sb("Vg", [128, 16 * (16 * 65)], BF16)
        Pb = sb("Pb", [128, 2048], BF16)
        ct = sb("ct", [128, 512], F32)
        sth = sb("sth", [128, 512], F32)
        xs = sb("xs", [128, 1024], F32)
        lnb = sb("lnb", [128, 1024], BF16)
        lnT = sb("lnT", [128, 1024], BF16)
        qbf = sb("qbf", [128, 1024], BF16)
        scr = sb("scr", [128, 1024], F32)
        outs = sb("outs", [128, 1024], F32)
        mu = sb("mu", [128, 1], F32)
        var = sb("var", [128, 1], F32)
        rstd = sb("rstd", [128, 1], F32)
        nmr = sb("nmr", [128, 1], F32)
        invs = sb("invs", [1, 512], F32)
        bcs = sb("bcs", [128, 512], F32)
        bob = sb("bob", [128, 1024], F32)
        brow = sb("brow", [1, 3 * 1024], BF16)
        borow = sb("borow", [1, 1024], F32)

        def wload(dst, src):
            s.add("gpsimd", lambda e, dst=dst, src=src: [e.dma_start(
                out=dst.rearrange("p (c j) -> p c j", j=1024),
                in_=src.rearrange("(c p) j -> p c j", p=128))], dma_n=1)
        wload(wq_sb.ap(), wdr["wq"])
        wload(wk_sb.ap(), wdr["wk"])
        wload(wv_sb.ap(), wdr["wv"])
        if has_bias:
            s.add("gpsimd", lambda e: [
                e.dma_start(out=brow[:, 0:1024], in_=wdr["bq"]),
                e.dma_start(out=brow[:, 1024:2048], in_=wdr["bk"]),
                e.dma_start(out=brow[:, 2048:3072], in_=wdr["bv"]),
            ], dma_n=3)
            s.add("sync", lambda e: [e.dma_start(out=borow[:, :], in_=wdr["bo"])], dma_n=1)
            emit_row_broadcast(ctx, lambda nb, w: borow[0:1, nb:nb + w], bob, 1024)
        else:
            s.add("vector", lambda e: e.memset(bob[:, :], 0.0))

        Vg4 = Vg.ap().rearrange("p (kc h j) -> p kc h j", h=16, j=65)

        for sq in range(NSEQ):
            r0s = sq * S
            for t in range(NT):
                r0 = r0s + t * 128
                s.add("sync", lambda e, r0=r0: [e.dma_start(out=xs[:, :], in_=xin[r0:r0 + 128, :])], dma_n=1)
                s.add("sync", lambda e, t=t: [
                    e.dma_start(out=ct[:, :], in_=ch_dram[t * 128:(t + 1) * 128, :]),
                    e.dma_start(out=sth[:, :], in_=sh_dram[t * 128:(t + 1) * 128, :]),
                ], dma_n=2)
                emit_ln_stats(ctx, xs[:, :], scr[:, :], mu, var, rstd, nmr, 1.0 / 1024.0)
                s.add("vector", lambda e: e.tensor_scalar(
                    out=lnb[:, :], in0=xs[:, :], scalar1=mu[:, :], scalar2=rstd[:, :],
                    op0=ALU.subtract, op1=ALU.mult))
                emit_transposes(ctx, lnb, 0, 8, lnT, [c * 128 for c in range(8)])
                for (dst0, w_sb, boff) in ((0, wq_sb, 0), (1024, wk_sb, 1024)):
                    for nb in range(2):
                        emit_matmul_group(
                            ctx,
                            lambda dst0=dst0, nb=nb: ctx.p2048[:, dst0 + nb * 512: dst0 + (nb + 1) * 512],
                            lambda c: lnT[:, c * 128:(c + 1) * 128],
                            lambda c, w_sb=w_sb, nb=nb: w_sb[:, c * 1024 + nb * 512: c * 1024 + (nb + 1) * 512],
                            8,
                            bias_rhs_fn=(lambda boff=boff, nb=nb: brow[0:1, boff + nb * 512: boff + (nb + 1) * 512]) if has_bias else None,
                            bias_lhsT=ctx.onesb[:, :] if has_bias else None)
                for nb in range(2):
                    emit_matmul_group(
                        ctx,
                        lambda nb=nb: ctx.pa[:, nb * 512:(nb + 1) * 512],
                        lambda c: lnT[:, c * 128:(c + 1) * 128],
                        lambda c, nb=nb: wv_sb[:, c * 1024 + nb * 512: c * 1024 + (nb + 1) * 512],
                        8,
                        bias_rhs_fn=(lambda nb=nb: brow[0:1, 2048 + nb * 512: 2048 + (nb + 1) * 512]) if has_bias else None,
                        bias_lhsT=ctx.onesb[:, :] if has_bias else None)
                for (half, dstT) in ((0, QT), (1, KT)):
                    pe = ctx.p2048.ap().rearrange("p (c j two) -> p c j two", c=2, two=2)[:, half, :, 0]
                    po = ctx.p2048.ap().rearrange("p (c j two) -> p c j two", c=2, two=2)[:, half, :, 1]
                    de = qbf.ap().rearrange("p (j two) -> p j two", two=2)[:, :, 0]
                    do = qbf.ap().rearrange("p (j two) -> p j two", two=2)[:, :, 1]
                    s.add("vector", lambda e, pe=pe: e.tensor_mul(scr[:, 0:512], pe, ct[:, :]))
                    s.add("vector", lambda e, po=po: e.tensor_mul(scr[:, 512:1024], po, sth[:, :]))
                    s.add("gpsimd", lambda e, de=de: e.tensor_tensor(
                        out=de, in0=scr[:, 0:512], in1=scr[:, 512:1024], op=ALU.subtract))
                    s.add("vector", lambda e, po=po: e.tensor_mul(scr[:, 0:512], po, ct[:, :]))
                    s.add("vector", lambda e, pe=pe: e.tensor_mul(scr[:, 512:1024], pe, sth[:, :]))
                    s.add("gpsimd", lambda e, do=do: e.tensor_tensor(
                        out=do, in0=scr[:, 0:512], in1=scr[:, 512:1024], op=ALU.add))
                    emit_transposes(ctx, qbf, 0, 8, dstT, [c * 2048 + t * 128 for c in range(8)])
                s.add("vector", lambda e, t=t: e.memset(Vg4[:, t, :, 64:65], 1.0))
                s.add("vector", lambda e, t=t: e.tensor_copy(
                    out=Vg4[:, t, :, 0:64],
                    in_=ctx.pa.ap().rearrange("p (h j) -> p h j", j=64)))

            for qb in range(4):
                for h in range(NH):
                    c = h // 2
                    r = 64 * (h % 2)
                    for g in range(4):
                        def sfn(e, g=g, c=c, r=r, qb=qb):
                            ins = None
                            for j in range(4):
                                kc = g * 4 + j
                                ins = e.matmul(
                                    ctx.p2048[:, j * 512:(j + 1) * 512],
                                    KT[r:r + 64, c * 2048 + kc * 128: c * 2048 + (kc + 1) * 128],
                                    QT[r:r + 64, c * 2048 + qb * 512: c * 2048 + (qb + 1) * 512],
                                    start=True, stop=True)
                            return ins
                        s.add("tensor", sfn)
                        s.add("scalar", lambda e: e.activation(
                            Pb[:, :], ctx.p2048[:, :], AF.Exp, scale=0.125))
                        def pvfn(e, g=g, h=h):
                            ins = None
                            for j in range(4):
                                kc = g * 4 + j
                                ins = e.matmul(
                                    ctx.pb1[0:65, :],
                                    Vg[:, kc * 1040 + h * 65: kc * 1040 + (h + 1) * 65],
                                    Pb[:, j * 512:(j + 1) * 512],
                                    start=(kc == 0), stop=(kc == 15))
                            return ins
                        s.add("tensor", pvfn)
                    s.add("vector", lambda e: e.reciprocal(invs[:, :], ctx.pb1[64:65, :]))
                    s.add("tensor", lambda e: e.matmul(
                        ctx.pa[:, 0:512], ctx.onesf[:, :], invs[:, :], start=True, stop=True))
                    s.add("scalar", lambda e: e.activation(bcs[:, :], ctx.pa[:, 0:512], AF.Identity))
                    s.add("vector", lambda e, c=c, r=r, qb=qb: e.tensor_tensor(
                        out=QT[r:r + 64, c * 2048 + qb * 512: c * 2048 + (qb + 1) * 512],
                        in0=ctx.pb1[0:64, :], in1=bcs[0:64, :], op=ALU.mult))

            s.add("gpsimd", lambda e: [e.dma_start(
                out=Vg.ap()[:, 0:8192].rearrange("p (c j) -> p c j", j=1024),
                in_=wdr["wo"].rearrange("(c p) j -> p c j", p=128))], dma_n=1)
            for t in range(NT):
                r0 = r0s + t * 128
                s.add("sync", lambda e, r0=r0: [e.dma_start(out=xs[:, :], in_=xin[r0:r0 + 128, :])], dma_n=1)
                for nb in range(2):
                    emit_matmul_group(
                        ctx,
                        lambda nb=nb: ctx.pa[:, nb * 512:(nb + 1) * 512],
                        lambda c, t=t: QT[:, c * 2048 + t * 128: c * 2048 + (t + 1) * 128],
                        lambda c, nb=nb: Vg[:, c * 1024 + nb * 512: c * 1024 + (nb + 1) * 512],
                        8)
                s.add("vector", lambda e: e.scalar_tensor_tensor(
                    out=scr[:, :], in0=ctx.pa[:, :], scalar=1.0, in1=bob[:, :],
                    op0=ALU.mult, op1=ALU.add))
                s.add("vector", lambda e: e.tensor_add(outs[:, :], scr[:, :], xs[:, :]))
                s.add("gpsimd", lambda e, r0=r0: [e.dma_start(out=xout[r0:r0 + 128, :], in_=outs[:, :])], dma_n=1)
        s.flush()



def stage_ffn(ctx, xin, xout, dr, kind, tag):
    nc, s = ctx.nc, ctx.s
    TB = 8
    NB = (NSEQ * NT) // TB
    NC2 = 16 if kind == 0 else 8
    with ExitStack() as st:
        sb = lambda n, shp, dt: ctx.sb(f"{tag}s_{n}", shp, dt, st)
        w1_sb = sb("w1", [128, 8 * H0], BF16)
        w2_sb = sb("w2", [128, NC2 * 1024], BF16)
        uT = sb("uT", [128, 8 * (TB * 128)], BF16)
        acc = sb("acc", [128, TB * 1024], F32)
        cw = sb("cw", [128, TB * 8], F32)
        gwrow = sb("gwrow", [128, 64], BF16)
        grbias = sb("grbias", [1, 8], BF16)
        b1e = sb("b1e", [1, H0], BF16)
        b2e = sb("b2e", [1, 1024], BF16)
        grow_a = sb("grow_a", [1, H0], F32)
        grow_b = sb("grow_b", [1, H0], F32)
        gb = sb("gb", [128, H0], F32)
        bb = sb("bb", [128, H0], F32)
        lngb = sb("lngb", [128, 1024], F32) if kind == 0 else None
        lnbb = sb("lnbb", [128, 1024], F32) if kind == 0 else None
        xs = sb("xs", [128, 1024], F32)
        ub = sb("ub", [128, 1024], BF16)
        ar1 = sb("ar1", [128, 2048], F32)
        ar2 = sb("ar2", [128, 2048], F32)
        hb = sb("hb", [128, 2048], BF16)
        hT = sb("hT", [128, 2048], BF16)
        mu = sb("mu", [128, 1], F32)
        var = sb("var", [128, 1], F32)
        rstd = sb("rstd", [128, 1], F32)
        nmr = sb("nmr", [128, 1], F32)
        m1 = sb("m1", [128, 1], F32)
        m2 = sb("m2", [128, 1], F32)
        gsum = sb("gsum", [128, 1], F32)
        gates = sb("gates", [128, 8], F32)
        msk = sb("msk", [128, 8], F32)
        gm = sb("gm", [128, 8], F32)

        wbf1 = nc.dram_tensor(f"{tag}_wbf1", [E * 1024, H0], BF16).ap()
        w2rows = 2048 if kind == 0 else 1024
        wbf2 = nc.dram_tensor(f"{tag}_wbf2", [E * w2rows, 1024], BF16).ap()

        def preconv(src, dst, rows, cols):
            for r in range(0, rows, 128):
                s.add("sync", lambda e, r=r: [e.dma_start(out=ar1[:, 0:cols], in_=src[r:r + 128, :])], dma_n=1)
                s.add("vector", lambda e: e.tensor_copy(out=hb[:, 0:cols], in_=ar1[:, 0:cols]))
                s.add("gpsimd", lambda e, r=r: [e.dma_start(out=dst[r:r + 128, :], in_=hb[:, 0:cols])], dma_n=1)
        preconv(dr["w1"], wbf1, E * 1024, H0)
        preconv(dr["w2"], wbf2, E * w2rows, 1024)

        s.add("gpsimd", lambda e: [e.dma_start(
            out=gwrow.ap().rearrange("p (c j) -> p c j", j=8),
            in_=dr["gate_w"].rearrange("(c p) j -> p c j", p=128))], dma_n=1)
        if kind == 0:
            s.add("sync", lambda e: [
                e.dma_start(out=grow_a[:, 0:1024], in_=dr["ln_g"]),
                e.dma_start(out=grow_b[:, 0:1024], in_=dr["ln_b"])], dma_n=2)
            emit_row_broadcast(ctx, lambda nb, w: grow_a[0:1, nb:nb + w], lngb, 1024)
            emit_row_broadcast(ctx, lambda nb, w: grow_b[0:1, nb:nb + w], lnbb, 1024)
        else:
            s.add("gpsimd", lambda e: [e.dma_start(out=grbias[:, :], in_=dr["router_bias"])], dma_n=1)

        for b in range(NB):
            t0 = b * TB
            for t in range(TB):
                r0 = (t0 + t) * 128
                s.add("sync", lambda e, r0=r0: [e.dma_start(out=xs[:, :], in_=xin[r0:r0 + 128, :])], dma_n=1)
                emit_ln_stats(ctx, xs[:, :], ar1[:, 0:1024], mu, var, rstd, nmr, 1.0 / 1024.0)
                s.add("vector", lambda e: e.tensor_scalar(
                    out=ub[:, :], in0=xs[:, :], scalar1=mu[:, :], scalar2=rstd[:, :],
                    op0=ALU.subtract, op1=ALU.mult))
                emit_transposes(ctx, ub, 0, 8, uT, [c * TB * 128 + t * 128 for c in range(8)])
                s.add("vector", lambda e, t=t: e.memset(acc[:, t * 1024:(t + 1) * 1024], 0.0))

                def glog(e, t=t):
                    ins = None
                    for c in range(8):
                        ins = e.matmul(ctx.pb1[:, 0:8],
                                       uT[:, c * TB * 128 + t * 128: c * TB * 128 + (t + 1) * 128],
                                       gwrow[:, c * 8:(c + 1) * 8],
                                       start=(c == 0), stop=(c == 7 and kind == 0))
                    if kind == 1:
                        ins = e.matmul(ctx.pb1[:, 0:8], ctx.onesb[:, :], grbias[:, :],
                                       start=False, stop=True)
                    return ins
                s.add("tensor", glog)
                s.add("vector", lambda e: e.reduce_max(m1[:, :], ctx.pb1[:, 0:8], axis=AX))
                s.add("vector", lambda e: e.tensor_scalar_mul(m2[:, :], m1[:, :], -1.0))
                s.add("scalar", lambda e: e.activation(gates[:, :], ctx.pb1[:, 0:8], AF.Exp,
                                                       bias=m2[:, :], accum_out=gsum[:, :]))
                s.add("vector", lambda e: e.reciprocal(gsum[:, :], gsum[:, :]))
                s.add("vector", lambda e: e.tensor_scalar_mul(gates[:, :], gates[:, :], gsum[:, :]))
                s.add("vector", lambda e: e.reduce_max(m1[:, :], gates[:, :], axis=AX))
                s.add("vector", lambda e: e.tensor_scalar(
                    out=msk[:, :], in0=gates[:, :], scalar1=m1[:, :], scalar2=None, op0=ALU.is_ge))
                s.add("vector", lambda e: e.scalar_tensor_tensor(
                    out=gm[:, :], in0=msk[:, :], scalar=-1e30, in1=gates[:, :],
                    op0=ALU.mult, op1=ALU.add))
                s.add("vector", lambda e: e.reduce_max(m2[:, :], gm[:, :], axis=AX))
                s.add("vector", lambda e: e.tensor_scalar(
                    out=msk[:, :], in0=gates[:, :], scalar1=m2[:, :], scalar2=None, op0=ALU.is_ge))
                s.add("vector", lambda e, t=t: e.tensor_mul(cw[:, t * 8:(t + 1) * 8], gates[:, :], msk[:, :]))
                if kind == 0:
                    s.add("vector", lambda e: e.tensor_add(m1[:, :], m1[:, :], m2[:, :]))
                    s.add("vector", lambda e: e.reciprocal(m1[:, :], m1[:, :]))
                    s.add("vector", lambda e, t=t: e.tensor_scalar_mul(
                        cw[:, t * 8:(t + 1) * 8], cw[:, t * 8:(t + 1) * 8], m1[:, :]))

            for ex in range(E):
                s.add("sync", lambda e, ex=ex: [
                    e.dma_start(out=w1_sb.ap().rearrange("p (c j) -> p c j", j=H0),
                                in_=wbf1.rearrange("(ee c p) j -> ee p c j", ee=E, p=128)[ex]),
                    e.dma_start(out=w2_sb.ap().rearrange("p (c j) -> p c j", j=1024),
                                in_=wbf2.rearrange("(ee c p) j -> ee p c j", ee=E, p=128)[ex]),
                ], dma_n=2)
                s.add("gpsimd", lambda e, ex=ex: [
                    e.dma_start(out=b1e[:, :], in_=dr["b1"][ex:ex + 1, :]),
                    e.dma_start(out=b2e[:, :], in_=dr["b2"][ex:ex + 1, :]),
                ], dma_n=2)
                if kind == 0:
                    s.add("sync", lambda e, ex=ex: [
                        e.dma_start(out=grow_a[:, :], in_=dr["ln1_g"][ex:ex + 1, :]),
                        e.dma_start(out=grow_b[:, :], in_=dr["ln1_b"][ex:ex + 1, :])], dma_n=2)
                    emit_row_broadcast(ctx, lambda nb, w: grow_a[0:1, nb:nb + w], gb, H0)
                    emit_row_broadcast(ctx, lambda nb, w: grow_b[0:1, nb:nb + w], bb, H0)
                for t in range(TB):
                    for nb in range(4):
                        emit_matmul_group(
                            ctx,
                            lambda nb=nb: ctx.p2048[:, nb * 512:(nb + 1) * 512],
                            lambda c, t=t: uT[:, c * TB * 128 + t * 128: c * TB * 128 + (t + 1) * 128],
                            lambda c, nb=nb: w1_sb[:, c * H0 + nb * 512: c * H0 + (nb + 1) * 512],
                            8,
                            bias_rhs_fn=lambda nb=nb: b1e[0:1, nb * 512:(nb + 1) * 512],
                            bias_lhsT=ctx.onesb[:, :])
                    if kind == 0:
                        emit_ln_stats(ctx, ctx.p2048[:, :], ar1[:, :], mu, var, rstd, nmr, 1.0 / 2048.0)
                        s.add("scalar", lambda e: e.activation(ar1[:, :], ctx.p2048[:, :], AF.Identity,
                                                               bias=nmr[:, :], scale=rstd[:, :]))
                        s.add("vector", lambda e: e.tensor_mul(ar2[:, :], ar1[:, :], gb[:, :]))
                        s.add("gpsimd", lambda e: e.tensor_add(ar1[:, :], ar2[:, :], bb[:, :]))
                        s.add("scalar", lambda e: e.activation(hb[:, :], ar1[:, :], AF.Gelu))
                        emit_transposes(ctx, hb, 0, 16, hT, [c * 128 for c in range(16)])
                    else:
                        s.add("vector", lambda e: e.tensor_scalar(
                            out=ar1[:, 0:1024], in0=ctx.p2048[:, 0:1024], scalar1=7.0,
                            scalar2=None, op0=ALU.min))
                        s.add("vector", lambda e: e.tensor_scalar(
                            out=ar1[:, 1024:2048], in0=ctx.p2048[:, 1024:2048], scalar1=-7.0,
                            scalar2=7.0, op0=ALU.max, op1=ALU.min))
                        s.add("scalar", lambda e: e.activation(ar2[:, 0:1024], ar1[:, 0:1024],
                                                               AF.Sigmoid, scale=1.702))
                        s.add("vector", lambda e: e.tensor_mul(ar2[:, 1024:2048], ar1[:, 0:1024], ar2[:, 0:1024]))
                        s.add("vector", lambda e: e.scalar_tensor_tensor(
                            out=hb[:, 0:1024], in0=ar1[:, 1024:2048], scalar=1.0,
                            in1=ar2[:, 1024:2048], op0=ALU.add, op1=ALU.mult))
                        emit_transposes(ctx, hb, 0, 8, hT, [c * 128 for c in range(8)])
                    for nb in range(2):
                        emit_matmul_group(
                            ctx,
                            lambda nb=nb: ctx.pa[:, nb * 512:(nb + 1) * 512],
                            lambda c: hT[:, c * 128:(c + 1) * 128],
                            lambda c, nb=nb: w2_sb[:, c * 1024 + nb * 512: c * 1024 + (nb + 1) * 512],
                            NC2,
                            bias_rhs_fn=lambda nb=nb: b2e[0:1, nb * 512:(nb + 1) * 512],
                            bias_lhsT=ctx.onesb[:, :])
                    s.add("vector", lambda e, t=t, ex=ex: e.scalar_tensor_tensor(
                        out=acc[:, t * 1024:(t + 1) * 1024], in0=ctx.pa[:, :],
                        scalar=cw[:, t * 8 + ex: t * 8 + ex + 1],
                        in1=acc[:, t * 1024:(t + 1) * 1024], op0=ALU.mult, op1=ALU.add))

            for t in range(TB):
                r0 = (t0 + t) * 128
                s.add("sync", lambda e, r0=r0: [e.dma_start(out=xs[:, :], in_=xin[r0:r0 + 128, :])], dma_n=1)
                if kind == 0:
                    emit_ln_stats(ctx, xs[:, :], ar1[:, 0:1024], mu, var, rstd, nmr, 1.0 / 1024.0)
                    s.add("vector", lambda e: e.tensor_scalar(
                        out=ar1[:, 0:1024], in0=xs[:, :], scalar1=mu[:, :], scalar2=rstd[:, :],
                        op0=ALU.subtract, op1=ALU.mult))
                    s.add("vector", lambda e, t=t: e.tensor_add(
                        ar1[:, 1024:2048], acc[:, t * 1024:(t + 1) * 1024], ar1[:, 0:1024]))
                    emit_ln_stats(ctx, ar1[:, 1024:2048], ar2[:, 0:1024], mu, var, rstd, nmr, 1.0 / 1024.0)
                    s.add("scalar", lambda e: e.activation(ar2[:, 0:1024], ar1[:, 1024:2048], AF.Identity,
                                                           bias=nmr[:, :], scale=rstd[:, :]))
                    s.add("vector", lambda e: e.tensor_mul(ar2[:, 1024:2048], ar2[:, 0:1024], lngb[:, :]))
                    s.add("gpsimd", lambda e: e.tensor_add(ar1[:, 0:1024], ar2[:, 1024:2048], lnbb[:, :]))
                    s.add("vector", lambda e: e.tensor_add(ar1[:, 1024:2048], ar1[:, 0:1024], xs[:, :]))
                else:
                    s.add("vector", lambda e, t=t: e.tensor_add(
                        ar1[:, 1024:2048], acc[:, t * 1024:(t + 1) * 1024], xs[:, :]))
                s.add("gpsimd", lambda e, r0=r0: [e.dma_start(out=xout[r0:r0 + 128, :], in_=ar1[:, 1024:2048])], dma_n=1)
        s.flush()



def host_tables():
    pos = np.arange(S, dtype=np.float64)[:, None]
    i = np.arange(32, dtype=np.float64)
    f0 = 10000.0 ** (2.0 * i / 64.0)
    a0 = pos / f0[None, :]
    ch0 = np.tile(np.cos(a0), (1, 16)).astype(np.float32)
    sh0 = np.tile(np.sin(a0), (1, 16)).astype(np.float32)
    cs, ss = [], []
    for sc in (1.0, 4.0, 16.0):
        f = 1.0 / (10000.0 * sc) ** (2.0 * i / 64.0)
        a = pos * f[None, :]
        cs.append(np.cos(a))
        ss.append(np.sin(a))
    ch1 = np.tile(np.mean(cs, axis=0), (1, 16)).astype(np.float32)
    sh1 = np.tile(np.mean(ss, axis=0), (1, 16)).astype(np.float32)
    return ch0, sh0, ch1, sh1


def build_stage_program(which):
    nc = bass.Bass("TRN2", target_bir_lowering=False)
    xin = nc.dram_tensor("xin", [ROWS, D], F32, kind="ExternalInput").ap()
    xout = nc.dram_tensor("xout", [ROWS, D], F32, kind="ExternalOutput").ap()
    specs = [("idn", (128, 128), BF16)]
    if which in ("a0", "all"):
        specs += [(n, (D, D), F32) for n in ("a0_wq", "a0_wk", "a0_wv", "a0_wo")]
        specs += [(n, (1, D), F32) for n in ("a0_bq", "a0_bk", "a0_bv", "a0_bo")]
        specs += [("ch0", (S, 512), F32), ("sh0", (S, 512), F32)]
    if which in ("f0", "all"):
        specs += [("f0_gate_w", (D, E), F32), ("f0_w1", (E * D, H0), F32),
                  ("f0_b1", (E, H0), F32), ("f0_ln1_g", (E, H0), F32),
                  ("f0_ln1_b", (E, H0), F32), ("f0_w2", (E * H0, D), F32),
                  ("f0_b2", (E, D), F32), ("f0_ln_g", (1, D), F32), ("f0_ln_b", (1, D), F32)]
    if which in ("a1", "all"):
        specs += [(n, (D, D), F32) for n in ("a1_wq", "a1_wk", "a1_wv", "a1_wo")]
        specs += [("ch1", (S, 512), F32), ("sh1", (S, 512), F32)]
    if which in ("f1", "all"):
        specs += [("f1_gate_w", (D, E), F32), ("f1_rbias", (1, E), F32),
                  ("f1_w1", (E * D, H0), F32), ("f1_b1", (E, H0), F32),
                  ("f1_w2", (E * I1, D), F32), ("f1_b2", (E, D), F32)]
    dr = {}
    for name, shape, dt in specs:
        dr[name] = nc.dram_tensor(name, list(shape), dt, kind="ExternalInput").ap()

    with ExitStack() as stack:
        ctx = Ctx(nc, stack)
        ctx.init_items(dr["idn"])
        ctx.s.flush()
        if which == "all":
            x1 = nc.dram_tensor("x1", [ROWS, D], F32).ap()
            x2 = nc.dram_tensor("x2", [ROWS, D], F32).ap()
            x3 = nc.dram_tensor("x3", [ROWS, D], F32).ap()
            io = {"a0": (xin, x1), "f0": (x1, x2), "a1": (x2, x3), "f1": (x3, xout)}
        else:
            io = {which: (xin, xout)}
        if which in ("a0", "all"):
            stage_attn(ctx, io["a0"][0], io["a0"][1],
                       dict(wq=dr["a0_wq"], wk=dr["a0_wk"], wv=dr["a0_wv"], wo=dr["a0_wo"],
                            bq=dr["a0_bq"], bk=dr["a0_bk"], bv=dr["a0_bv"], bo=dr["a0_bo"]),
                       True, dr["ch0"], dr["sh0"], "a0")
        if which in ("f0", "all"):
            stage_ffn(ctx, io["f0"][0], io["f0"][1],
                      dict(gate_w=dr["f0_gate_w"], w1=dr["f0_w1"], b1=dr["f0_b1"],
                           ln1_g=dr["f0_ln1_g"], ln1_b=dr["f0_ln1_b"], w2=dr["f0_w2"],
                           b2=dr["f0_b2"], ln_g=dr["f0_ln_g"], ln_b=dr["f0_ln_b"]),
                      0, "f0")
        if which in ("a1", "all"):
            stage_attn(ctx, io["a1"][0], io["a1"][1],
                       dict(wq=dr["a1_wq"], wk=dr["a1_wk"], wv=dr["a1_wv"], wo=dr["a1_wo"]),
                       False, dr["ch1"], dr["sh1"], "a1")
        if which in ("f1", "all"):
            stage_ffn(ctx, io["f1"][0], io["f1"][1],
                      dict(gate_w=dr["f1_gate_w"], router_bias=dr["f1_rbias"],
                           w1=dr["f1_w1"], b1=dr["f1_b1"], w2=dr["f1_w2"], b2=dr["f1_b2"]),
                      1, "f1")
        ctx.s.add("vector", lambda e: e.engine_nop())
        ctx.s.flush()
    return nc


def _prep_host(inputs):
    ch0, sh0, ch1, sh1 = host_tables()
    c = lambda a: np.ascontiguousarray(np.asarray(a), dtype=np.float32) if not isinstance(a, np.ndarray) or a.dtype != np.float32 or not a.flags.c_contiguous else a
    def cc(a):
        return np.ascontiguousarray(np.asarray(a, dtype=np.float32))
    g = {
        "idn": np.eye(128, dtype=ml_dtypes.bfloat16),
        "ch0": ch0, "sh0": sh0, "ch1": ch1, "sh1": sh1,
        "a0_wq": cc(np.asarray(inputs["L0_attn_wq"]).T),
        "a0_wk": cc(np.asarray(inputs["L0_attn_wk"]).T),
        "a0_wv": cc(np.asarray(inputs["L0_attn_wv"]).T),
        "a0_wo": cc(np.asarray(inputs["L0_attn_wo"]).T),
        "a0_bq": cc(inputs["L0_attn_bq"]).reshape(1, D),
        "a0_bk": cc(inputs["L0_attn_bk"]).reshape(1, D),
        "a0_bv": cc(inputs["L0_attn_bv"]).reshape(1, D),
        "a0_bo": cc(inputs["L0_attn_bo"]).reshape(1, D),
        "f0_gate_w": cc(inputs["L0_ffn_gate_w"]),
        "f0_w1": cc(inputs["L0_ffn_w1"]).reshape(E * D, H0),
        "f0_b1": cc(inputs["L0_ffn_b1"]),
        "f0_ln1_g": cc(inputs["L0_ffn_ln1_g"]),
        "f0_ln1_b": cc(inputs["L0_ffn_ln1_b"]),
        "f0_w2": cc(inputs["L0_ffn_w2"]).reshape(E * H0, D),
        "f0_b2": cc(inputs["L0_ffn_b2"]),
        "f0_ln_g": cc(inputs["L0_ffn_ln_g"]).reshape(1, D),
        "f0_ln_b": cc(inputs["L0_ffn_ln_b"]).reshape(1, D),
        "a1_wq": cc(inputs["L1_attn_Wq"]),
        "a1_wk": cc(inputs["L1_attn_Wk"]),
        "a1_wv": cc(inputs["L1_attn_Wv"]),
        "a1_wo": cc(inputs["L1_attn_Wo"]),
        "f1_gate_w": cc(np.asarray(inputs["L1_ffn_router_weight"]).T),
        "f1_rbias": cc(inputs["L1_ffn_router_bias"]).reshape(1, E),
        "f1_w1": cc(inputs["L1_ffn_gate_up_proj"]).reshape(E * D, 2 * I1),
        "f1_b1": cc(inputs["L1_ffn_gate_up_bias"]),
        "f1_w2": cc(inputs["L1_ffn_down_proj"]).reshape(E * I1, D),
        "f1_b2": cc(inputs["L1_ffn_down_bias"]),
    }
    return g


_STAGE_KEYS = {
    "a0": ["idn", "a0_wq", "a0_wk", "a0_wv", "a0_wo", "a0_bq", "a0_bk", "a0_bv",
           "a0_bo", "ch0", "sh0"],
    "f0": ["idn", "f0_gate_w", "f0_w1", "f0_b1", "f0_ln1_g", "f0_ln1_b", "f0_w2",
           "f0_b2", "f0_ln_g", "f0_ln_b"],
    "a1": ["idn", "a1_wq", "a1_wk", "a1_wv", "a1_wo", "ch1", "sh1"],
    "f1": ["idn", "f1_gate_w", "f1_rbias", "f1_w1", "f1_b1", "f1_w2", "f1_b2"],
}


FUSED = True


def kernel(**inputs):
    g = _prep_host(inputs)
    x = np.ascontiguousarray(np.asarray(inputs["L0_attn_x"], dtype=np.float32)
                             ).reshape(BATCH * S, D)
    shards = [np.ascontiguousarray(x[c * ROWS:(c + 1) * ROWS]) for c in range(NCORES)]

    stages = ("all",) if FUSED else ("a0", "f0", "a1", "f1")
    for which in stages:
        nc = build_stage_program(which)
        keys = (set().union(*_STAGE_KEYS.values()) if which == "all"
                else set(_STAGE_KEYS[which]))
        in_maps = []
        for c in range(NCORES):
            m = {k: g[k] for k in keys}
            m["xin"] = shards[c]
            in_maps.append(m)
        res = run_bass_kernel_spmd(nc, in_maps, core_ids=list(range(NCORES)))
        shards = [np.ascontiguousarray(np.asarray(r["xout"], dtype=np.float32))
                  for r in res.results]

    out = np.concatenate(shards, axis=0).reshape(BATCH, S, D)
    return out.astype(np.float32)


# version 03
# speedup vs baseline: 1.1396x; 1.1378x over previous
import numpy as np
import ml_dtypes
from contextlib import ExitStack

import concourse.bass as bass
import concourse.mybir as mybir
from concourse.bass_utils import run_bass_kernel_spmd

F32 = mybir.dt.float32
BF16 = mybir.dt.bfloat16
AF = mybir.ActivationFunctionType
ALU = mybir.AluOpType
AX = mybir.AxisListType.X

NCORES = 8
BATCH = 32
NSEQ = BATCH // NCORES
S = 2048
D = 1024
NT = S // 128
ROWS = NSEQ * S
NH = 16
E = 8
H0 = 2048
I1 = 1024
EPS = 1e-5

CONSTS = [0.0, 1.0, -1.0, EPS, 0.125, 1.702, 7.0, -7.0]

import os
_SER1 = os.environ.get("KSER1") == "1"
_SER2 = os.environ.get("KSER2") == "1"
_SER3 = os.environ.get("KSER3") == "1"


class Sched:

    ENGINES = ["sync", "gpsimd", "tensor", "vector", "scalar"]
    SEMS = ENGINES + ["gdma"]

    def __init__(self, nc, stack):
        self.nc = nc
        self.stack = stack
        self.sems = {n: stack.enter_context(nc.semaphore(f"qsem_{n}"))
                     for n in self.SEMS}
        self.totals = {n: 0 for n in self.SEMS}
        self.prev = None
        self.pending = []
        self.n_items = 0

    def stream_sem(self, name):
        if name not in self.sems:
            self.sems[name] = self.stack.enter_context(self.nc.semaphore(f"qsem_{name}"))
            self.totals[name] = 0

    def add(self, eng, fn, dma_n=0, after="prev", stream=None):
        if stream is not None:
            assert dma_n
            self.stream_sem(stream)
            sem = stream
        else:
            sem = "gdma" if (eng == "gpsimd" and dma_n) else eng
        if after == "prev":
            deps = [self.prev] if self.prev is not None else []
        elif after is None:
            deps = []
        else:
            deps = [m for m in after if m is not None]
        self.pending.append((eng, sem, fn, dma_n, deps))
        inc = 16 * dma_n if dma_n else 1
        self.totals[sem] += inc
        mark = (sem, self.totals[sem])
        self.prev = mark
        self.n_items += 1
        return mark

    def barrier(self):
        marks = [(n, v) for n, v in self.totals.items() if v > 0]
        self.prev = None
        m = self.add("vector", lambda e: e.engine_nop(), after=marks)
        self.prev = m
        return m

    def flush(self):
        if not self.pending:
            return
        per_engine = {n: [] for n in self.ENGINES}
        for it in self.pending:
            per_engine[it[0]].append(it)
        self.pending = []
        sems = self.sems

        def make(name):
            def emitter(e):
                seen = {}
                for (_, sem, fn, dma_n, deps) in per_engine[name]:
                    for (dsem, dval) in deps:
                        if seen.get(dsem, -1) < dval:
                            e.wait_ge(sems[dsem], dval)
                            seen[dsem] = dval
                    ins = fn(e)
                    if dma_n:
                        assert isinstance(ins, (list, tuple)) and len(ins) == dma_n
                        for i in ins:
                            i.then_inc(sems[sem], 16)
                    else:
                        if isinstance(ins, (list, tuple)):
                            ins = ins[-1]
                        ins.then_inc(sems[sem], 1)
            return emitter

        with self.nc.Block() as block:
            for name in self.ENGINES:
                if per_engine[name]:
                    getattr(block, name)(make(name))


class Ctx:
    def __init__(self, nc, stack):
        self.nc = nc
        self.stack = stack
        self.s = Sched(nc, stack)
        sb = lambda n, shp, dt: stack.enter_context(nc.sbuf_tensor(n, shp, dt))
        ps = lambda n, shp, dt: stack.enter_context(nc.psum_tensor(n, shp, dt))
        self.cpool = sb("cpool", [128, len(CONSTS)], F32)
        self.idn = sb("idn_sb", [128, 128], BF16)
        self.onesb = sb("onesb", [1, 128], BF16)
        self.onesf = sb("onesf", [1, 128], F32)
        self.p2048 = ps("p2048", [128, 2048], F32)
        self.pa = ps("pa", [128, 1024], F32)
        self.pb1 = ps("pb1", [128, 512], F32)
        self.ptr = ps("ptr", [128, 512], BF16)
        for i, v in enumerate(CONSTS):
            nc.const_aps.register(F32, v, self.cpool[:, i:i + 1])

    def init_items(self, idn_dram):
        s = self.s
        for i, v in enumerate(CONSTS):
            s.add("vector", lambda e, i=i, v=v: e.memset(self.cpool[:, i:i + 1], v))
        s.add("vector", lambda e: e.memset(self.onesb[:, :], 1.0))
        s.add("vector", lambda e: e.memset(self.onesf[:, :], 1.0))
        s.add("sync", lambda e: [e.dma_start(out=self.idn[:, :], in_=idn_dram)], dma_n=1)

    def sb(self, n, shp, dt, stack=None):
        return (stack or self.stack).enter_context(self.nc.sbuf_tensor(n, shp, dt))


def emit_transposes(ctx, src_sb, src_col0, nchunk, dst_sb, dst_cols):
    s = ctx.s
    for g0 in range(0, nchunk, 4):
        gn = min(4, nchunk - g0)

        def tfn(e, g0=g0, gn=gn):
            out = None
            for j in range(gn):
                c = g0 + j
                out = e.transpose(ctx.ptr[:, j * 128:(j + 1) * 128],
                                  src_sb[:, src_col0 + c * 128: src_col0 + (c + 1) * 128],
                                  ctx.idn[:, :])
            return out
        s.add("tensor", tfn)

        def cfn(e, g0=g0, gn=gn):
            ins = None
            for j in range(gn):
                c = g0 + j
                ins = e.tensor_copy(out=dst_sb[:, dst_cols[c]:dst_cols[c] + 128],
                                    in_=ctx.ptr[:, j * 128:(j + 1) * 128])
            return ins
        s.add("vector", cfn)


def emit_matmul_group(ctx, psum_ap_fn, lhsT_fn, rhs_fn, nks, bias_rhs_fn=None,
                      bias_lhsT=None):
    def fn(e):
        ins = None
        for k in range(nks):
            ins = e.matmul(psum_ap_fn(), lhsT_fn(k), rhs_fn(k),
                           start=(k == 0), stop=(k == nks - 1 and bias_rhs_fn is None))
        if bias_rhs_fn is not None:
            ins = e.matmul(psum_ap_fn(), bias_lhsT, bias_rhs_fn(),
                           start=False, stop=True)
        return ins
    ctx.s.add("tensor", fn)


def emit_ln_stats(ctx, src_ap, sq_scratch, mu, var, rstd, nmr, inv_w):
    s = ctx.s
    s.add("vector", lambda e: e.reduce_sum(mu[:, :], src_ap, axis=AX))
    s.add("scalar", lambda e: e.activation(sq_scratch, src_ap, AF.Square,
                                           accum_out=var[:, :]))
    s.add("vector", lambda e: e.tensor_scalar_mul(mu[:, :], mu[:, :], inv_w))
    s.add("vector", lambda e: e.tensor_scalar_mul(var[:, :], var[:, :], inv_w))
    s.add("vector", lambda e: e.tensor_mul(nmr[:, :], mu[:, :], mu[:, :]))
    s.add("vector", lambda e: e.tensor_sub(var[:, :], var[:, :], nmr[:, :]))
    s.add("scalar", lambda e: e.activation(var[:, :], var[:, :], AF.Sqrt, bias=EPS))
    s.add("vector", lambda e: e.reciprocal(rstd[:, :], var[:, :]))
    s.add("vector", lambda e: e.tensor_mul(nmr[:, :], mu[:, :], rstd[:, :]))
    s.add("vector", lambda e: e.tensor_scalar_mul(nmr[:, :], nmr[:, :], -1.0))


def emit_row_broadcast(ctx, row_ap_fn, dst_sb, width):
    s = ctx.s
    for nb in range(0, width, 512):
        w = min(512, width - nb)
        s.add("tensor", lambda e, nb=nb, w=w: e.matmul(
            ctx.pb1[:, 0:w], ctx.onesf[:, :], row_ap_fn(nb, w), start=True, stop=True))
        s.add("scalar", lambda e, nb=nb, w=w: e.activation(
            dst_sb[:, nb:nb + w], ctx.pb1[:, 0:w], AF.Identity))



def emit_ln_bn(ctx, src_slice_fn, W, B, deps):
    s = ctx.s
    nch = W // 512

    def bnfn(e):
        ins = None
        for j in range(nch):
            ins = e.bn_stats(B["bnst"][:, j * 6:(j + 1) * 6], src_slice_fn(j * 512, 512))
        return ins
    m1 = s.add("vector", bnfn, after=deps)
    m2 = s.add("vector", lambda e: e.bn_aggr(B["bnmv"][:, :], B["bnst"][:, 0:6 * nch]), after=[m1])
    m3 = s.add("scalar", lambda e: e.activation(B["sd"][:, :], B["bnmv"][:, 1:2], AF.Sqrt, bias=EPS), after=[m2])
    m4 = s.add("vector", lambda e: e.reciprocal(B["rstd"][:, :], B["sd"][:, :]), after=[m3])
    m5 = s.add("vector", lambda e: e.tensor_scalar(
        out=B["nmr"][:, :], in0=B["bnmv"][:, 0:1], scalar1=B["rstd"][:, :], scalar2=-1.0,
        op0=ALU.mult, op1=ALU.mult), after=[m4])
    return m5


def emit_transposes2(ctx, src_sb, src_col0, nchunk, dst_sb, dst_cols, start_deps,
                     cfn_extra_deps, prev_cfn_last):
    s = ctx.s
    tmark = None
    cmark = prev_cfn_last
    for g0 in range(0, nchunk, 4):
        gn = min(4, nchunk - g0)

        def tfn(e, g0=g0, gn=gn):
            out = None
            for j in range(gn):
                c = g0 + j
                out = e.transpose(ctx.ptr[:, j * 128:(j + 1) * 128],
                                  src_sb[:, src_col0 + c * 128: src_col0 + (c + 1) * 128],
                                  ctx.idn[:, :])
            return out
        deps = list(start_deps)
        if cmark is not None:
            deps.append(cmark)
        tmark = s.add("tensor", tfn, after=deps)

        def cfn(e, g0=g0, gn=gn):
            ins = None
            for j in range(gn):
                c = g0 + j
                ins = e.tensor_copy(out=dst_sb[:, dst_cols[c]:dst_cols[c] + 128],
                                    in_=ctx.ptr[:, j * 128:(j + 1) * 128])
            return ins
        cmark = s.add("vector", cfn, after=[tmark] + list(cfn_extra_deps))
    return tmark, cmark


def stage_attn(ctx, xin, xout, wdr, has_bias, ch_dram, sh_dram, tag):
    nc, s = ctx.nc, ctx.s
    with ExitStack() as st:
        sb = lambda n, shp, dt: ctx.sb(f"{tag}s_{n}", shp, dt, st)
        wq_sb = sb("wq", [128, 8 * 1024], BF16)
        wk_sb = sb("wk", [128, 8 * 1024], BF16)
        wv_sb = sb("wv", [128, 8 * 1024], BF16)
        QT = sb("QT", [128, 8 * 2048], BF16)
        KT = sb("KT", [128, 8 * 2048], BF16)
        Vg = sb("Vg", [128, 16 * (16 * 65)], BF16)
        Pb = sb("Pb", [128, 2048], BF16)
        ct = sb("ct", [128, 512], F32)
        sth = sb("sth", [128, 512], F32)
        xs = sb("xs", [128, 1024], F32)
        lnb = sb("lnb", [128, 1024], BF16)
        lnT = sb("lnT", [128, 1024], BF16)
        qbf = sb("qbf", [128, 1024], BF16)
        scr = sb("scr", [128, 1024], F32)
        outs = sb("outs", [128, 1024], F32)
        bnst = sb("bnst", [128, 12], F32)
        bnmv = sb("bnmv", [128, 2], F32)
        sd = sb("sd", [128, 1], F32)
        rstd = sb("rstd", [128, 1], F32)
        nmr = sb("nmr", [128, 1], F32)
        invs = sb("invs", [1, 512], F32)
        bcs = sb("bcs", [128, 512], F32)
        bob = sb("bob", [128, 1024], F32)
        brow = sb("brow", [1, 3 * 1024], BF16)
        borow = sb("borow", [1, 1024], F32)
        BNB = dict(bnst=bnst, bnmv=bnmv, sd=sd, rstd=rstd, nmr=nmr)

        def wload(dst, src):
            s.add("gpsimd", lambda e, dst=dst, src=src: [e.dma_start(
                out=dst.rearrange("p (c j) -> p c j", j=1024),
                in_=src.rearrange("(c p) j -> p c j", p=128))], dma_n=1)
        wload(wq_sb.ap(), wdr["wq"])
        wload(wk_sb.ap(), wdr["wk"])
        wload(wv_sb.ap(), wdr["wv"])
        if has_bias:
            s.add("gpsimd", lambda e: [
                e.dma_start(out=brow[:, 0:1024], in_=wdr["bq"]),
                e.dma_start(out=brow[:, 1024:2048], in_=wdr["bk"]),
                e.dma_start(out=brow[:, 2048:3072], in_=wdr["bv"]),
            ], dma_n=3)
            s.add("sync", lambda e: [e.dma_start(out=borow[:, :], in_=wdr["bo"])], dma_n=1)
            emit_row_broadcast(ctx, lambda nb, w: borow[0:1, nb:nb + w], bob, 1024)
        else:
            s.add("vector", lambda e: e.memset(bob[:, :], 0.0))
        s.barrier()

        Vg4 = Vg.ap().rearrange("p (kc h j) -> p kc h j", h=16, j=65)

        for sq in range(NSEQ):
            r0s = sq * S
            mk = {"ts": None, "cfnln": None, "tfnln": None,
                  "qmm": None, "kmm": None, "vmm": None,
                  "ropeQ4": None, "ropeK4": None, "gpQ2": None, "gpK2": None,
                  "tfnQ": None, "cfnQ": None, "tfnK": None, "cfnK": None,
                  "vg": None, "dmax": None, "dmat": None}
            mk["dmax"] = s.add("sync", lambda e, r0s=r0s: [e.dma_start(out=xs[:, :], in_=xin[r0s:r0s + 128, :])], dma_n=1, stream="d_xs")
            mk["dmat"] = s.add("sync", lambda e: [
                e.dma_start(out=ct[:, :], in_=ch_dram[0:128, :]),
                e.dma_start(out=sth[:, :], in_=sh_dram[0:128, :])], dma_n=2, stream="d_tab")
            for t in range(NT):
                r0 = r0s + t * 128
                mln = emit_ln_bn(ctx, lambda lo, w: xs[:, lo:lo + w], 1024, BNB,
                                 [mk["dmax"]])
                m_ts = s.add("vector", lambda e: e.tensor_scalar(
                    out=lnb[:, :], in0=xs[:, :], scalar1=bnmv[:, 0:1], scalar2=rstd[:, :],
                    op0=ALU.subtract, op1=ALU.mult),
                    after=[mln] + ([mk["tfnln"]] if mk["tfnln"] else []))
                if t + 1 < NT:
                    mk["dmax"] = s.add("sync", lambda e, r1=r0 + 128: [
                        e.dma_start(out=xs[:, :], in_=xin[r1:r1 + 128, :])], dma_n=1, after=[m_ts], stream="d_xs")
                tfnln, cfnln = emit_transposes2(
                    ctx, lnb, 0, 8, lnT, [c * 128 for c in range(8)],
                    [m_ts], [mk["vmm"]] if mk["vmm"] else [], mk["cfnK"])
                mk["tfnln"], mk["cfnln"] = tfnln, cfnln

                def qkfn2(e, dst0, w_sb, boff):
                    ins = None
                    for nb in range(2):
                        nk = 8
                        for c in range(nk):
                            last = (c == nk - 1) and not has_bias
                            ins = e.matmul(
                                ctx.p2048[:, dst0 + nb * 512: dst0 + (nb + 1) * 512],
                                lnT[:, c * 128:(c + 1) * 128],
                                w_sb[:, c * 1024 + nb * 512: c * 1024 + (nb + 1) * 512],
                                start=(c == 0), stop=last)
                        if has_bias:
                            ins = e.matmul(ctx.p2048[:, dst0 + nb * 512: dst0 + (nb + 1) * 512],
                                           ctx.onesb[:, :],
                                           brow[0:1, boff + nb * 512: boff + (nb + 1) * 512],
                                           start=False, stop=True)
                    return ins
                mk["qmm"] = s.add("tensor", lambda e: qkfn2(e, 0, wq_sb, 0),
                                  after=[cfnln] + ([mk["ropeQ4"]] if mk["ropeQ4"] else []))
                mk["kmm"] = s.add("tensor", lambda e: qkfn2(e, 1024, wk_sb, 1024),
                                  after=[cfnln] + ([mk["ropeK4"]] if mk["ropeK4"] else []))

                def vfn(e):
                    ins = None
                    for nb in range(2):
                        for c in range(8):
                            last = (c == 7) and not has_bias
                            ins = e.matmul(ctx.pa[:, nb * 512:(nb + 1) * 512],
                                           lnT[:, c * 128:(c + 1) * 128],
                                           wv_sb[:, c * 1024 + nb * 512: c * 1024 + (nb + 1) * 512],
                                           start=(c == 0), stop=last)
                        if has_bias:
                            ins = e.matmul(ctx.pa[:, nb * 512:(nb + 1) * 512],
                                           ctx.onesb[:, :],
                                           brow[0:1, 2048 + nb * 512: 2048 + (nb + 1) * 512],
                                           start=False, stop=True)
                    return ins
                mk["vmm"] = s.add("tensor", vfn,
                                  after=[cfnln] + ([mk["vg"]] if mk["vg"] else []))

                for half in (0, 1):
                    pe = ctx.p2048.ap().rearrange("p (c j two) -> p c j two", c=2, two=2)[:, half, :, 0]
                    po = ctx.p2048.ap().rearrange("p (c j two) -> p c j two", c=2, two=2)[:, half, :, 1]
                    de = qbf.ap().rearrange("p (j two) -> p j two", two=2)[:, :, 0]
                    do = qbf.ap().rearrange("p (j two) -> p j two", two=2)[:, :, 1]
                    mmm = mk["qmm"] if half == 0 else mk["kmm"]
                    scrdep = mk["gpK2"] if half == 0 else mk["gpQ2"]
                    qbfdep = (mk["tfnK"] if half == 0 else mk["tfnQ"])
                    mu1 = s.add("vector", lambda e, pe=pe: e.tensor_mul(scr[:, 0:512], pe, ct[:, :]),
                                after=[mmm, mk["dmat"]] + ([scrdep] if scrdep else []))
                    mu2 = s.add("vector", lambda e, po=po: e.tensor_mul(scr[:, 512:1024], po, sth[:, :]),
                                after=[mu1])
                    g1 = s.add("gpsimd", lambda e, de=de: e.tensor_tensor(
                        out=de, in0=scr[:, 0:512], in1=scr[:, 512:1024], op=ALU.subtract),
                        after=[mu2] + ([qbfdep] if qbfdep else []))
                    mu3 = s.add("vector", lambda e, po=po: e.tensor_mul(scr[:, 0:512], po, ct[:, :]),
                                after=[g1])
                    mu4 = s.add("vector", lambda e, pe=pe: e.tensor_mul(scr[:, 512:1024], pe, sth[:, :]),
                                after=[mu3])
                    g2 = s.add("gpsimd", lambda e, do=do: e.tensor_tensor(
                        out=do, in0=scr[:, 0:512], in1=scr[:, 512:1024], op=ALU.add),
                        after=[mu4])
                    if half == 0:
                        mk["ropeQ4"], mk["gpQ2"] = mu4, g2
                        tq, cq = emit_transposes2(
                            ctx, qbf, 0, 8, QT, [c * 2048 + t * 128 for c in range(8)],
                            [g1, g2], [], mk["cfnln"])
                        mk["tfnQ"], mk["cfnQ"] = tq, cq
                    else:
                        mk["ropeK4"], mk["gpK2"] = mu4, g2
                        tk, ck = emit_transposes2(
                            ctx, qbf, 0, 8, KT, [c * 2048 + t * 128 for c in range(8)],
                            [g1, g2], [], mk["cfnQ"])
                        mk["tfnK"], mk["cfnK"] = tk, ck
                if t + 1 < NT:
                    mk["dmat"] = s.add("sync", lambda e, t1=t + 1: [
                        e.dma_start(out=ct[:, :], in_=ch_dram[t1 * 128:(t1 + 1) * 128, :]),
                        e.dma_start(out=sth[:, :], in_=sh_dram[t1 * 128:(t1 + 1) * 128, :])],
                        dma_n=2, after=[mk["ropeK4"]], stream="d_tab")
                mo = s.add("vector", lambda e, t=t: e.memset(Vg4[:, t, :, 64:65], 1.0),
                           after=[mk["vmm"]])
                mk["vg"] = s.add("vector", lambda e, t=t: e.tensor_copy(
                    out=Vg4[:, t, :, 0:64],
                    in_=ctx.pa.ap().rearrange("p (h j) -> p h j", j=64)), after=[mo])
                if _SER1:
                    bm = s.barrier()
                    for k in mk:
                        if mk[k] is not None:
                            mk[k] = bm
            m_bar = s.barrier()

            m_norm = None
            m_actc = None
            m_exp = [None, None]
            m_pv = [None, None]
            for qb in range(4):
                for h in range(NH):
                    c = h // 2
                    r = 64 * (h % 2)
                    m_pv_last = None
                    for g in range(8):
                        p = g % 2

                        def sfn(e, g=g, p=p, c=c, r=r, qb=qb):
                            ins = None
                            for j in range(2):
                                kc = g * 2 + j
                                ins = e.matmul(
                                    ctx.p2048[:, p * 1024 + j * 512: p * 1024 + (j + 1) * 512],
                                    KT[r:r + 64, c * 2048 + kc * 128: c * 2048 + (kc + 1) * 128],
                                    QT[r:r + 64, c * 2048 + qb * 512: c * 2048 + (qb + 1) * 512],
                                    start=True, stop=True)
                            return ins
                        m_s = s.add("tensor", sfn, after=[m_exp[p] if m_exp[p] else m_bar])
                        m_exp[p] = s.add("scalar", lambda e, p=p: e.activation(
                            Pb[:, p * 1024:(p + 1) * 1024],
                            ctx.p2048[:, p * 1024:(p + 1) * 1024], AF.Exp, scale=0.125),
                            after=[m_s] + ([m_pv[p]] if m_pv[p] else []))

                        def pvfn(e, g=g, p=p, h=h):
                            ins = None
                            for j in range(2):
                                kc = g * 2 + j
                                ins = e.matmul(
                                    ctx.pb1[0:65, :],
                                    Vg[:, kc * 1040 + h * 65: kc * 1040 + (h + 1) * 65],
                                    Pb[:, p * 1024 + j * 512: p * 1024 + (j + 1) * 512],
                                    start=(kc == 0), stop=(kc == 15))
                            return ins
                        pvdeps = [m_exp[p]]
                        if g == 0:
                            pvdeps.append(m_norm if m_norm else m_bar)
                        m_pv[p] = s.add("tensor", pvfn, after=pvdeps)
                        m_pv_last = m_pv[p]
                    m_rc = s.add("vector", lambda e: e.reciprocal(invs[:, :], ctx.pb1[64:65, :]),
                                 after=[m_pv_last])
                    m_bc = s.add("tensor", lambda e: e.matmul(
                        ctx.pa[:, 0:512], ctx.onesf[:, :], invs[:, :], start=True, stop=True),
                        after=[m_rc] + ([m_actc] if m_actc else []))
                    m_actc = s.add("scalar", lambda e: e.activation(
                        bcs[:, :], ctx.pa[:, 0:512], AF.Identity),
                        after=[m_bc] + ([m_norm] if m_norm else []))
                    m_norm = s.add("vector", lambda e, c=c, r=r, qb=qb: e.tensor_tensor(
                        out=QT[r:r + 64, c * 2048 + qb * 512: c * 2048 + (qb + 1) * 512],
                        in0=ctx.pb1[0:64, :], in1=bcs[0:64, :], op=ALU.mult),
                        after=[m_actc])
                    if _SER2:
                        bm = s.barrier()
                        m_norm = m_actc = bm
                        m_exp = [bm, bm]
                        m_pv = [bm, bm]
            s.barrier()

            s.add("gpsimd", lambda e: [e.dma_start(
                out=Vg.ap()[:, 0:8192].rearrange("p (c j) -> p c j", j=1024),
                in_=wdr["wo"].rearrange("(c p) j -> p c j", p=128))], dma_n=1, stream="d_wo")
            m_wo = s.barrier()
            m_stt = None
            m_add = None
            m_out = None
            m_x = s.add("sync", lambda e, r0s=r0s: [e.dma_start(out=xs[:, :], in_=xin[r0s:r0s + 128, :])],
                        dma_n=1, after=[m_wo], stream="d_xs")
            for t in range(NT):
                r0 = r0s + t * 128

                def wofn(e, t=t):
                    ins = None
                    for nb in range(2):
                        for c in range(8):
                            ins = e.matmul(
                                ctx.pa[:, nb * 512:(nb + 1) * 512],
                                QT[:, c * 2048 + t * 128: c * 2048 + (t + 1) * 128],
                                Vg[:, c * 1024 + nb * 512: c * 1024 + (nb + 1) * 512],
                                start=(c == 0), stop=(c == 7))
                    return ins
                m_mm = s.add("tensor", wofn, after=[m_stt] if m_stt else [m_wo])
                m_stt = s.add("vector", lambda e: e.scalar_tensor_tensor(
                    out=scr[:, :], in0=ctx.pa[:, :], scalar=1.0, in1=bob[:, :],
                    op0=ALU.mult, op1=ALU.add),
                    after=[m_mm] + ([m_add] if m_add else []))
                m_add = s.add("vector", lambda e: e.tensor_add(outs[:, :], scr[:, :], xs[:, :]),
                              after=[m_stt, m_x] + ([m_out] if m_out else []))
                if t + 1 < NT:
                    m_x = s.add("sync", lambda e, r1=r0 + 128: [
                        e.dma_start(out=xs[:, :], in_=xin[r1:r1 + 128, :])], dma_n=1, after=[m_add], stream="d_xs")
                m_out = s.add("gpsimd", lambda e, r0=r0: [
                    e.dma_start(out=xout[r0:r0 + 128, :], in_=outs[:, :])], dma_n=1,
                    after=[m_add], stream="d_out")
                if _SER3:
                    bm = s.barrier()
                    m_stt = m_add = m_out = m_x = bm
            s.barrier()
        s.flush()



def stage_ffn(ctx, xin, xout, dr, kind, tag):
    nc, s = ctx.nc, ctx.s
    TB = 8
    NB = (NSEQ * NT) // TB
    NC2 = 16 if kind == 0 else 8
    with ExitStack() as st:
        sb = lambda n, shp, dt: ctx.sb(f"{tag}s_{n}", shp, dt, st)
        w1_sb = sb("w1", [128, 8 * H0], BF16)
        w2_sb = sb("w2", [128, NC2 * 1024], BF16)
        uT = sb("uT", [128, 8 * (TB * 128)], BF16)
        acc = sb("acc", [128, TB * 1024], F32)
        cw = sb("cw", [128, TB * 8], F32)
        gwrow = sb("gwrow", [128, 64], BF16)
        grbias = sb("grbias", [1, 8], BF16)
        b1e = sb("b1e", [1, H0], BF16)
        b2e = sb("b2e", [1, 1024], BF16)
        grow_a = sb("grow_a", [1, H0], F32)
        grow_b = sb("grow_b", [1, H0], F32)
        gb = sb("gb", [128, H0], F32)
        bb = sb("bb", [128, H0], F32)
        lngb = sb("lngb", [128, 1024], F32) if kind == 0 else None
        lnbb = sb("lnbb", [128, 1024], F32) if kind == 0 else None
        xs = sb("xs", [128, 1024], F32)
        ub = sb("ub", [128, 1024], BF16)
        ar1 = sb("ar1", [128, 2048], F32)
        ar2 = sb("ar2", [128, 2048], F32)
        hb = sb("hb", [128, 2048], BF16)
        hT = sb("hT", [128, 2048], BF16)
        bnst = sb("bnst", [128, 24], F32)
        bnmv = sb("bnmv", [128, 2], F32)
        sd = sb("sd", [128, 1], F32)
        rstd = sb("rstd", [128, 1], F32)
        nmr = sb("nmr", [128, 1], F32)
        m1 = sb("m1", [128, 1], F32)
        m2 = sb("m2", [128, 1], F32)
        gsum = sb("gsum", [128, 1], F32)
        gates = sb("gates", [128, 8], F32)
        msk = sb("msk", [128, 8], F32)
        gm = sb("gm", [128, 8], F32)
        BNB = dict(bnst=bnst, bnmv=bnmv, sd=sd, rstd=rstd, nmr=nmr)

        wbf1 = nc.dram_tensor(f"{tag}_wbf1", [E * 1024, H0], BF16).ap()
        w2rows = 2048 if kind == 0 else 1024
        wbf2 = nc.dram_tensor(f"{tag}_wbf2", [E * w2rows, 1024], BF16).ap()

        def preconv(src, dst, rows, cols):
            for r in range(0, rows, 128):
                s.add("sync", lambda e, r=r: [e.dma_start(out=ar1[:, 0:cols], in_=src[r:r + 128, :])], dma_n=1)
                s.add("vector", lambda e: e.tensor_copy(out=hb[:, 0:cols], in_=ar1[:, 0:cols]))
                s.add("gpsimd", lambda e, r=r: [e.dma_start(out=dst[r:r + 128, :], in_=hb[:, 0:cols])], dma_n=1)
        preconv(dr["w1"], wbf1, E * 1024, H0)
        preconv(dr["w2"], wbf2, E * w2rows, 1024)

        s.add("gpsimd", lambda e: [e.dma_start(
            out=gwrow.ap().rearrange("p (c j) -> p c j", j=8),
            in_=dr["gate_w"].rearrange("(c p) j -> p c j", p=128))], dma_n=1)
        if kind == 0:
            s.add("sync", lambda e: [
                e.dma_start(out=grow_a[:, 0:1024], in_=dr["ln_g"]),
                e.dma_start(out=grow_b[:, 0:1024], in_=dr["ln_b"])], dma_n=2)
            emit_row_broadcast(ctx, lambda nb, w: grow_a[0:1, nb:nb + w], lngb, 1024)
            emit_row_broadcast(ctx, lambda nb, w: grow_b[0:1, nb:nb + w], lnbb, 1024)
        else:
            s.add("gpsimd", lambda e: [e.dma_start(out=grbias[:, :], in_=dr["router_bias"])], dma_n=1)
        s.barrier()

        for b in range(NB):
            t0 = b * TB
            for t in range(TB):
                r0 = (t0 + t) * 128
                s.add("sync", lambda e, r0=r0: [e.dma_start(out=xs[:, :], in_=xin[r0:r0 + 128, :])], dma_n=1)
                emit_ln_bn(ctx, lambda lo, w: xs[:, lo:lo + w], 1024, BNB, "prev")
                s.add("vector", lambda e: e.tensor_scalar(
                    out=ub[:, :], in0=xs[:, :], scalar1=bnmv[:, 0:1], scalar2=rstd[:, :],
                    op0=ALU.subtract, op1=ALU.mult))
                emit_transposes(ctx, ub, 0, 8, uT, [c * TB * 128 + t * 128 for c in range(8)])
                s.add("vector", lambda e, t=t: e.memset(acc[:, t * 1024:(t + 1) * 1024], 0.0))

                def glog(e, t=t):
                    ins = None
                    for c in range(8):
                        ins = e.matmul(ctx.pb1[:, 0:8],
                                       uT[:, c * TB * 128 + t * 128: c * TB * 128 + (t + 1) * 128],
                                       gwrow[:, c * 8:(c + 1) * 8],
                                       start=(c == 0), stop=(c == 7 and kind == 0))
                    if kind == 1:
                        ins = e.matmul(ctx.pb1[:, 0:8], ctx.onesb[:, :], grbias[:, :],
                                       start=False, stop=True)
                    return ins
                s.add("tensor", glog)
                s.add("vector", lambda e: e.reduce_max(m1[:, :], ctx.pb1[:, 0:8], axis=AX))
                s.add("vector", lambda e: e.tensor_scalar_mul(m2[:, :], m1[:, :], -1.0))
                s.add("scalar", lambda e: e.activation(gates[:, :], ctx.pb1[:, 0:8], AF.Exp,
                                                       bias=m2[:, :], accum_out=gsum[:, :]))
                s.add("vector", lambda e: e.reciprocal(gsum[:, :], gsum[:, :]))
                s.add("vector", lambda e: e.tensor_scalar_mul(gates[:, :], gates[:, :], gsum[:, :]))
                s.add("vector", lambda e: e.reduce_max(m1[:, :], gates[:, :], axis=AX))
                s.add("vector", lambda e: e.tensor_scalar(
                    out=msk[:, :], in0=gates[:, :], scalar1=m1[:, :], scalar2=None, op0=ALU.is_ge))
                s.add("vector", lambda e: e.scalar_tensor_tensor(
                    out=gm[:, :], in0=msk[:, :], scalar=-1e30, in1=gates[:, :],
                    op0=ALU.mult, op1=ALU.add))
                s.add("vector", lambda e: e.reduce_max(m2[:, :], gm[:, :], axis=AX))
                s.add("vector", lambda e: e.tensor_scalar(
                    out=msk[:, :], in0=gates[:, :], scalar1=m2[:, :], scalar2=None, op0=ALU.is_ge))
                s.add("vector", lambda e, t=t: e.tensor_mul(cw[:, t * 8:(t + 1) * 8], gates[:, :], msk[:, :]))
                if kind == 0:
                    s.add("vector", lambda e: e.tensor_add(m1[:, :], m1[:, :], m2[:, :]))
                    s.add("vector", lambda e: e.reciprocal(m1[:, :], m1[:, :]))
                    s.add("vector", lambda e, t=t: e.tensor_scalar_mul(
                        cw[:, t * 8:(t + 1) * 8], cw[:, t * 8:(t + 1) * 8], m1[:, :]))
            m_ph1 = s.barrier()

            m_wld = s.add("sync", lambda e: [
                e.dma_start(out=w1_sb.ap().rearrange("p (c j) -> p c j", j=H0),
                            in_=wbf1.rearrange("(ee c p) j -> ee p c j", ee=E, p=128)[0]),
                e.dma_start(out=w2_sb.ap().rearrange("p (c j) -> p c j", j=1024),
                            in_=wbf2.rearrange("(ee c p) j -> ee p c j", ee=E, p=128)[0]),
            ], dma_n=2, after=[m_ph1], stream="d_wld")
            m_brow = s.add("gpsimd", lambda e: [
                e.dma_start(out=b1e[:, :], in_=dr["b1"][0:1, :]),
                e.dma_start(out=b2e[:, :], in_=dr["b2"][0:1, :]),
            ], dma_n=2, after=[m_ph1], stream="d_brow")
            if kind == 0:
                m_grow = s.add("sync", lambda e: [
                    e.dma_start(out=grow_a[:, :], in_=dr["ln1_g"][0:1, :]),
                    e.dma_start(out=grow_b[:, :], in_=dr["ln1_b"][0:1, :])], dma_n=2,
                    after=[m_ph1], stream="d_grow")
            m_gbb = None
            mp = {"w1": None, "w2": None, "apply": None, "mulgb": None, "addbb": None,
                  "gelu": None, "tfn": None, "cfn": None, "acc": None,
                  "tsmin": None, "tsclip": None, "sig": None, "mulglu": None, "stt": None}
            for ex in range(E):
                if kind == 0:
                    gdeps = [m_grow] + [m for m in (mp["mulgb"], mp["addbb"], m_gbb) if m]
                    last = None
                    for nb in range(0, H0, 512):
                        mmm = s.add("tensor", lambda e, nb=nb: e.matmul(
                            ctx.pb1[:, 0:512], ctx.onesf[:, :], grow_a[0:1, nb:nb + 512],
                            start=True, stop=True), after=gdeps if last is None else [last])
                        last = s.add("scalar", lambda e, nb=nb: e.activation(
                            gb[:, nb:nb + 512], ctx.pb1[:, 0:512], AF.Identity), after=[mmm])
                    for nb in range(0, H0, 512):
                        mmm = s.add("tensor", lambda e, nb=nb: e.matmul(
                            ctx.pb1[:, 0:512], ctx.onesf[:, :], grow_b[0:1, nb:nb + 512],
                            start=True, stop=True), after=[last])
                        last = s.add("scalar", lambda e, nb=nb: e.activation(
                            bb[:, nb:nb + 512], ctx.pb1[:, 0:512], AF.Identity), after=[mmm])
                    m_gbb = last
                for t in range(TB):
                    def w1fn(e, t=t):
                        ins = None
                        for c in range(8):
                            for nb in range(4):
                                ins = e.matmul(
                                    ctx.p2048[:, nb * 512:(nb + 1) * 512],
                                    uT[:, c * TB * 128 + t * 128: c * TB * 128 + (t + 1) * 128],
                                    w1_sb[:, c * H0 + nb * 512: c * H0 + (nb + 1) * 512],
                                    start=(c == 0), stop=False)
                        for nb in range(4):
                            ins = e.matmul(ctx.p2048[:, nb * 512:(nb + 1) * 512],
                                           ctx.onesb[:, :], b1e[0:1, nb * 512:(nb + 1) * 512],
                                           start=False, stop=True)
                        return ins
                    w1deps = [m_wld, m_brow]
                    if kind == 0 and mp["apply"]:
                        w1deps.append(mp["apply"])
                    if kind == 1:
                        w1deps += [m for m in (mp["tsmin"], mp["tsclip"]) if m]
                    mp["w1"] = s.add("tensor", w1fn, after=w1deps)
                    if kind == 0:
                        mln = emit_ln_bn(ctx, lambda lo, w: ctx.p2048[:, lo:lo + w], 2048,
                                         BNB, [mp["w1"]])
                        mp["apply"] = s.add("scalar", lambda e: e.activation(
                            ar1[:, :], ctx.p2048[:, :], AF.Identity,
                            bias=nmr[:, :], scale=rstd[:, :]),
                            after=[mln] + ([mp["mulgb"]] if mp["mulgb"] else []))
                        mp["mulgb"] = s.add("vector", lambda e: e.tensor_mul(
                            ar2[:, :], ar1[:, :], gb[:, :]),
                            after=[mp["apply"], m_gbb] + ([mp["addbb"]] if mp["addbb"] else []))
                        mp["addbb"] = s.add("gpsimd", lambda e: e.tensor_add(
                            ar1[:, :], ar2[:, :], bb[:, :]), after=[mp["mulgb"]])
                        mp["gelu"] = s.add("scalar", lambda e: e.activation(
                            hb[:, :], ar1[:, :], AF.Gelu),
                            after=[mp["addbb"]] + ([mp["tfn"]] if mp["tfn"] else []))
                        nch = 16
                    else:
                        mp["tsmin"] = s.add("vector", lambda e: e.tensor_scalar(
                            out=ar1[:, 0:1024], in0=ctx.p2048[:, 0:1024], scalar1=7.0,
                            scalar2=None, op0=ALU.min),
                            after=[mp["w1"]] + ([mp["sig"]] if mp["sig"] else []))
                        mp["tsclip"] = s.add("vector", lambda e: e.tensor_scalar(
                            out=ar1[:, 1024:2048], in0=ctx.p2048[:, 1024:2048], scalar1=-7.0,
                            scalar2=7.0, op0=ALU.max, op1=ALU.min), after=[mp["tsmin"]])
                        mp["sig"] = s.add("scalar", lambda e: e.activation(
                            ar2[:, 0:1024], ar1[:, 0:1024], AF.Sigmoid, scale=1.702),
                            after=[mp["tsmin"]] + ([mp["mulglu"]] if mp["mulglu"] else []))
                        mp["mulglu"] = s.add("vector", lambda e: e.tensor_mul(
                            ar2[:, 1024:2048], ar1[:, 0:1024], ar2[:, 0:1024]),
                            after=[mp["sig"]])
                        mp["gelu"] = s.add("vector", lambda e: e.scalar_tensor_tensor(
                            out=hb[:, 0:1024], in0=ar1[:, 1024:2048], scalar=1.0,
                            in1=ar2[:, 1024:2048], op0=ALU.add, op1=ALU.mult),
                            after=[mp["mulglu"], mp["tsclip"]] + ([mp["tfn"]] if mp["tfn"] else []))
                        nch = 8
                    tfn, cfn = emit_transposes2(
                        ctx, hb, 0, nch, hT, [c * 128 for c in range(nch)],
                        [mp["gelu"]], [mp["w2"]] if mp["w2"] else [], mp["cfn"])
                    mp["tfn"], mp["cfn"] = tfn, cfn

                    def w2fn(e, t=t):
                        ins = None
                        for c in range(NC2):
                            for nb in range(2):
                                ins = e.matmul(
                                    ctx.pa[:, nb * 512:(nb + 1) * 512],
                                    hT[:, c * 128:(c + 1) * 128],
                                    w2_sb[:, c * 1024 + nb * 512: c * 1024 + (nb + 1) * 512],
                                    start=(c == 0), stop=False)
                        for nb in range(2):
                            ins = e.matmul(ctx.pa[:, nb * 512:(nb + 1) * 512],
                                           ctx.onesb[:, :], b2e[0:1, nb * 512:(nb + 1) * 512],
                                           start=False, stop=True)
                        return ins
                    mp["w2"] = s.add("tensor", w2fn,
                                     after=[mp["cfn"]] + ([mp["acc"]] if mp["acc"] else []))
                    mp["acc"] = s.add("vector", lambda e, t=t, ex=ex: e.scalar_tensor_tensor(
                        out=acc[:, t * 1024:(t + 1) * 1024], in0=ctx.pa[:, :],
                        scalar=cw[:, t * 8 + ex: t * 8 + ex + 1],
                        in1=acc[:, t * 1024:(t + 1) * 1024], op0=ALU.mult, op1=ALU.add),
                        after=[mp["w2"]])
                if ex + 1 < E:
                    m_wld = s.add("sync", lambda e, ex1=ex + 1: [
                        e.dma_start(out=w1_sb.ap().rearrange("p (c j) -> p c j", j=H0),
                                    in_=wbf1.rearrange("(ee c p) j -> ee p c j", ee=E, p=128)[ex1]),
                        e.dma_start(out=w2_sb.ap().rearrange("p (c j) -> p c j", j=1024),
                                    in_=wbf2.rearrange("(ee c p) j -> ee p c j", ee=E, p=128)[ex1]),
                    ], dma_n=2, after=[mp["w1"], mp["w2"]], stream="d_wld")
                    m_brow = s.add("gpsimd", lambda e, ex1=ex + 1: [
                        e.dma_start(out=b1e[:, :], in_=dr["b1"][ex1:ex1 + 1, :]),
                        e.dma_start(out=b2e[:, :], in_=dr["b2"][ex1:ex1 + 1, :]),
                    ], dma_n=2, after=[mp["w1"], mp["w2"]], stream="d_brow")
                    if kind == 0:
                        m_grow = s.add("sync", lambda e, ex1=ex + 1: [
                            e.dma_start(out=grow_a[:, :], in_=dr["ln1_g"][ex1:ex1 + 1, :]),
                            e.dma_start(out=grow_b[:, :], in_=dr["ln1_b"][ex1:ex1 + 1, :])],
                            dma_n=2, after=[m_gbb], stream="d_grow")
            s.barrier()

            for t in range(TB):
                r0 = (t0 + t) * 128
                s.add("sync", lambda e, r0=r0: [e.dma_start(out=xs[:, :], in_=xin[r0:r0 + 128, :])], dma_n=1)
                if kind == 0:
                    emit_ln_bn(ctx, lambda lo, w: xs[:, lo:lo + w], 1024, BNB, "prev")
                    s.add("vector", lambda e: e.tensor_scalar(
                        out=ar1[:, 0:1024], in0=xs[:, :], scalar1=bnmv[:, 0:1], scalar2=rstd[:, :],
                        op0=ALU.subtract, op1=ALU.mult))
                    s.add("vector", lambda e, t=t: e.tensor_add(
                        ar1[:, 1024:2048], acc[:, t * 1024:(t + 1) * 1024], ar1[:, 0:1024]))
                    emit_ln_bn(ctx, lambda lo, w: ar1[:, 1024 + lo:1024 + lo + w], 1024, BNB, "prev")
                    s.add("scalar", lambda e: e.activation(ar2[:, 0:1024], ar1[:, 1024:2048], AF.Identity,
                                                           bias=nmr[:, :], scale=rstd[:, :]))
                    s.add("vector", lambda e: e.tensor_mul(ar2[:, 1024:2048], ar2[:, 0:1024], lngb[:, :]))
                    s.add("gpsimd", lambda e: e.tensor_add(ar1[:, 0:1024], ar2[:, 1024:2048], lnbb[:, :]))
                    s.add("vector", lambda e: e.tensor_add(ar1[:, 1024:2048], ar1[:, 0:1024], xs[:, :]))
                else:
                    s.add("vector", lambda e, t=t: e.tensor_add(
                        ar1[:, 1024:2048], acc[:, t * 1024:(t + 1) * 1024], xs[:, :]))
                s.add("gpsimd", lambda e, r0=r0: [e.dma_start(out=xout[r0:r0 + 128, :], in_=ar1[:, 1024:2048])], dma_n=1)
            s.barrier()
        s.flush()



def host_tables():
    pos = np.arange(S, dtype=np.float64)[:, None]
    i = np.arange(32, dtype=np.float64)
    f0 = 10000.0 ** (2.0 * i / 64.0)
    a0 = pos / f0[None, :]
    ch0 = np.tile(np.cos(a0), (1, 16)).astype(np.float32)
    sh0 = np.tile(np.sin(a0), (1, 16)).astype(np.float32)
    cs, ss = [], []
    for sc in (1.0, 4.0, 16.0):
        f = 1.0 / (10000.0 * sc) ** (2.0 * i / 64.0)
        a = pos * f[None, :]
        cs.append(np.cos(a))
        ss.append(np.sin(a))
    ch1 = np.tile(np.mean(cs, axis=0), (1, 16)).astype(np.float32)
    sh1 = np.tile(np.mean(ss, axis=0), (1, 16)).astype(np.float32)
    return ch0, sh0, ch1, sh1


def build_stage_program(which):
    nc = bass.Bass("TRN2", target_bir_lowering=False)
    xin = nc.dram_tensor("xin", [ROWS, D], F32, kind="ExternalInput").ap()
    xout = nc.dram_tensor("xout", [ROWS, D], F32, kind="ExternalOutput").ap()
    specs = [("idn", (128, 128), BF16)]
    if which in ("a0", "all"):
        specs += [(n, (D, D), F32) for n in ("a0_wq", "a0_wk", "a0_wv", "a0_wo")]
        specs += [(n, (1, D), F32) for n in ("a0_bq", "a0_bk", "a0_bv", "a0_bo")]
        specs += [("ch0", (S, 512), F32), ("sh0", (S, 512), F32)]
    if which in ("f0", "all"):
        specs += [("f0_gate_w", (D, E), F32), ("f0_w1", (E * D, H0), F32),
                  ("f0_b1", (E, H0), F32), ("f0_ln1_g", (E, H0), F32),
                  ("f0_ln1_b", (E, H0), F32), ("f0_w2", (E * H0, D), F32),
                  ("f0_b2", (E, D), F32), ("f0_ln_g", (1, D), F32), ("f0_ln_b", (1, D), F32)]
    if which in ("a1", "all"):
        specs += [(n, (D, D), F32) for n in ("a1_wq", "a1_wk", "a1_wv", "a1_wo")]
        specs += [("ch1", (S, 512), F32), ("sh1", (S, 512), F32)]
    if which in ("f1", "all"):
        specs += [("f1_gate_w", (D, E), F32), ("f1_rbias", (1, E), F32),
                  ("f1_w1", (E * D, H0), F32), ("f1_b1", (E, H0), F32),
                  ("f1_w2", (E * I1, D), F32), ("f1_b2", (E, D), F32)]
    dr = {}
    for name, shape, dt in specs:
        dr[name] = nc.dram_tensor(name, list(shape), dt, kind="ExternalInput").ap()

    with ExitStack() as stack:
        ctx = Ctx(nc, stack)
        ctx.init_items(dr["idn"])
        ctx.s.flush()
        if which == "all":
            x1 = nc.dram_tensor("x1", [ROWS, D], F32).ap()
            x2 = nc.dram_tensor("x2", [ROWS, D], F32).ap()
            x3 = nc.dram_tensor("x3", [ROWS, D], F32).ap()
            io = {"a0": (xin, x1), "f0": (x1, x2), "a1": (x2, x3), "f1": (x3, xout)}
        else:
            io = {which: (xin, xout)}
        if which in ("a0", "all"):
            stage_attn(ctx, io["a0"][0], io["a0"][1],
                       dict(wq=dr["a0_wq"], wk=dr["a0_wk"], wv=dr["a0_wv"], wo=dr["a0_wo"],
                            bq=dr["a0_bq"], bk=dr["a0_bk"], bv=dr["a0_bv"], bo=dr["a0_bo"]),
                       True, dr["ch0"], dr["sh0"], "a0")
        if which in ("f0", "all"):
            stage_ffn(ctx, io["f0"][0], io["f0"][1],
                      dict(gate_w=dr["f0_gate_w"], w1=dr["f0_w1"], b1=dr["f0_b1"],
                           ln1_g=dr["f0_ln1_g"], ln1_b=dr["f0_ln1_b"], w2=dr["f0_w2"],
                           b2=dr["f0_b2"], ln_g=dr["f0_ln_g"], ln_b=dr["f0_ln_b"]),
                      0, "f0")
        if which in ("a1", "all"):
            stage_attn(ctx, io["a1"][0], io["a1"][1],
                       dict(wq=dr["a1_wq"], wk=dr["a1_wk"], wv=dr["a1_wv"], wo=dr["a1_wo"]),
                       False, dr["ch1"], dr["sh1"], "a1")
        if which in ("f1", "all"):
            stage_ffn(ctx, io["f1"][0], io["f1"][1],
                      dict(gate_w=dr["f1_gate_w"], router_bias=dr["f1_rbias"],
                           w1=dr["f1_w1"], b1=dr["f1_b1"], w2=dr["f1_w2"], b2=dr["f1_b2"]),
                      1, "f1")
        ctx.s.add("vector", lambda e: e.engine_nop())
        ctx.s.flush()
    return nc


def _prep_host(inputs):
    ch0, sh0, ch1, sh1 = host_tables()
    c = lambda a: np.ascontiguousarray(np.asarray(a), dtype=np.float32) if not isinstance(a, np.ndarray) or a.dtype != np.float32 or not a.flags.c_contiguous else a
    def cc(a):
        return np.ascontiguousarray(np.asarray(a, dtype=np.float32))
    g = {
        "idn": np.eye(128, dtype=ml_dtypes.bfloat16),
        "ch0": ch0, "sh0": sh0, "ch1": ch1, "sh1": sh1,
        "a0_wq": cc(np.asarray(inputs["L0_attn_wq"]).T),
        "a0_wk": cc(np.asarray(inputs["L0_attn_wk"]).T),
        "a0_wv": cc(np.asarray(inputs["L0_attn_wv"]).T),
        "a0_wo": cc(np.asarray(inputs["L0_attn_wo"]).T),
        "a0_bq": cc(inputs["L0_attn_bq"]).reshape(1, D),
        "a0_bk": cc(inputs["L0_attn_bk"]).reshape(1, D),
        "a0_bv": cc(inputs["L0_attn_bv"]).reshape(1, D),
        "a0_bo": cc(inputs["L0_attn_bo"]).reshape(1, D),
        "f0_gate_w": cc(inputs["L0_ffn_gate_w"]),
        "f0_w1": cc(inputs["L0_ffn_w1"]).reshape(E * D, H0),
        "f0_b1": cc(inputs["L0_ffn_b1"]),
        "f0_ln1_g": cc(inputs["L0_ffn_ln1_g"]),
        "f0_ln1_b": cc(inputs["L0_ffn_ln1_b"]),
        "f0_w2": cc(inputs["L0_ffn_w2"]).reshape(E * H0, D),
        "f0_b2": cc(inputs["L0_ffn_b2"]),
        "f0_ln_g": cc(inputs["L0_ffn_ln_g"]).reshape(1, D),
        "f0_ln_b": cc(inputs["L0_ffn_ln_b"]).reshape(1, D),
        "a1_wq": cc(inputs["L1_attn_Wq"]),
        "a1_wk": cc(inputs["L1_attn_Wk"]),
        "a1_wv": cc(inputs["L1_attn_Wv"]),
        "a1_wo": cc(inputs["L1_attn_Wo"]),
        "f1_gate_w": cc(np.asarray(inputs["L1_ffn_router_weight"]).T),
        "f1_rbias": cc(inputs["L1_ffn_router_bias"]).reshape(1, E),
        "f1_w1": cc(inputs["L1_ffn_gate_up_proj"]).reshape(E * D, 2 * I1),
        "f1_b1": cc(inputs["L1_ffn_gate_up_bias"]),
        "f1_w2": cc(inputs["L1_ffn_down_proj"]).reshape(E * I1, D),
        "f1_b2": cc(inputs["L1_ffn_down_bias"]),
    }
    return g


_STAGE_KEYS = {
    "a0": ["idn", "a0_wq", "a0_wk", "a0_wv", "a0_wo", "a0_bq", "a0_bk", "a0_bv",
           "a0_bo", "ch0", "sh0"],
    "f0": ["idn", "f0_gate_w", "f0_w1", "f0_b1", "f0_ln1_g", "f0_ln1_b", "f0_w2",
           "f0_b2", "f0_ln_g", "f0_ln_b"],
    "a1": ["idn", "a1_wq", "a1_wk", "a1_wv", "a1_wo", "ch1", "sh1"],
    "f1": ["idn", "f1_gate_w", "f1_rbias", "f1_w1", "f1_b1", "f1_w2", "f1_b2"],
}


FUSED = True


def kernel(**inputs):
    g = _prep_host(inputs)
    x = np.ascontiguousarray(np.asarray(inputs["L0_attn_x"], dtype=np.float32)
                             ).reshape(BATCH * S, D)
    shards = [np.ascontiguousarray(x[c * ROWS:(c + 1) * ROWS]) for c in range(NCORES)]

    stages = ("all",) if FUSED else ("a0", "f0", "a1", "f1")
    for which in stages:
        nc = build_stage_program(which)
        keys = (set().union(*_STAGE_KEYS.values()) if which == "all"
                else set(_STAGE_KEYS[which]))
        in_maps = []
        for c in range(NCORES):
            m = {k: g[k] for k in keys}
            m["xin"] = shards[c]
            in_maps.append(m)
        res = run_bass_kernel_spmd(nc, in_maps, core_ids=list(range(NCORES)))
        shards = [np.ascontiguousarray(np.asarray(r["xout"], dtype=np.float32))
                  for r in res.results]

    out = np.concatenate(shards, axis=0).reshape(BATCH, S, D)
    return out.astype(np.float32)


# version 04
# speedup vs baseline: 1.2082x; 1.0601x over previous
import numpy as np
import ml_dtypes
from contextlib import ExitStack

import concourse.bass as bass
import concourse.mybir as mybir
from concourse.bass_utils import run_bass_kernel_spmd

F32 = mybir.dt.float32
BF16 = mybir.dt.bfloat16
AF = mybir.ActivationFunctionType
ALU = mybir.AluOpType
AX = mybir.AxisListType.X

NCORES = 8
BATCH = 32
NSEQ = BATCH // NCORES
S = 2048
D = 1024
NT = S // 128
ROWS = NSEQ * S
NH = 16
E = 8
H0 = 2048
I1 = 1024
EPS = 1e-5

CONSTS = [0.0, 1.0, -1.0, EPS, 0.125, 1.702, 7.0, -7.0]

import os
_SER1 = os.environ.get("KSER1") == "1"
_SER2 = os.environ.get("KSER2") == "1"
_SER3 = os.environ.get("KSER3") == "1"


class Sched:

    ENGINES = ["sync", "gpsimd", "tensor", "vector", "scalar"]
    SEMS = ENGINES + ["gdma"]

    def __init__(self, nc, stack):
        self.nc = nc
        self.stack = stack
        self.sems = {n: stack.enter_context(nc.semaphore(f"qsem_{n}"))
                     for n in self.SEMS}
        self.totals = {n: 0 for n in self.SEMS}
        self.prev = None
        self.pending = []
        self.n_items = 0

    def stream_sem(self, name):
        if name not in self.sems:
            self.sems[name] = self.stack.enter_context(self.nc.semaphore(f"qsem_{name}"))
            self.totals[name] = 0

    def add(self, eng, fn, dma_n=0, after="prev", stream=None):
        if stream is not None:
            assert dma_n
            self.stream_sem(stream)
            sem = stream
        else:
            sem = "gdma" if (eng == "gpsimd" and dma_n) else eng
        if after == "prev":
            deps = [self.prev] if self.prev is not None else []
        elif after is None:
            deps = []
        else:
            deps = [m for m in after if m is not None]
        self.pending.append((eng, sem, fn, dma_n, deps))
        inc = 16 * dma_n if dma_n else 1
        self.totals[sem] += inc
        mark = (sem, self.totals[sem])
        self.prev = mark
        self.n_items += 1
        return mark

    def barrier(self):
        marks = [(n, v) for n, v in self.totals.items() if v > 0]
        self.prev = None
        m = self.add("vector", lambda e: e.engine_nop(), after=marks)
        self.prev = m
        return m

    def flush(self):
        if not self.pending:
            return
        per_engine = {n: [] for n in self.ENGINES}
        for it in self.pending:
            per_engine[it[0]].append(it)
        self.pending = []
        sems = self.sems

        def make(name):
            def emitter(e):
                seen = {}
                for (_, sem, fn, dma_n, deps) in per_engine[name]:
                    for (dsem, dval) in deps:
                        if seen.get(dsem, -1) < dval:
                            e.wait_ge(sems[dsem], dval)
                            seen[dsem] = dval
                    ins = fn(e)
                    if dma_n:
                        assert isinstance(ins, (list, tuple)) and len(ins) == dma_n
                        for i in ins:
                            i.then_inc(sems[sem], 16)
                    else:
                        if isinstance(ins, (list, tuple)):
                            ins = ins[-1]
                        ins.then_inc(sems[sem], 1)
            return emitter

        with self.nc.Block() as block:
            for name in self.ENGINES:
                if per_engine[name]:
                    getattr(block, name)(make(name))


class Ctx:
    def __init__(self, nc, stack):
        self.nc = nc
        self.stack = stack
        self.s = Sched(nc, stack)
        sb = lambda n, shp, dt: stack.enter_context(nc.sbuf_tensor(n, shp, dt))
        ps = lambda n, shp, dt: stack.enter_context(nc.psum_tensor(n, shp, dt))
        self.cpool = sb("cpool", [128, len(CONSTS)], F32)
        self.idn = sb("idn_sb", [128, 128], BF16)
        self.onesb = sb("onesb", [1, 128], BF16)
        self.onesf = sb("onesf", [1, 128], F32)
        self.p2048 = ps("p2048", [128, 2048], F32)
        self.pa = ps("pa", [128, 1024], F32)
        self.pb1 = ps("pb1", [128, 512], F32)
        self.ptr = ps("ptr", [128, 512], BF16)
        for i, v in enumerate(CONSTS):
            nc.const_aps.register(F32, v, self.cpool[:, i:i + 1])

    def init_items(self, idn_dram):
        s = self.s
        for i, v in enumerate(CONSTS):
            s.add("vector", lambda e, i=i, v=v: e.memset(self.cpool[:, i:i + 1], v))
        s.add("vector", lambda e: e.memset(self.onesb[:, :], 1.0))
        s.add("vector", lambda e: e.memset(self.onesf[:, :], 1.0))
        s.add("sync", lambda e: [e.dma_start(out=self.idn[:, :], in_=idn_dram)], dma_n=1)

    def sb(self, n, shp, dt, stack=None):
        return (stack or self.stack).enter_context(self.nc.sbuf_tensor(n, shp, dt))


def emit_transposes(ctx, src_sb, src_col0, nchunk, dst_sb, dst_cols):
    s = ctx.s
    for g0 in range(0, nchunk, 4):
        gn = min(4, nchunk - g0)

        def tfn(e, g0=g0, gn=gn):
            out = None
            for j in range(gn):
                c = g0 + j
                out = e.transpose(ctx.ptr[:, j * 128:(j + 1) * 128],
                                  src_sb[:, src_col0 + c * 128: src_col0 + (c + 1) * 128],
                                  ctx.idn[:, :])
            return out
        s.add("tensor", tfn)

        def cfn(e, g0=g0, gn=gn):
            ins = None
            for j in range(gn):
                c = g0 + j
                ins = e.tensor_copy(out=dst_sb[:, dst_cols[c]:dst_cols[c] + 128],
                                    in_=ctx.ptr[:, j * 128:(j + 1) * 128])
            return ins
        s.add("vector", cfn)


def emit_matmul_group(ctx, psum_ap_fn, lhsT_fn, rhs_fn, nks, bias_rhs_fn=None,
                      bias_lhsT=None):
    def fn(e):
        ins = None
        for k in range(nks):
            ins = e.matmul(psum_ap_fn(), lhsT_fn(k), rhs_fn(k),
                           start=(k == 0), stop=(k == nks - 1 and bias_rhs_fn is None))
        if bias_rhs_fn is not None:
            ins = e.matmul(psum_ap_fn(), bias_lhsT, bias_rhs_fn(),
                           start=False, stop=True)
        return ins
    ctx.s.add("tensor", fn)


def emit_ln_stats(ctx, src_ap, sq_scratch, mu, var, rstd, nmr, inv_w):
    s = ctx.s
    s.add("vector", lambda e: e.reduce_sum(mu[:, :], src_ap, axis=AX))
    s.add("scalar", lambda e: e.activation(sq_scratch, src_ap, AF.Square,
                                           accum_out=var[:, :]))
    s.add("vector", lambda e: e.tensor_scalar_mul(mu[:, :], mu[:, :], inv_w))
    s.add("vector", lambda e: e.tensor_scalar_mul(var[:, :], var[:, :], inv_w))
    s.add("vector", lambda e: e.tensor_mul(nmr[:, :], mu[:, :], mu[:, :]))
    s.add("vector", lambda e: e.tensor_sub(var[:, :], var[:, :], nmr[:, :]))
    s.add("scalar", lambda e: e.activation(var[:, :], var[:, :], AF.Sqrt, bias=EPS))
    s.add("vector", lambda e: e.reciprocal(rstd[:, :], var[:, :]))
    s.add("vector", lambda e: e.tensor_mul(nmr[:, :], mu[:, :], rstd[:, :]))
    s.add("vector", lambda e: e.tensor_scalar_mul(nmr[:, :], nmr[:, :], -1.0))


def emit_row_broadcast(ctx, row_ap_fn, dst_sb, width):
    s = ctx.s
    for nb in range(0, width, 512):
        w = min(512, width - nb)
        s.add("tensor", lambda e, nb=nb, w=w: e.matmul(
            ctx.pb1[:, 0:w], ctx.onesf[:, :], row_ap_fn(nb, w), start=True, stop=True))
        s.add("scalar", lambda e, nb=nb, w=w: e.activation(
            dst_sb[:, nb:nb + w], ctx.pb1[:, 0:w], AF.Identity))



def emit_ln_bn(ctx, src_slice_fn, W, B, deps):
    s = ctx.s
    nch = W // 512

    def bnfn(e):
        ins = None
        for j in range(nch):
            ins = e.bn_stats(B["bnst"][:, j * 6:(j + 1) * 6], src_slice_fn(j * 512, 512))
        return ins
    m1 = s.add("vector", bnfn, after=deps)
    m2 = s.add("vector", lambda e: e.bn_aggr(B["bnmv"][:, :], B["bnst"][:, 0:6 * nch]), after=[m1])
    m3 = s.add("scalar", lambda e: e.activation(B["sd"][:, :], B["bnmv"][:, 1:2], AF.Sqrt, bias=EPS), after=[m2])
    m4 = s.add("vector", lambda e: e.reciprocal(B["rstd"][:, :], B["sd"][:, :]), after=[m3])
    m5 = s.add("vector", lambda e: e.tensor_scalar(
        out=B["nmr"][:, :], in0=B["bnmv"][:, 0:1], scalar1=B["rstd"][:, :], scalar2=-1.0,
        op0=ALU.mult, op1=ALU.mult), after=[m4])
    return m5


def emit_transposes2(ctx, src_sb, src_col0, nchunk, dst_sb, dst_cols, start_deps,
                     cfn_extra_deps, prev_cfn_last):
    s = ctx.s
    tmark = None
    cmark = prev_cfn_last
    for g0 in range(0, nchunk, 4):
        gn = min(4, nchunk - g0)

        def tfn(e, g0=g0, gn=gn):
            out = None
            for j in range(gn):
                c = g0 + j
                out = e.transpose(ctx.ptr[:, j * 128:(j + 1) * 128],
                                  src_sb[:, src_col0 + c * 128: src_col0 + (c + 1) * 128],
                                  ctx.idn[:, :])
            return out
        deps = list(start_deps)
        if cmark is not None:
            deps.append(cmark)
        tmark = s.add("tensor", tfn, after=deps)

        def cfn(e, g0=g0, gn=gn):
            ins = None
            for j in range(gn):
                c = g0 + j
                ins = e.tensor_copy(out=dst_sb[:, dst_cols[c]:dst_cols[c] + 128],
                                    in_=ctx.ptr[:, j * 128:(j + 1) * 128])
            return ins
        cmark = s.add("vector", cfn, after=[tmark] + list(cfn_extra_deps))
    return tmark, cmark


def stage_attn(ctx, xin, xout, wdr, has_bias, ch_dram, sh_dram, tag):
    nc, s = ctx.nc, ctx.s
    with ExitStack() as st:
        sb = lambda n, shp, dt: ctx.sb(f"{tag}s_{n}", shp, dt, st)
        wq_sb = sb("wq", [128, 8 * 1024], BF16)
        wk_sb = sb("wk", [128, 8 * 1024], BF16)
        wv_sb = sb("wv", [128, 8 * 1024], BF16)
        QT = sb("QT", [128, 8 * 2048], BF16)
        KT = sb("KT", [128, 8 * 2048], BF16)
        Vg = sb("Vg", [128, 16 * (16 * 65)], BF16)
        Pb = sb("Pb", [128, 2048], BF16)
        ct = sb("ct", [128, 512], F32)
        sth = sb("sth", [128, 512], F32)
        xs = sb("xs", [128, 1024], F32)
        lnb = sb("lnb", [128, 1024], BF16)
        lnT = sb("lnT", [128, 1024], BF16)
        qbf = sb("qbf", [128, 1024], BF16)
        scr = sb("scr", [128, 1024], F32)
        outs = sb("outs", [128, 1024], F32)
        bnst = sb("bnst", [128, 12], F32)
        bnmv = sb("bnmv", [128, 2], F32)
        sd = sb("sd", [128, 1], F32)
        rstd = sb("rstd", [128, 1], F32)
        nmr = sb("nmr", [128, 1], F32)
        invs = sb("invs", [1, 512], F32)
        bcs = sb("bcs", [128, 512], F32)
        bob = sb("bob", [128, 1024], F32)
        brow = sb("brow", [1, 3 * 1024], BF16)
        borow = sb("borow", [1, 1024], F32)
        BNB = dict(bnst=bnst, bnmv=bnmv, sd=sd, rstd=rstd, nmr=nmr)

        def wload(dst, src):
            s.add("gpsimd", lambda e, dst=dst, src=src: [e.dma_start(
                out=dst.rearrange("p (c j) -> p c j", j=1024),
                in_=src.rearrange("(c p) j -> p c j", p=128))], dma_n=1)
        wload(wq_sb.ap(), wdr["wq"])
        wload(wk_sb.ap(), wdr["wk"])
        wload(wv_sb.ap(), wdr["wv"])
        if has_bias:
            s.add("gpsimd", lambda e: [
                e.dma_start(out=brow[:, 0:1024], in_=wdr["bq"]),
                e.dma_start(out=brow[:, 1024:2048], in_=wdr["bk"]),
                e.dma_start(out=brow[:, 2048:3072], in_=wdr["bv"]),
            ], dma_n=3)
            s.add("sync", lambda e: [e.dma_start(out=borow[:, :], in_=wdr["bo"])], dma_n=1)
            emit_row_broadcast(ctx, lambda nb, w: borow[0:1, nb:nb + w], bob, 1024)
        else:
            s.add("vector", lambda e: e.memset(bob[:, :], 0.0))
        s.barrier()

        Vg4 = Vg.ap().rearrange("p (kc h j) -> p kc h j", h=16, j=65)

        for sq in range(NSEQ):
            r0s = sq * S
            mk = {"ts": None, "cfnln": None, "tfnln": None,
                  "qmm": None, "kmm": None, "vmm": None,
                  "ropeQ4": None, "ropeK4": None, "gpQ2": None, "gpK2": None,
                  "tfnQ": None, "cfnQ": None, "tfnK": None, "cfnK": None,
                  "vg": None, "dmax": None, "dmat": None}
            mk["dmax"] = s.add("sync", lambda e, r0s=r0s: [e.dma_start(out=xs[:, :], in_=xin[r0s:r0s + 128, :])], dma_n=1, stream="d_xs")
            mk["dmat"] = s.add("sync", lambda e: [
                e.dma_start(out=ct[:, :], in_=ch_dram[0:128, :]),
                e.dma_start(out=sth[:, :], in_=sh_dram[0:128, :])], dma_n=2, stream="d_tab")
            for t in range(NT):
                r0 = r0s + t * 128
                mln = emit_ln_bn(ctx, lambda lo, w: xs[:, lo:lo + w], 1024, BNB,
                                 [mk["dmax"]])
                m_ts = s.add("vector", lambda e: e.tensor_scalar(
                    out=lnb[:, :], in0=xs[:, :], scalar1=bnmv[:, 0:1], scalar2=rstd[:, :],
                    op0=ALU.subtract, op1=ALU.mult),
                    after=[mln] + ([mk["tfnln"]] if mk["tfnln"] else []))
                if t + 1 < NT:
                    mk["dmax"] = s.add("sync", lambda e, r1=r0 + 128: [
                        e.dma_start(out=xs[:, :], in_=xin[r1:r1 + 128, :])], dma_n=1, after=[m_ts], stream="d_xs")
                tfnln, cfnln = emit_transposes2(
                    ctx, lnb, 0, 8, lnT, [c * 128 for c in range(8)],
                    [m_ts], [mk["vmm"]] if mk["vmm"] else [], mk["cfnK"])
                mk["tfnln"], mk["cfnln"] = tfnln, cfnln

                def qkfn2(e, dst0, w_sb, boff):
                    ins = None
                    for nb in range(2):
                        nk = 8
                        for c in range(nk):
                            last = (c == nk - 1) and not has_bias
                            ins = e.matmul(
                                ctx.p2048[:, dst0 + nb * 512: dst0 + (nb + 1) * 512],
                                lnT[:, c * 128:(c + 1) * 128],
                                w_sb[:, c * 1024 + nb * 512: c * 1024 + (nb + 1) * 512],
                                start=(c == 0), stop=last)
                        if has_bias:
                            ins = e.matmul(ctx.p2048[:, dst0 + nb * 512: dst0 + (nb + 1) * 512],
                                           ctx.onesb[:, :],
                                           brow[0:1, boff + nb * 512: boff + (nb + 1) * 512],
                                           start=False, stop=True)
                    return ins
                mk["qmm"] = s.add("tensor", lambda e: qkfn2(e, 0, wq_sb, 0),
                                  after=[cfnln] + ([mk["ropeQ4"]] if mk["ropeQ4"] else []))
                mk["kmm"] = s.add("tensor", lambda e: qkfn2(e, 1024, wk_sb, 1024),
                                  after=[cfnln] + ([mk["ropeK4"]] if mk["ropeK4"] else []))

                def vfn(e):
                    ins = None
                    for nb in range(2):
                        for c in range(8):
                            last = (c == 7) and not has_bias
                            ins = e.matmul(ctx.pa[:, nb * 512:(nb + 1) * 512],
                                           lnT[:, c * 128:(c + 1) * 128],
                                           wv_sb[:, c * 1024 + nb * 512: c * 1024 + (nb + 1) * 512],
                                           start=(c == 0), stop=last)
                        if has_bias:
                            ins = e.matmul(ctx.pa[:, nb * 512:(nb + 1) * 512],
                                           ctx.onesb[:, :],
                                           brow[0:1, 2048 + nb * 512: 2048 + (nb + 1) * 512],
                                           start=False, stop=True)
                    return ins
                mk["vmm"] = s.add("tensor", vfn,
                                  after=[cfnln] + ([mk["vg"]] if mk["vg"] else []))

                for half in (0, 1):
                    pe = ctx.p2048.ap().rearrange("p (c j two) -> p c j two", c=2, two=2)[:, half, :, 0]
                    po = ctx.p2048.ap().rearrange("p (c j two) -> p c j two", c=2, two=2)[:, half, :, 1]
                    de = qbf.ap().rearrange("p (j two) -> p j two", two=2)[:, :, 0]
                    do = qbf.ap().rearrange("p (j two) -> p j two", two=2)[:, :, 1]
                    mmm = mk["qmm"] if half == 0 else mk["kmm"]
                    scrdep = mk["gpK2"] if half == 0 else mk["gpQ2"]
                    qbfdep = (mk["tfnK"] if half == 0 else mk["tfnQ"])
                    mu1 = s.add("vector", lambda e, pe=pe: e.tensor_mul(scr[:, 0:512], pe, ct[:, :]),
                                after=[mmm, mk["dmat"]] + ([scrdep] if scrdep else []))
                    mu2 = s.add("vector", lambda e, po=po: e.tensor_mul(scr[:, 512:1024], po, sth[:, :]),
                                after=[mu1])
                    g1 = s.add("gpsimd", lambda e, de=de: e.tensor_tensor(
                        out=de, in0=scr[:, 0:512], in1=scr[:, 512:1024], op=ALU.subtract),
                        after=[mu2] + ([qbfdep] if qbfdep else []))
                    mu3 = s.add("vector", lambda e, po=po: e.tensor_mul(scr[:, 0:512], po, ct[:, :]),
                                after=[g1])
                    mu4 = s.add("vector", lambda e, pe=pe: e.tensor_mul(scr[:, 512:1024], pe, sth[:, :]),
                                after=[mu3])
                    g2 = s.add("gpsimd", lambda e, do=do: e.tensor_tensor(
                        out=do, in0=scr[:, 0:512], in1=scr[:, 512:1024], op=ALU.add),
                        after=[mu4])
                    if half == 0:
                        mk["ropeQ4"], mk["gpQ2"] = mu4, g2
                        tq, cq = emit_transposes2(
                            ctx, qbf, 0, 8, QT, [c * 2048 + t * 128 for c in range(8)],
                            [g1, g2], [], mk["cfnln"])
                        mk["tfnQ"], mk["cfnQ"] = tq, cq
                    else:
                        mk["ropeK4"], mk["gpK2"] = mu4, g2
                        tk, ck = emit_transposes2(
                            ctx, qbf, 0, 8, KT, [c * 2048 + t * 128 for c in range(8)],
                            [g1, g2], [], mk["cfnQ"])
                        mk["tfnK"], mk["cfnK"] = tk, ck
                if t + 1 < NT:
                    mk["dmat"] = s.add("sync", lambda e, t1=t + 1: [
                        e.dma_start(out=ct[:, :], in_=ch_dram[t1 * 128:(t1 + 1) * 128, :]),
                        e.dma_start(out=sth[:, :], in_=sh_dram[t1 * 128:(t1 + 1) * 128, :])],
                        dma_n=2, after=[mk["ropeK4"]], stream="d_tab")
                mo = s.add("vector", lambda e, t=t: e.memset(Vg4[:, t, :, 64:65], 1.0),
                           after=[mk["vmm"]])
                mk["vg"] = s.add("vector", lambda e, t=t: e.tensor_copy(
                    out=Vg4[:, t, :, 0:64],
                    in_=ctx.pa.ap().rearrange("p (h j) -> p h j", j=64)), after=[mo])
                if _SER1:
                    bm = s.barrier()
                    for k in mk:
                        if mk[k] is not None:
                            mk[k] = bm
            m_bar = s.barrier()

            m_norm = None
            m_actc = None
            m_exp = [None, None]
            m_pv = [None, None]
            for qb in range(4):
                for h in range(NH):
                    c = h // 2
                    r = 64 * (h % 2)
                    m_pv_last = None
                    for g in range(8):
                        p = g % 2

                        def sfn(e, g=g, p=p, c=c, r=r, qb=qb):
                            ins = None
                            for j in range(2):
                                kc = g * 2 + j
                                ins = e.matmul(
                                    ctx.p2048[:, p * 1024 + j * 512: p * 1024 + (j + 1) * 512],
                                    KT[r:r + 64, c * 2048 + kc * 128: c * 2048 + (kc + 1) * 128],
                                    QT[r:r + 64, c * 2048 + qb * 512: c * 2048 + (qb + 1) * 512],
                                    start=True, stop=True)
                            return ins
                        m_s = s.add("tensor", sfn, after=[m_exp[p] if m_exp[p] else m_bar])
                        m_exp[p] = s.add("scalar", lambda e, p=p: e.activation(
                            Pb[:, p * 1024:(p + 1) * 1024],
                            ctx.p2048[:, p * 1024:(p + 1) * 1024], AF.Exp, scale=0.125),
                            after=[m_s] + ([m_pv[p]] if m_pv[p] else []))

                        def pvfn(e, g=g, p=p, h=h):
                            ins = None
                            for j in range(2):
                                kc = g * 2 + j
                                ins = e.matmul(
                                    ctx.pb1[0:65, :],
                                    Vg[:, kc * 1040 + h * 65: kc * 1040 + (h + 1) * 65],
                                    Pb[:, p * 1024 + j * 512: p * 1024 + (j + 1) * 512],
                                    start=(kc == 0), stop=(kc == 15))
                            return ins
                        pvdeps = [m_exp[p]]
                        if g == 0:
                            pvdeps.append(m_norm if m_norm else m_bar)
                        m_pv[p] = s.add("tensor", pvfn, after=pvdeps)
                        m_pv_last = m_pv[p]
                    m_rc = s.add("vector", lambda e: e.reciprocal(invs[:, :], ctx.pb1[64:65, :]),
                                 after=[m_pv_last])
                    m_bc = s.add("tensor", lambda e: e.matmul(
                        ctx.pa[:, 0:512], ctx.onesf[:, :], invs[:, :], start=True, stop=True),
                        after=[m_rc] + ([m_actc] if m_actc else []))
                    m_actc = s.add("scalar", lambda e: e.activation(
                        bcs[:, :], ctx.pa[:, 0:512], AF.Identity),
                        after=[m_bc] + ([m_norm] if m_norm else []))
                    m_norm = s.add("vector", lambda e, c=c, r=r, qb=qb: e.tensor_tensor(
                        out=QT[r:r + 64, c * 2048 + qb * 512: c * 2048 + (qb + 1) * 512],
                        in0=ctx.pb1[0:64, :], in1=bcs[0:64, :], op=ALU.mult),
                        after=[m_actc])
                    if _SER2:
                        bm = s.barrier()
                        m_norm = m_actc = bm
                        m_exp = [bm, bm]
                        m_pv = [bm, bm]
            s.barrier()

            s.add("gpsimd", lambda e: [e.dma_start(
                out=Vg.ap()[:, 0:8192].rearrange("p (c j) -> p c j", j=1024),
                in_=wdr["wo"].rearrange("(c p) j -> p c j", p=128))], dma_n=1, stream="d_wo")
            m_wo = s.barrier()
            m_stt = None
            m_add = None
            m_out = None
            m_x = s.add("sync", lambda e, r0s=r0s: [e.dma_start(out=xs[:, :], in_=xin[r0s:r0s + 128, :])],
                        dma_n=1, after=[m_wo], stream="d_xs")
            for t in range(NT):
                r0 = r0s + t * 128

                def wofn(e, t=t):
                    ins = None
                    for nb in range(2):
                        for c in range(8):
                            ins = e.matmul(
                                ctx.pa[:, nb * 512:(nb + 1) * 512],
                                QT[:, c * 2048 + t * 128: c * 2048 + (t + 1) * 128],
                                Vg[:, c * 1024 + nb * 512: c * 1024 + (nb + 1) * 512],
                                start=(c == 0), stop=(c == 7))
                    return ins
                m_mm = s.add("tensor", wofn, after=[m_stt] if m_stt else [m_wo])
                m_stt = s.add("vector", lambda e: e.scalar_tensor_tensor(
                    out=scr[:, :], in0=ctx.pa[:, :], scalar=1.0, in1=bob[:, :],
                    op0=ALU.mult, op1=ALU.add),
                    after=[m_mm] + ([m_add] if m_add else []))
                m_add = s.add("vector", lambda e: e.tensor_add(outs[:, :], scr[:, :], xs[:, :]),
                              after=[m_stt, m_x] + ([m_out] if m_out else []))
                if t + 1 < NT:
                    m_x = s.add("sync", lambda e, r1=r0 + 128: [
                        e.dma_start(out=xs[:, :], in_=xin[r1:r1 + 128, :])], dma_n=1, after=[m_add], stream="d_xs")
                m_out = s.add("gpsimd", lambda e, r0=r0: [
                    e.dma_start(out=xout[r0:r0 + 128, :], in_=outs[:, :])], dma_n=1,
                    after=[m_add], stream="d_out")
                if _SER3:
                    bm = s.barrier()
                    m_stt = m_add = m_out = m_x = bm
            s.barrier()
        s.flush()



def stage_ffn(ctx, xin, xout, dr, kind, tag):
    nc, s = ctx.nc, ctx.s
    TB = 8
    NB = (NSEQ * NT) // TB
    NC2 = 16 if kind == 0 else 8
    with ExitStack() as st:
        sb = lambda n, shp, dt: ctx.sb(f"{tag}s_{n}", shp, dt, st)
        w1_sb = sb("w1", [128, 8 * H0], BF16)
        w2_sb = sb("w2", [128, NC2 * 1024], BF16)
        uT = sb("uT", [128, 8 * (TB * 128)], BF16)
        acc = sb("acc", [128, TB * 1024], F32)
        cw = sb("cw", [128, TB * 8], F32)
        gwrow = sb("gwrow", [128, 64], BF16)
        grbias = sb("grbias", [1, 8], BF16)
        b1e = sb("b1e", [1, H0], BF16)
        b2e = sb("b2e", [1, 1024], BF16)
        grow_a = sb("grow_a", [1, H0], F32)
        grow_b = sb("grow_b", [1, H0], F32)
        gb = sb("gb", [128, H0], F32)
        bb = sb("bb", [128, H0], F32)
        lngb = sb("lngb", [128, 1024], F32) if kind == 0 else None
        lnbb = sb("lnbb", [128, 1024], F32) if kind == 0 else None
        xs = sb("xs", [128, 1024], F32)
        ub = sb("ub", [128, 1024], BF16)
        ar1 = sb("ar1", [128, 2048], F32)
        ar2 = sb("ar2", [128, 2048], F32)
        hb = sb("hb", [128, 2048], BF16)
        hT = sb("hT", [128, 2048], BF16)
        bnst = sb("bnst", [128, 24], F32)
        bnmv = sb("bnmv", [128, 2], F32)
        sd = sb("sd", [128, 1], F32)
        rstd = sb("rstd", [128, 1], F32)
        nmr = sb("nmr", [128, 1], F32)
        m1 = sb("m1", [128, 1], F32)
        m2 = sb("m2", [128, 1], F32)
        gsum = sb("gsum", [128, 1], F32)
        gates = sb("gates", [128, 8], F32)
        msk = sb("msk", [128, 8], F32)
        gm = sb("gm", [128, 8], F32)
        BNB = dict(bnst=bnst, bnmv=bnmv, sd=sd, rstd=rstd, nmr=nmr)

        wbf1 = nc.dram_tensor(f"{tag}_wbf1", [E * 1024, H0], BF16).ap()
        w2rows = 2048 if kind == 0 else 1024
        wbf2 = nc.dram_tensor(f"{tag}_wbf2", [E * w2rows, 1024], BF16).ap()

        def preconv(src, dst, rows, cols):
            for r in range(0, rows, 128):
                s.add("sync", lambda e, r=r: [e.dma_start(out=ar1[:, 0:cols], in_=src[r:r + 128, :])], dma_n=1)
                s.add("vector", lambda e: e.tensor_copy(out=hb[:, 0:cols], in_=ar1[:, 0:cols]))
                s.add("gpsimd", lambda e, r=r: [e.dma_start(out=dst[r:r + 128, :], in_=hb[:, 0:cols])], dma_n=1)
        preconv(dr["w1"], wbf1, E * 1024, H0)
        preconv(dr["w2"], wbf2, E * w2rows, 1024)

        s.add("gpsimd", lambda e: [e.dma_start(
            out=gwrow.ap().rearrange("p (c j) -> p c j", j=8),
            in_=dr["gate_w"].rearrange("(c p) j -> p c j", p=128))], dma_n=1)
        if kind == 0:
            s.add("sync", lambda e: [
                e.dma_start(out=grow_a[:, 0:1024], in_=dr["ln_g"]),
                e.dma_start(out=grow_b[:, 0:1024], in_=dr["ln_b"])], dma_n=2)
            emit_row_broadcast(ctx, lambda nb, w: grow_a[0:1, nb:nb + w], lngb, 1024)
            emit_row_broadcast(ctx, lambda nb, w: grow_b[0:1, nb:nb + w], lnbb, 1024)
        else:
            s.add("gpsimd", lambda e: [e.dma_start(out=grbias[:, :], in_=dr["router_bias"])], dma_n=1)
        s.barrier()

        for b in range(NB):
            t0 = b * TB
            for t in range(TB):
                r0 = (t0 + t) * 128
                s.add("sync", lambda e, r0=r0: [e.dma_start(out=xs[:, :], in_=xin[r0:r0 + 128, :])], dma_n=1)
                emit_ln_bn(ctx, lambda lo, w: xs[:, lo:lo + w], 1024, BNB, "prev")
                s.add("vector", lambda e: e.tensor_scalar(
                    out=ub[:, :], in0=xs[:, :], scalar1=bnmv[:, 0:1], scalar2=rstd[:, :],
                    op0=ALU.subtract, op1=ALU.mult))
                emit_transposes(ctx, ub, 0, 8, uT, [c * TB * 128 + t * 128 for c in range(8)])
                s.add("vector", lambda e, t=t: e.memset(acc[:, t * 1024:(t + 1) * 1024], 0.0))

                def glog(e, t=t):
                    ins = None
                    for c in range(8):
                        ins = e.matmul(ctx.pb1[:, 0:8],
                                       uT[:, c * TB * 128 + t * 128: c * TB * 128 + (t + 1) * 128],
                                       gwrow[:, c * 8:(c + 1) * 8],
                                       start=(c == 0), stop=(c == 7 and kind == 0))
                    if kind == 1:
                        ins = e.matmul(ctx.pb1[:, 0:8], ctx.onesb[:, :], grbias[:, :],
                                       start=False, stop=True)
                    return ins
                s.add("tensor", glog)
                s.add("vector", lambda e: e.reduce_max(m1[:, :], ctx.pb1[:, 0:8], axis=AX))
                s.add("vector", lambda e: e.tensor_scalar_mul(m2[:, :], m1[:, :], -1.0))
                s.add("scalar", lambda e: e.activation(gates[:, :], ctx.pb1[:, 0:8], AF.Exp,
                                                       bias=m2[:, :], accum_out=gsum[:, :]))
                s.add("vector", lambda e: e.reciprocal(gsum[:, :], gsum[:, :]))
                s.add("vector", lambda e: e.tensor_scalar_mul(gates[:, :], gates[:, :], gsum[:, :]))
                s.add("vector", lambda e: e.reduce_max(m1[:, :], gates[:, :], axis=AX))
                s.add("vector", lambda e: e.tensor_scalar(
                    out=msk[:, :], in0=gates[:, :], scalar1=m1[:, :], scalar2=None, op0=ALU.is_ge))
                s.add("vector", lambda e: e.scalar_tensor_tensor(
                    out=gm[:, :], in0=msk[:, :], scalar=-1e30, in1=gates[:, :],
                    op0=ALU.mult, op1=ALU.add))
                s.add("vector", lambda e: e.reduce_max(m2[:, :], gm[:, :], axis=AX))
                s.add("vector", lambda e: e.tensor_scalar(
                    out=msk[:, :], in0=gates[:, :], scalar1=m2[:, :], scalar2=None, op0=ALU.is_ge))
                s.add("vector", lambda e, t=t: e.tensor_mul(cw[:, t * 8:(t + 1) * 8], gates[:, :], msk[:, :]))
                if kind == 0:
                    s.add("vector", lambda e: e.tensor_add(m1[:, :], m1[:, :], m2[:, :]))
                    s.add("vector", lambda e: e.reciprocal(m1[:, :], m1[:, :]))
                    s.add("vector", lambda e, t=t: e.tensor_scalar_mul(
                        cw[:, t * 8:(t + 1) * 8], cw[:, t * 8:(t + 1) * 8], m1[:, :]))
            m_ph1 = s.barrier()

            m_wld = s.add("sync", lambda e: [
                e.dma_start(out=w1_sb.ap().rearrange("p (c j) -> p c j", j=H0),
                            in_=wbf1.rearrange("(ee c p) j -> ee p c j", ee=E, p=128)[0]),
                e.dma_start(out=w2_sb.ap().rearrange("p (c j) -> p c j", j=1024),
                            in_=wbf2.rearrange("(ee c p) j -> ee p c j", ee=E, p=128)[0]),
            ], dma_n=2, after=[m_ph1], stream="d_wld")
            m_brow = s.add("gpsimd", lambda e: [
                e.dma_start(out=b1e[:, :], in_=dr["b1"][0:1, :]),
                e.dma_start(out=b2e[:, :], in_=dr["b2"][0:1, :]),
            ], dma_n=2, after=[m_ph1], stream="d_brow")
            if kind == 0:
                m_grow = s.add("sync", lambda e: [
                    e.dma_start(out=grow_a[:, :], in_=dr["ln1_g"][0:1, :]),
                    e.dma_start(out=grow_b[:, :], in_=dr["ln1_b"][0:1, :])], dma_n=2,
                    after=[m_ph1], stream="d_grow")
            m_gbb = None
            mp = {"w1": None, "w2": None, "apply": None, "mulgb": None, "addbb": None,
                  "gelu": None, "tfn": None, "cfn": None, "acc": None, "ev": None,
                  "tsmin": None, "tsclip": None, "sig": None, "mulglu": None, "stt": None,
                  "ex": 0, "b2dep": None}
            for ex in range(E):
                if kind == 0:
                    gdeps = [m_grow] + [m for m in (mp["mulgb"], mp["addbb"], m_gbb) if m]
                    last = None
                    for nb in range(0, H0, 512):
                        mmm = s.add("tensor", lambda e, nb=nb: e.matmul(
                            ctx.pb1[:, 0:512], ctx.onesf[:, :], grow_a[0:1, nb:nb + 512],
                            start=True, stop=True), after=gdeps if last is None else [last])
                        last = s.add("scalar", lambda e, nb=nb: e.activation(
                            gb[:, nb:nb + 512], ctx.pb1[:, 0:512], AF.Identity), after=[mmm])
                    for nb in range(0, H0, 512):
                        mmm = s.add("tensor", lambda e, nb=nb: e.matmul(
                            ctx.pb1[:, 0:512], ctx.onesf[:, :], grow_b[0:1, nb:nb + 512],
                            start=True, stop=True), after=[last])
                        last = s.add("scalar", lambda e, nb=nb: e.activation(
                            bb[:, nb:nb + 512], ctx.pb1[:, 0:512], AF.Identity), after=[mmm])
                    m_gbb = last
                def emit_tr_w2_acc(tp):
                    nch = 16 if kind == 0 else 8
                    tfn, cfn = emit_transposes2(
                        ctx, hb, 0, nch, hT, [c * 128 for c in range(nch)],
                        [mp["gelu"]], [mp["w2"]] if mp["w2"] else [], mp["cfn"])
                    mp["tfn"], mp["cfn"] = tfn, cfn

                    def w2fn(e):
                        ins = None
                        for c in range(NC2):
                            for nb in range(2):
                                ins = e.matmul(
                                    ctx.pa[:, nb * 512:(nb + 1) * 512],
                                    hT[:, c * 128:(c + 1) * 128],
                                    w2_sb[:, c * 1024 + nb * 512: c * 1024 + (nb + 1) * 512],
                                    start=(c == 0), stop=False)
                        for nb in range(2):
                            ins = e.matmul(ctx.pa[:, nb * 512:(nb + 1) * 512],
                                           ctx.onesb[:, :], b2e[0:1, nb * 512:(nb + 1) * 512],
                                           start=False, stop=True)
                        return ins
                    mp["w2"] = s.add("tensor", w2fn,
                                     after=[mp["cfn"], mp["b2dep"]] + ([mp["acc"]] if mp["acc"] else []))
                    mp["acc"] = s.add("vector", lambda e, tp=tp, exp=mp["ex"]: e.scalar_tensor_tensor(
                        out=acc[:, tp * 1024:(tp + 1) * 1024], in0=ctx.pa[:, :],
                        scalar=cw[:, tp * 8 + exp: tp * 8 + exp + 1],
                        in1=acc[:, tp * 1024:(tp + 1) * 1024], op0=ALU.mult, op1=ALU.add),
                        after=[mp["w2"]])

                mp["ex"] = ex
                mp["b2dep"] = m_brow
                for t in range(TB):
                    def w1fn(e, t=t):
                        ins = None
                        for c in range(8):
                            for nb in range(4):
                                ins = e.matmul(
                                    ctx.p2048[:, nb * 512:(nb + 1) * 512],
                                    uT[:, c * TB * 128 + t * 128: c * TB * 128 + (t + 1) * 128],
                                    w1_sb[:, c * H0 + nb * 512: c * H0 + (nb + 1) * 512],
                                    start=(c == 0), stop=False)
                        for nb in range(4):
                            ins = e.matmul(ctx.p2048[:, nb * 512:(nb + 1) * 512],
                                           ctx.onesb[:, :], b1e[0:1, nb * 512:(nb + 1) * 512],
                                           start=False, stop=True)
                        return ins
                    w1deps = [m_wld, m_brow]
                    if kind == 0 and mp["ev"]:
                        w1deps.append(mp["ev"])
                    if kind == 1:
                        w1deps += [m for m in (mp["tsmin"], mp["tsclip"]) if m]
                    mp["w1"] = s.add("tensor", w1fn, after=w1deps)
                    if t > 0:
                        emit_tr_w2_acc(t - 1)
                    if kind == 0:
                        mp["ev"] = s.add("scalar", lambda e: e.activation(
                            ar1[:, :], ctx.p2048[:, :], AF.Identity),
                            after=[mp["w1"]] + ([mp["addbb"]] if mp["addbb"] else []))
                        mln = emit_ln_bn(ctx, lambda lo, w: ar1[:, lo:lo + w], 2048,
                                         BNB, [mp["ev"]])
                        mp["apply"] = s.add("vector", lambda e: e.tensor_scalar(
                            out=ar2[:, :], in0=ar1[:, :], scalar1=bnmv[:, 0:1],
                            scalar2=rstd[:, :], op0=ALU.subtract, op1=ALU.mult),
                            after=[mln] + ([mp["gelu"]] if mp["gelu"] else []))
                        mp["mulgb"] = s.add("vector", lambda e: e.tensor_mul(
                            ar1[:, :], ar2[:, :], gb[:, :]),
                            after=[mp["apply"], m_gbb])
                        mp["addbb"] = s.add("gpsimd", lambda e: e.tensor_add(
                            ar2[:, :], ar1[:, :], bb[:, :]), after=[mp["mulgb"]])
                        mp["gelu"] = s.add("scalar", lambda e: e.activation(
                            hb[:, :], ar2[:, :], AF.Gelu),
                            after=[mp["addbb"]] + ([mp["tfn"]] if mp["tfn"] else []))
                    else:
                        mp["tsmin"] = s.add("vector", lambda e: e.tensor_scalar(
                            out=ar1[:, 0:1024], in0=ctx.p2048[:, 0:1024], scalar1=7.0,
                            scalar2=None, op0=ALU.min),
                            after=[mp["w1"]] + ([mp["sig"]] if mp["sig"] else []))
                        mp["tsclip"] = s.add("vector", lambda e: e.tensor_scalar(
                            out=ar1[:, 1024:2048], in0=ctx.p2048[:, 1024:2048], scalar1=-7.0,
                            scalar2=7.0, op0=ALU.max, op1=ALU.min), after=[mp["tsmin"]])
                        mp["sig"] = s.add("scalar", lambda e: e.activation(
                            ar2[:, 0:1024], ar1[:, 0:1024], AF.Sigmoid, scale=1.702),
                            after=[mp["tsmin"]] + ([mp["mulglu"]] if mp["mulglu"] else []))
                        mp["mulglu"] = s.add("vector", lambda e: e.tensor_mul(
                            ar2[:, 1024:2048], ar1[:, 0:1024], ar2[:, 0:1024]),
                            after=[mp["sig"]])
                        mp["gelu"] = s.add("vector", lambda e: e.scalar_tensor_tensor(
                            out=hb[:, 0:1024], in0=ar1[:, 1024:2048], scalar=1.0,
                            in1=ar2[:, 1024:2048], op0=ALU.add, op1=ALU.mult),
                            after=[mp["mulglu"], mp["tsclip"]] + ([mp["tfn"]] if mp["tfn"] else []))
                emit_tr_w2_acc(TB - 1)
                if ex + 1 < E:
                    m_wld = s.add("sync", lambda e, ex1=ex + 1: [
                        e.dma_start(out=w1_sb.ap().rearrange("p (c j) -> p c j", j=H0),
                                    in_=wbf1.rearrange("(ee c p) j -> ee p c j", ee=E, p=128)[ex1]),
                        e.dma_start(out=w2_sb.ap().rearrange("p (c j) -> p c j", j=1024),
                                    in_=wbf2.rearrange("(ee c p) j -> ee p c j", ee=E, p=128)[ex1]),
                    ], dma_n=2, after=[mp["w1"], mp["w2"]], stream="d_wld")
                    m_brow = s.add("gpsimd", lambda e, ex1=ex + 1: [
                        e.dma_start(out=b1e[:, :], in_=dr["b1"][ex1:ex1 + 1, :]),
                        e.dma_start(out=b2e[:, :], in_=dr["b2"][ex1:ex1 + 1, :]),
                    ], dma_n=2, after=[mp["w1"], mp["w2"]], stream="d_brow")
                    if kind == 0:
                        m_grow = s.add("sync", lambda e, ex1=ex + 1: [
                            e.dma_start(out=grow_a[:, :], in_=dr["ln1_g"][ex1:ex1 + 1, :]),
                            e.dma_start(out=grow_b[:, :], in_=dr["ln1_b"][ex1:ex1 + 1, :])],
                            dma_n=2, after=[m_gbb], stream="d_grow")
            s.barrier()

            for t in range(TB):
                r0 = (t0 + t) * 128
                s.add("sync", lambda e, r0=r0: [e.dma_start(out=xs[:, :], in_=xin[r0:r0 + 128, :])], dma_n=1)
                if kind == 0:
                    emit_ln_bn(ctx, lambda lo, w: xs[:, lo:lo + w], 1024, BNB, "prev")
                    s.add("vector", lambda e: e.tensor_scalar(
                        out=ar1[:, 0:1024], in0=xs[:, :], scalar1=bnmv[:, 0:1], scalar2=rstd[:, :],
                        op0=ALU.subtract, op1=ALU.mult))
                    s.add("vector", lambda e, t=t: e.tensor_add(
                        ar1[:, 1024:2048], acc[:, t * 1024:(t + 1) * 1024], ar1[:, 0:1024]))
                    emit_ln_bn(ctx, lambda lo, w: ar1[:, 1024 + lo:1024 + lo + w], 1024, BNB, "prev")
                    s.add("scalar", lambda e: e.activation(ar2[:, 0:1024], ar1[:, 1024:2048], AF.Identity,
                                                           bias=nmr[:, :], scale=rstd[:, :]))
                    s.add("vector", lambda e: e.tensor_mul(ar2[:, 1024:2048], ar2[:, 0:1024], lngb[:, :]))
                    s.add("gpsimd", lambda e: e.tensor_add(ar1[:, 0:1024], ar2[:, 1024:2048], lnbb[:, :]))
                    s.add("vector", lambda e: e.tensor_add(ar1[:, 1024:2048], ar1[:, 0:1024], xs[:, :]))
                else:
                    s.add("vector", lambda e, t=t: e.tensor_add(
                        ar1[:, 1024:2048], acc[:, t * 1024:(t + 1) * 1024], xs[:, :]))
                s.add("gpsimd", lambda e, r0=r0: [e.dma_start(out=xout[r0:r0 + 128, :], in_=ar1[:, 1024:2048])], dma_n=1)
            s.barrier()
        s.flush()



def host_tables():
    pos = np.arange(S, dtype=np.float64)[:, None]
    i = np.arange(32, dtype=np.float64)
    f0 = 10000.0 ** (2.0 * i / 64.0)
    a0 = pos / f0[None, :]
    ch0 = np.tile(np.cos(a0), (1, 16)).astype(np.float32)
    sh0 = np.tile(np.sin(a0), (1, 16)).astype(np.float32)
    cs, ss = [], []
    for sc in (1.0, 4.0, 16.0):
        f = 1.0 / (10000.0 * sc) ** (2.0 * i / 64.0)
        a = pos * f[None, :]
        cs.append(np.cos(a))
        ss.append(np.sin(a))
    ch1 = np.tile(np.mean(cs, axis=0), (1, 16)).astype(np.float32)
    sh1 = np.tile(np.mean(ss, axis=0), (1, 16)).astype(np.float32)
    return ch0, sh0, ch1, sh1


def build_stage_program(which):
    nc = bass.Bass("TRN2", target_bir_lowering=False)
    xin = nc.dram_tensor("xin", [ROWS, D], F32, kind="ExternalInput").ap()
    xout = nc.dram_tensor("xout", [ROWS, D], F32, kind="ExternalOutput").ap()
    specs = [("idn", (128, 128), BF16)]
    if which in ("a0", "all"):
        specs += [(n, (D, D), F32) for n in ("a0_wq", "a0_wk", "a0_wv", "a0_wo")]
        specs += [(n, (1, D), F32) for n in ("a0_bq", "a0_bk", "a0_bv", "a0_bo")]
        specs += [("ch0", (S, 512), F32), ("sh0", (S, 512), F32)]
    if which in ("f0", "all"):
        specs += [("f0_gate_w", (D, E), F32), ("f0_w1", (E * D, H0), F32),
                  ("f0_b1", (E, H0), F32), ("f0_ln1_g", (E, H0), F32),
                  ("f0_ln1_b", (E, H0), F32), ("f0_w2", (E * H0, D), F32),
                  ("f0_b2", (E, D), F32), ("f0_ln_g", (1, D), F32), ("f0_ln_b", (1, D), F32)]
    if which in ("a1", "all"):
        specs += [(n, (D, D), F32) for n in ("a1_wq", "a1_wk", "a1_wv", "a1_wo")]
        specs += [("ch1", (S, 512), F32), ("sh1", (S, 512), F32)]
    if which in ("f1", "all"):
        specs += [("f1_gate_w", (D, E), F32), ("f1_rbias", (1, E), F32),
                  ("f1_w1", (E * D, H0), F32), ("f1_b1", (E, H0), F32),
                  ("f1_w2", (E * I1, D), F32), ("f1_b2", (E, D), F32)]
    dr = {}
    for name, shape, dt in specs:
        dr[name] = nc.dram_tensor(name, list(shape), dt, kind="ExternalInput").ap()

    with ExitStack() as stack:
        ctx = Ctx(nc, stack)
        ctx.init_items(dr["idn"])
        ctx.s.flush()
        if which == "all":
            x1 = nc.dram_tensor("x1", [ROWS, D], F32).ap()
            x2 = nc.dram_tensor("x2", [ROWS, D], F32).ap()
            x3 = nc.dram_tensor("x3", [ROWS, D], F32).ap()
            io = {"a0": (xin, x1), "f0": (x1, x2), "a1": (x2, x3), "f1": (x3, xout)}
        else:
            io = {which: (xin, xout)}
        if which in ("a0", "all"):
            stage_attn(ctx, io["a0"][0], io["a0"][1],
                       dict(wq=dr["a0_wq"], wk=dr["a0_wk"], wv=dr["a0_wv"], wo=dr["a0_wo"],
                            bq=dr["a0_bq"], bk=dr["a0_bk"], bv=dr["a0_bv"], bo=dr["a0_bo"]),
                       True, dr["ch0"], dr["sh0"], "a0")
        if which in ("f0", "all"):
            stage_ffn(ctx, io["f0"][0], io["f0"][1],
                      dict(gate_w=dr["f0_gate_w"], w1=dr["f0_w1"], b1=dr["f0_b1"],
                           ln1_g=dr["f0_ln1_g"], ln1_b=dr["f0_ln1_b"], w2=dr["f0_w2"],
                           b2=dr["f0_b2"], ln_g=dr["f0_ln_g"], ln_b=dr["f0_ln_b"]),
                      0, "f0")
        if which in ("a1", "all"):
            stage_attn(ctx, io["a1"][0], io["a1"][1],
                       dict(wq=dr["a1_wq"], wk=dr["a1_wk"], wv=dr["a1_wv"], wo=dr["a1_wo"]),
                       False, dr["ch1"], dr["sh1"], "a1")
        if which in ("f1", "all"):
            stage_ffn(ctx, io["f1"][0], io["f1"][1],
                      dict(gate_w=dr["f1_gate_w"], router_bias=dr["f1_rbias"],
                           w1=dr["f1_w1"], b1=dr["f1_b1"], w2=dr["f1_w2"], b2=dr["f1_b2"]),
                      1, "f1")
        ctx.s.add("vector", lambda e: e.engine_nop())
        ctx.s.flush()
    return nc


def _prep_host(inputs):
    ch0, sh0, ch1, sh1 = host_tables()
    c = lambda a: np.ascontiguousarray(np.asarray(a), dtype=np.float32) if not isinstance(a, np.ndarray) or a.dtype != np.float32 or not a.flags.c_contiguous else a
    def cc(a):
        return np.ascontiguousarray(np.asarray(a, dtype=np.float32))
    g = {
        "idn": np.eye(128, dtype=ml_dtypes.bfloat16),
        "ch0": ch0, "sh0": sh0, "ch1": ch1, "sh1": sh1,
        "a0_wq": cc(np.asarray(inputs["L0_attn_wq"]).T),
        "a0_wk": cc(np.asarray(inputs["L0_attn_wk"]).T),
        "a0_wv": cc(np.asarray(inputs["L0_attn_wv"]).T),
        "a0_wo": cc(np.asarray(inputs["L0_attn_wo"]).T),
        "a0_bq": cc(inputs["L0_attn_bq"]).reshape(1, D),
        "a0_bk": cc(inputs["L0_attn_bk"]).reshape(1, D),
        "a0_bv": cc(inputs["L0_attn_bv"]).reshape(1, D),
        "a0_bo": cc(inputs["L0_attn_bo"]).reshape(1, D),
        "f0_gate_w": cc(inputs["L0_ffn_gate_w"]),
        "f0_w1": cc(inputs["L0_ffn_w1"]).reshape(E * D, H0),
        "f0_b1": cc(inputs["L0_ffn_b1"]),
        "f0_ln1_g": cc(inputs["L0_ffn_ln1_g"]),
        "f0_ln1_b": cc(inputs["L0_ffn_ln1_b"]),
        "f0_w2": cc(inputs["L0_ffn_w2"]).reshape(E * H0, D),
        "f0_b2": cc(inputs["L0_ffn_b2"]),
        "f0_ln_g": cc(inputs["L0_ffn_ln_g"]).reshape(1, D),
        "f0_ln_b": cc(inputs["L0_ffn_ln_b"]).reshape(1, D),
        "a1_wq": cc(inputs["L1_attn_Wq"]),
        "a1_wk": cc(inputs["L1_attn_Wk"]),
        "a1_wv": cc(inputs["L1_attn_Wv"]),
        "a1_wo": cc(inputs["L1_attn_Wo"]),
        "f1_gate_w": cc(np.asarray(inputs["L1_ffn_router_weight"]).T),
        "f1_rbias": cc(inputs["L1_ffn_router_bias"]).reshape(1, E),
        "f1_w1": cc(inputs["L1_ffn_gate_up_proj"]).reshape(E * D, 2 * I1),
        "f1_b1": cc(inputs["L1_ffn_gate_up_bias"]),
        "f1_w2": cc(inputs["L1_ffn_down_proj"]).reshape(E * I1, D),
        "f1_b2": cc(inputs["L1_ffn_down_bias"]),
    }
    return g


_STAGE_KEYS = {
    "a0": ["idn", "a0_wq", "a0_wk", "a0_wv", "a0_wo", "a0_bq", "a0_bk", "a0_bv",
           "a0_bo", "ch0", "sh0"],
    "f0": ["idn", "f0_gate_w", "f0_w1", "f0_b1", "f0_ln1_g", "f0_ln1_b", "f0_w2",
           "f0_b2", "f0_ln_g", "f0_ln_b"],
    "a1": ["idn", "a1_wq", "a1_wk", "a1_wv", "a1_wo", "ch1", "sh1"],
    "f1": ["idn", "f1_gate_w", "f1_rbias", "f1_w1", "f1_b1", "f1_w2", "f1_b2"],
}


FUSED = True


def kernel(**inputs):
    g = _prep_host(inputs)
    x = np.ascontiguousarray(np.asarray(inputs["L0_attn_x"], dtype=np.float32)
                             ).reshape(BATCH * S, D)
    shards = [np.ascontiguousarray(x[c * ROWS:(c + 1) * ROWS]) for c in range(NCORES)]

    stages = ("all",) if FUSED else ("a0", "f0", "a1", "f1")
    for which in stages:
        nc = build_stage_program(which)
        keys = (set().union(*_STAGE_KEYS.values()) if which == "all"
                else set(_STAGE_KEYS[which]))
        in_maps = []
        for c in range(NCORES):
            m = {k: g[k] for k in keys}
            m["xin"] = shards[c]
            in_maps.append(m)
        res = run_bass_kernel_spmd(nc, in_maps, core_ids=list(range(NCORES)))
        shards = [np.ascontiguousarray(np.asarray(r["xout"], dtype=np.float32))
                  for r in res.results]

    out = np.concatenate(shards, axis=0).reshape(BATCH, S, D)
    return out.astype(np.float32)


# version 05 (final)
# speedup vs baseline: 1.2330x; 1.0205x over previous
import numpy as np
import ml_dtypes
from contextlib import ExitStack

import concourse.bass as bass
import concourse.mybir as mybir
from concourse.bass_utils import run_bass_kernel_spmd

F32 = mybir.dt.float32
BF16 = mybir.dt.bfloat16
AF = mybir.ActivationFunctionType
ALU = mybir.AluOpType
AX = mybir.AxisListType.X

NCORES = 8
BATCH = 32
NSEQ = BATCH // NCORES
S = 2048
D = 1024
NT = S // 128
ROWS = NSEQ * S
NH = 16
E = 8
H0 = 2048
I1 = 1024
EPS = 1e-5

CONSTS = [0.0, 1.0, -1.0, EPS, 0.125, 1.702, 7.0, -7.0]

import os
_SER1 = os.environ.get("KSER1") == "1"
_SER2 = os.environ.get("KSER2") == "1"
_SER3 = os.environ.get("KSER3") == "1"


class Sched:

    ENGINES = ["sync", "gpsimd", "tensor", "vector", "scalar"]
    SEMS = ENGINES + ["gdma"]

    def __init__(self, nc, stack):
        self.nc = nc
        self.stack = stack
        self.sems = {n: stack.enter_context(nc.semaphore(f"qsem_{n}"))
                     for n in self.SEMS}
        self.totals = {n: 0 for n in self.SEMS}
        self.prev = None
        self.pending = []
        self.n_items = 0

    def stream_sem(self, name):
        if name not in self.sems:
            self.sems[name] = self.stack.enter_context(self.nc.semaphore(f"qsem_{name}"))
            self.totals[name] = 0

    def add(self, eng, fn, dma_n=0, after="prev", stream=None):
        if stream is not None:
            assert dma_n
            self.stream_sem(stream)
            sem = stream
        else:
            sem = "gdma" if (eng == "gpsimd" and dma_n) else eng
        if after == "prev":
            deps = [self.prev] if self.prev is not None else []
        elif after is None:
            deps = []
        else:
            deps = [m for m in after if m is not None]
        self.pending.append((eng, sem, fn, dma_n, deps))
        inc = 16 * dma_n if dma_n else 1
        self.totals[sem] += inc
        mark = (sem, self.totals[sem])
        self.prev = mark
        self.n_items += 1
        return mark

    def barrier(self):
        marks = [(n, v) for n, v in self.totals.items() if v > 0]
        self.prev = None
        m = self.add("vector", lambda e: e.engine_nop(), after=marks)
        self.prev = m
        return m

    def flush(self):
        if not self.pending:
            return
        per_engine = {n: [] for n in self.ENGINES}
        for it in self.pending:
            per_engine[it[0]].append(it)
        self.pending = []
        sems = self.sems

        def make(name):
            def emitter(e):
                seen = {}
                for (_, sem, fn, dma_n, deps) in per_engine[name]:
                    for (dsem, dval) in deps:
                        if seen.get(dsem, -1) < dval:
                            e.wait_ge(sems[dsem], dval)
                            seen[dsem] = dval
                    ins = fn(e)
                    if dma_n:
                        assert isinstance(ins, (list, tuple)) and len(ins) == dma_n
                        for i in ins:
                            i.then_inc(sems[sem], 16)
                    else:
                        if isinstance(ins, (list, tuple)):
                            ins = ins[-1]
                        ins.then_inc(sems[sem], 1)
            return emitter

        with self.nc.Block() as block:
            for name in self.ENGINES:
                if per_engine[name]:
                    getattr(block, name)(make(name))


class Ctx:
    def __init__(self, nc, stack):
        self.nc = nc
        self.stack = stack
        self.s = Sched(nc, stack)
        sb = lambda n, shp, dt: stack.enter_context(nc.sbuf_tensor(n, shp, dt))
        ps = lambda n, shp, dt: stack.enter_context(nc.psum_tensor(n, shp, dt))
        self.cpool = sb("cpool", [128, len(CONSTS)], F32)
        self.idn = sb("idn_sb", [128, 128], BF16)
        self.onesb = sb("onesb", [1, 128], BF16)
        self.onesf = sb("onesf", [1, 128], F32)
        self.p2048 = ps("p2048", [128, 2048], F32)
        self.pa = ps("pa", [128, 1024], F32)
        self.pb1 = ps("pb1", [128, 512], F32)
        self.ptr = ps("ptr", [128, 512], BF16)
        for i, v in enumerate(CONSTS):
            nc.const_aps.register(F32, v, self.cpool[:, i:i + 1])

    def init_items(self, idn_dram):
        s = self.s
        for i, v in enumerate(CONSTS):
            s.add("vector", lambda e, i=i, v=v: e.memset(self.cpool[:, i:i + 1], v))
        s.add("vector", lambda e: e.memset(self.onesb[:, :], 1.0))
        s.add("vector", lambda e: e.memset(self.onesf[:, :], 1.0))
        s.add("sync", lambda e: [e.dma_start(out=self.idn[:, :], in_=idn_dram)], dma_n=1)

    def sb(self, n, shp, dt, stack=None):
        return (stack or self.stack).enter_context(self.nc.sbuf_tensor(n, shp, dt))


def emit_transposes(ctx, src_sb, src_col0, nchunk, dst_sb, dst_cols):
    s = ctx.s
    for g0 in range(0, nchunk, 4):
        gn = min(4, nchunk - g0)

        def tfn(e, g0=g0, gn=gn):
            out = None
            for j in range(gn):
                c = g0 + j
                out = e.transpose(ctx.ptr[:, j * 128:(j + 1) * 128],
                                  src_sb[:, src_col0 + c * 128: src_col0 + (c + 1) * 128],
                                  ctx.idn[:, :])
            return out
        s.add("tensor", tfn)

        def cfn(e, g0=g0, gn=gn):
            ins = None
            for j in range(gn):
                c = g0 + j
                ins = e.tensor_copy(out=dst_sb[:, dst_cols[c]:dst_cols[c] + 128],
                                    in_=ctx.ptr[:, j * 128:(j + 1) * 128])
            return ins
        s.add("vector", cfn)


def emit_matmul_group(ctx, psum_ap_fn, lhsT_fn, rhs_fn, nks, bias_rhs_fn=None,
                      bias_lhsT=None):
    def fn(e):
        ins = None
        for k in range(nks):
            ins = e.matmul(psum_ap_fn(), lhsT_fn(k), rhs_fn(k),
                           start=(k == 0), stop=(k == nks - 1 and bias_rhs_fn is None))
        if bias_rhs_fn is not None:
            ins = e.matmul(psum_ap_fn(), bias_lhsT, bias_rhs_fn(),
                           start=False, stop=True)
        return ins
    ctx.s.add("tensor", fn)


def emit_ln_stats(ctx, src_ap, sq_scratch, mu, var, rstd, nmr, inv_w):
    s = ctx.s
    s.add("vector", lambda e: e.reduce_sum(mu[:, :], src_ap, axis=AX))
    s.add("scalar", lambda e: e.activation(sq_scratch, src_ap, AF.Square,
                                           accum_out=var[:, :]))
    s.add("vector", lambda e: e.tensor_scalar_mul(mu[:, :], mu[:, :], inv_w))
    s.add("vector", lambda e: e.tensor_scalar_mul(var[:, :], var[:, :], inv_w))
    s.add("vector", lambda e: e.tensor_mul(nmr[:, :], mu[:, :], mu[:, :]))
    s.add("vector", lambda e: e.tensor_sub(var[:, :], var[:, :], nmr[:, :]))
    s.add("scalar", lambda e: e.activation(var[:, :], var[:, :], AF.Sqrt, bias=EPS))
    s.add("vector", lambda e: e.reciprocal(rstd[:, :], var[:, :]))
    s.add("vector", lambda e: e.tensor_mul(nmr[:, :], mu[:, :], rstd[:, :]))
    s.add("vector", lambda e: e.tensor_scalar_mul(nmr[:, :], nmr[:, :], -1.0))


def emit_row_broadcast(ctx, row_ap_fn, dst_sb, width):
    s = ctx.s
    for nb in range(0, width, 512):
        w = min(512, width - nb)
        s.add("tensor", lambda e, nb=nb, w=w: e.matmul(
            ctx.pb1[:, 0:w], ctx.onesf[:, :], row_ap_fn(nb, w), start=True, stop=True))
        s.add("scalar", lambda e, nb=nb, w=w: e.activation(
            dst_sb[:, nb:nb + w], ctx.pb1[:, 0:w], AF.Identity))



def emit_ln_bn(ctx, src_slice_fn, W, B, deps):
    s = ctx.s
    nch = W // 512

    def bnfn(e):
        ins = None
        for j in range(nch):
            ins = e.bn_stats(B["bnst"][:, j * 6:(j + 1) * 6], src_slice_fn(j * 512, 512))
        return ins
    m1 = s.add("vector", bnfn, after=deps)
    m2 = s.add("vector", lambda e: e.bn_aggr(B["bnmv"][:, :], B["bnst"][:, 0:6 * nch]), after=[m1])
    m3 = s.add("scalar", lambda e: e.activation(B["sd"][:, :], B["bnmv"][:, 1:2], AF.Sqrt, bias=EPS), after=[m2])
    m4 = s.add("vector", lambda e: e.reciprocal(B["rstd"][:, :], B["sd"][:, :]), after=[m3])
    m5 = s.add("vector", lambda e: e.tensor_scalar(
        out=B["nmr"][:, :], in0=B["bnmv"][:, 0:1], scalar1=B["rstd"][:, :], scalar2=-1.0,
        op0=ALU.mult, op1=ALU.mult), after=[m4])
    return m5


def emit_transposes2(ctx, src_sb, src_col0, nchunk, dst_sb, dst_cols, start_deps,
                     cfn_extra_deps, prev_cfn_last):
    s = ctx.s
    tmark = None
    cmark = prev_cfn_last
    for g0 in range(0, nchunk, 4):
        gn = min(4, nchunk - g0)

        def tfn(e, g0=g0, gn=gn):
            out = None
            for j in range(gn):
                c = g0 + j
                out = e.transpose(ctx.ptr[:, j * 128:(j + 1) * 128],
                                  src_sb[:, src_col0 + c * 128: src_col0 + (c + 1) * 128],
                                  ctx.idn[:, :])
            return out
        deps = list(start_deps)
        if cmark is not None:
            deps.append(cmark)
        tmark = s.add("tensor", tfn, after=deps)

        def cfn(e, g0=g0, gn=gn):
            ins = None
            for j in range(gn):
                c = g0 + j
                ins = e.tensor_copy(out=dst_sb[:, dst_cols[c]:dst_cols[c] + 128],
                                    in_=ctx.ptr[:, j * 128:(j + 1) * 128])
            return ins
        cmark = s.add("vector", cfn, after=[tmark] + list(cfn_extra_deps))
    return tmark, cmark


def stage_attn(ctx, xin, xout, wdr, has_bias, ch_dram, sh_dram, tag):
    nc, s = ctx.nc, ctx.s
    with ExitStack() as st:
        sb = lambda n, shp, dt: ctx.sb(f"{tag}s_{n}", shp, dt, st)
        wq_sb = sb("wq", [128, 8 * 1024], BF16)
        wk_sb = sb("wk", [128, 8 * 1024], BF16)
        wv_sb = sb("wv", [128, 8 * 1024], BF16)
        QT = sb("QT", [128, 8 * 2048], BF16)
        KT = sb("KT", [128, 8 * 2048], BF16)
        Vg = sb("Vg", [128, 16 * (16 * 65)], BF16)
        Pb = sb("Pb", [128, 2048], BF16)
        ct = sb("ct", [128, 512], F32)
        sth = sb("sth", [128, 512], F32)
        xs = sb("xs", [128, 1024], F32)
        lnb = sb("lnb", [128, 1024], BF16)
        lnT = sb("lnT", [128, 1024], BF16)
        qbf = sb("qbf", [128, 1024], BF16)
        scr = sb("scr", [128, 1024], F32)
        outs = sb("outs", [128, 1024], F32)
        bnst = sb("bnst", [128, 12], F32)
        bnmv = sb("bnmv", [128, 2], F32)
        sd = sb("sd", [128, 1], F32)
        rstd = sb("rstd", [128, 1], F32)
        nmr = sb("nmr", [128, 1], F32)
        invs = sb("invs", [1, 512], F32)
        bcs = sb("bcs", [128, 512], F32)
        bob = sb("bob", [128, 1024], F32)
        brow = sb("brow", [1, 3 * 1024], BF16)
        borow = sb("borow", [1, 1024], F32)
        BNB = dict(bnst=bnst, bnmv=bnmv, sd=sd, rstd=rstd, nmr=nmr)

        def wload(dst, src):
            s.add("gpsimd", lambda e, dst=dst, src=src: [e.dma_start(
                out=dst.rearrange("p (c j) -> p c j", j=1024),
                in_=src.rearrange("(c p) j -> p c j", p=128))], dma_n=1)
        wload(wq_sb.ap(), wdr["wq"])
        wload(wk_sb.ap(), wdr["wk"])
        wload(wv_sb.ap(), wdr["wv"])
        if has_bias:
            s.add("gpsimd", lambda e: [
                e.dma_start(out=brow[:, 0:1024], in_=wdr["bq"]),
                e.dma_start(out=brow[:, 1024:2048], in_=wdr["bk"]),
                e.dma_start(out=brow[:, 2048:3072], in_=wdr["bv"]),
            ], dma_n=3)
            s.add("sync", lambda e: [e.dma_start(out=borow[:, :], in_=wdr["bo"])], dma_n=1)
            emit_row_broadcast(ctx, lambda nb, w: borow[0:1, nb:nb + w], bob, 1024)
        else:
            s.add("vector", lambda e: e.memset(bob[:, :], 0.0))
        s.barrier()

        Vg4 = Vg.ap().rearrange("p (kc h j) -> p kc h j", h=16, j=65)

        for sq in range(NSEQ):
            r0s = sq * S
            mk = {"ts": None, "cfnln": None, "tfnln": None,
                  "qmm": None, "kmm": None, "vmm": None,
                  "ropeQ4": None, "ropeK4": None, "gpQ2": None, "gpK2": None,
                  "tfnQ": None, "cfnQ": None, "tfnK": None, "cfnK": None,
                  "vg": None, "dmax": None, "dmat": None}
            mk["dmax"] = s.add("sync", lambda e, r0s=r0s: [e.dma_start(out=xs[:, :], in_=xin[r0s:r0s + 128, :])], dma_n=1, stream="d_xs")
            mk["dmat"] = s.add("sync", lambda e: [
                e.dma_start(out=ct[:, :], in_=ch_dram[0:128, :]),
                e.dma_start(out=sth[:, :], in_=sh_dram[0:128, :])], dma_n=2, stream="d_tab")
            for t in range(NT):
                r0 = r0s + t * 128
                mln = emit_ln_bn(ctx, lambda lo, w: xs[:, lo:lo + w], 1024, BNB,
                                 [mk["dmax"]])
                m_ts = s.add("vector", lambda e: e.tensor_scalar(
                    out=lnb[:, :], in0=xs[:, :], scalar1=bnmv[:, 0:1], scalar2=rstd[:, :],
                    op0=ALU.subtract, op1=ALU.mult),
                    after=[mln] + ([mk["tfnln"]] if mk["tfnln"] else []))
                if t + 1 < NT:
                    mk["dmax"] = s.add("sync", lambda e, r1=r0 + 128: [
                        e.dma_start(out=xs[:, :], in_=xin[r1:r1 + 128, :])], dma_n=1, after=[m_ts], stream="d_xs")
                tfnln, cfnln = emit_transposes2(
                    ctx, lnb, 0, 8, lnT, [c * 128 for c in range(8)],
                    [m_ts], [mk["vmm"]] if mk["vmm"] else [], mk["cfnK"])
                mk["tfnln"], mk["cfnln"] = tfnln, cfnln

                def qkfn2(e, dst0, w_sb, boff):
                    ins = None
                    for nb in range(2):
                        nk = 8
                        for c in range(nk):
                            last = (c == nk - 1) and not has_bias
                            ins = e.matmul(
                                ctx.p2048[:, dst0 + nb * 512: dst0 + (nb + 1) * 512],
                                lnT[:, c * 128:(c + 1) * 128],
                                w_sb[:, c * 1024 + nb * 512: c * 1024 + (nb + 1) * 512],
                                start=(c == 0), stop=last)
                        if has_bias:
                            ins = e.matmul(ctx.p2048[:, dst0 + nb * 512: dst0 + (nb + 1) * 512],
                                           ctx.onesb[:, :],
                                           brow[0:1, boff + nb * 512: boff + (nb + 1) * 512],
                                           start=False, stop=True)
                    return ins
                mk["qmm"] = s.add("tensor", lambda e: qkfn2(e, 0, wq_sb, 0),
                                  after=[cfnln] + ([mk["ropeQ4"]] if mk["ropeQ4"] else []))
                mk["kmm"] = s.add("tensor", lambda e: qkfn2(e, 1024, wk_sb, 1024),
                                  after=[cfnln] + ([mk["ropeK4"]] if mk["ropeK4"] else []))

                def vfn(e):
                    ins = None
                    for nb in range(2):
                        for c in range(8):
                            last = (c == 7) and not has_bias
                            ins = e.matmul(ctx.pa[:, nb * 512:(nb + 1) * 512],
                                           lnT[:, c * 128:(c + 1) * 128],
                                           wv_sb[:, c * 1024 + nb * 512: c * 1024 + (nb + 1) * 512],
                                           start=(c == 0), stop=last)
                        if has_bias:
                            ins = e.matmul(ctx.pa[:, nb * 512:(nb + 1) * 512],
                                           ctx.onesb[:, :],
                                           brow[0:1, 2048 + nb * 512: 2048 + (nb + 1) * 512],
                                           start=False, stop=True)
                    return ins
                mk["vmm"] = s.add("tensor", vfn,
                                  after=[cfnln] + ([mk["vg"]] if mk["vg"] else []))

                for half in (0, 1):
                    pe = ctx.p2048.ap().rearrange("p (c j two) -> p c j two", c=2, two=2)[:, half, :, 0]
                    po = ctx.p2048.ap().rearrange("p (c j two) -> p c j two", c=2, two=2)[:, half, :, 1]
                    de = qbf.ap().rearrange("p (j two) -> p j two", two=2)[:, :, 0]
                    do = qbf.ap().rearrange("p (j two) -> p j two", two=2)[:, :, 1]
                    mmm = mk["qmm"] if half == 0 else mk["kmm"]
                    scrdep = mk["gpK2"] if half == 0 else mk["gpQ2"]
                    qbfdep = (mk["tfnK"] if half == 0 else mk["tfnQ"])
                    def mab(e, pe=pe, po=po):
                        e.tensor_mul(scr[:, 0:512], pe, ct[:, :])
                        return e.tensor_mul(scr[:, 512:1024], po, sth[:, :])
                    mu2 = s.add("vector", mab,
                                after=[mmm, mk["dmat"]] + ([scrdep] if scrdep else []))
                    g1 = s.add("gpsimd", lambda e, de=de: e.tensor_tensor(
                        out=de, in0=scr[:, 0:512], in1=scr[:, 512:1024], op=ALU.subtract),
                        after=[mu2] + ([qbfdep] if qbfdep else []))

                    def mcd(e, pe=pe, po=po):
                        e.tensor_mul(scr[:, 0:512], po, ct[:, :])
                        return e.tensor_mul(scr[:, 512:1024], pe, sth[:, :])
                    mu4 = s.add("vector", mcd, after=[g1])
                    g2 = s.add("gpsimd", lambda e, do=do: e.tensor_tensor(
                        out=do, in0=scr[:, 0:512], in1=scr[:, 512:1024], op=ALU.add),
                        after=[mu4])
                    if half == 0:
                        mk["ropeQ4"], mk["gpQ2"] = mu4, g2
                        tq, cq = emit_transposes2(
                            ctx, qbf, 0, 8, QT, [c * 2048 + t * 128 for c in range(8)],
                            [g1, g2], [], mk["cfnln"])
                        mk["tfnQ"], mk["cfnQ"] = tq, cq
                    else:
                        mk["ropeK4"], mk["gpK2"] = mu4, g2
                        tk, ck = emit_transposes2(
                            ctx, qbf, 0, 8, KT, [c * 2048 + t * 128 for c in range(8)],
                            [g1, g2], [], mk["cfnQ"])
                        mk["tfnK"], mk["cfnK"] = tk, ck
                if t + 1 < NT:
                    mk["dmat"] = s.add("sync", lambda e, t1=t + 1: [
                        e.dma_start(out=ct[:, :], in_=ch_dram[t1 * 128:(t1 + 1) * 128, :]),
                        e.dma_start(out=sth[:, :], in_=sh_dram[t1 * 128:(t1 + 1) * 128, :])],
                        dma_n=2, after=[mk["ropeK4"]], stream="d_tab")
                mo = s.add("vector", lambda e, t=t: e.memset(Vg4[:, t, :, 64:65], 1.0),
                           after=[mk["vmm"]])
                mk["vg"] = s.add("vector", lambda e, t=t: e.tensor_copy(
                    out=Vg4[:, t, :, 0:64],
                    in_=ctx.pa.ap().rearrange("p (h j) -> p h j", j=64)), after=[mo])
                if _SER1:
                    bm = s.barrier()
                    for k in mk:
                        if mk[k] is not None:
                            mk[k] = bm
            m_bar = s.barrier()

            m_norm = None
            m_actc = None
            m_exp = [None, None]
            m_pv = [None, None]
            for qb in range(4):
                for h in range(NH):
                    c = h // 2
                    r = 64 * (h % 2)
                    m_pv_last = None
                    for g in range(4):
                        def sfn(e, g=g, c=c, r=r, qb=qb):
                            ins = None
                            for j in range(4):
                                kc = g * 4 + j
                                ins = e.matmul(
                                    ctx.p2048[:, j * 512:(j + 1) * 512],
                                    KT[r:r + 64, c * 2048 + kc * 128: c * 2048 + (kc + 1) * 128],
                                    QT[r:r + 64, c * 2048 + qb * 512: c * 2048 + (qb + 1) * 512],
                                    start=True, stop=True)
                            return ins
                        m_s = s.add("tensor", sfn, after=[m_exp[0] if m_exp[0] else m_bar])
                        m_exp[0] = s.add("scalar", lambda e: e.activation(
                            Pb[:, :], ctx.p2048[:, :], AF.Exp, scale=0.125),
                            after=[m_s] + ([m_pv[0]] if m_pv[0] else []))

                        def pvfn(e, g=g, h=h):
                            ins = None
                            for j in range(4):
                                kc = g * 4 + j
                                ins = e.matmul(
                                    ctx.pb1[0:65, :],
                                    Vg[:, kc * 1040 + h * 65: kc * 1040 + (h + 1) * 65],
                                    Pb[:, j * 512:(j + 1) * 512],
                                    start=(kc == 0), stop=(kc == 15))
                            return ins
                        pvdeps = [m_exp[0]]
                        if g == 0:
                            pvdeps.append(m_norm if m_norm else m_bar)
                        m_pv[0] = s.add("tensor", pvfn, after=pvdeps)
                        m_pv_last = m_pv[0]
                    m_rc = s.add("vector", lambda e: e.reciprocal(invs[:, :], ctx.pb1[64:65, :]),
                                 after=[m_pv_last])
                    m_bc = s.add("tensor", lambda e: e.matmul(
                        ctx.pa[:, 0:512], ctx.onesf[:, :], invs[:, :], start=True, stop=True),
                        after=[m_rc] + ([m_actc] if m_actc else []))
                    m_actc = s.add("scalar", lambda e: e.activation(
                        bcs[:, :], ctx.pa[:, 0:512], AF.Identity),
                        after=[m_bc] + ([m_norm] if m_norm else []))
                    m_norm = s.add("vector", lambda e, c=c, r=r, qb=qb: e.tensor_tensor(
                        out=QT[r:r + 64, c * 2048 + qb * 512: c * 2048 + (qb + 1) * 512],
                        in0=ctx.pb1[0:64, :], in1=bcs[0:64, :], op=ALU.mult),
                        after=[m_actc])
                    if _SER2:
                        bm = s.barrier()
                        m_norm = m_actc = bm
                        m_exp = [bm, bm]
                        m_pv = [bm, bm]
            s.barrier()

            s.add("gpsimd", lambda e: [e.dma_start(
                out=Vg.ap()[:, 0:8192].rearrange("p (c j) -> p c j", j=1024),
                in_=wdr["wo"].rearrange("(c p) j -> p c j", p=128))], dma_n=1, stream="d_wo")
            m_wo = s.barrier()
            m_stt = None
            m_add = None
            m_out = None
            m_x = s.add("sync", lambda e, r0s=r0s: [e.dma_start(out=xs[:, :], in_=xin[r0s:r0s + 128, :])],
                        dma_n=1, after=[m_wo], stream="d_xs")
            for t in range(NT):
                r0 = r0s + t * 128

                def wofn(e, t=t):
                    ins = None
                    for nb in range(2):
                        for c in range(8):
                            ins = e.matmul(
                                ctx.pa[:, nb * 512:(nb + 1) * 512],
                                QT[:, c * 2048 + t * 128: c * 2048 + (t + 1) * 128],
                                Vg[:, c * 1024 + nb * 512: c * 1024 + (nb + 1) * 512],
                                start=(c == 0), stop=(c == 7))
                    return ins
                m_mm = s.add("tensor", wofn, after=[m_stt] if m_stt else [m_wo])
                m_stt = s.add("vector", lambda e: e.scalar_tensor_tensor(
                    out=scr[:, :], in0=ctx.pa[:, :], scalar=1.0, in1=bob[:, :],
                    op0=ALU.mult, op1=ALU.add),
                    after=[m_mm] + ([m_add] if m_add else []))
                m_add = s.add("vector", lambda e: e.tensor_add(outs[:, :], scr[:, :], xs[:, :]),
                              after=[m_stt, m_x] + ([m_out] if m_out else []))
                if t + 1 < NT:
                    m_x = s.add("sync", lambda e, r1=r0 + 128: [
                        e.dma_start(out=xs[:, :], in_=xin[r1:r1 + 128, :])], dma_n=1, after=[m_add], stream="d_xs")
                m_out = s.add("gpsimd", lambda e, r0=r0: [
                    e.dma_start(out=xout[r0:r0 + 128, :], in_=outs[:, :])], dma_n=1,
                    after=[m_add], stream="d_out")
                if _SER3:
                    bm = s.barrier()
                    m_stt = m_add = m_out = m_x = bm
            s.barrier()
        s.flush()



def stage_ffn(ctx, xin, xout, dr, kind, tag):
    nc, s = ctx.nc, ctx.s
    TB = 8
    NB = (NSEQ * NT) // TB
    NC2 = 16 if kind == 0 else 8
    with ExitStack() as st:
        sb = lambda n, shp, dt: ctx.sb(f"{tag}s_{n}", shp, dt, st)
        w1_sb = sb("w1", [128, 8 * H0], BF16)
        w2_sb = sb("w2", [128, NC2 * 1024], BF16)
        uT = sb("uT", [128, 8 * (TB * 128)], BF16)
        acc = sb("acc", [128, TB * 1024], F32)
        cw = sb("cw", [128, TB * 8], F32)
        gwrow = sb("gwrow", [128, 64], BF16)
        grbias = sb("grbias", [1, 8], BF16)
        b1e = sb("b1e", [1, H0], BF16)
        b2e = sb("b2e", [1, 1024], BF16)
        grow_a = sb("grow_a", [1, H0], F32)
        grow_b = sb("grow_b", [1, H0], F32)
        gb = sb("gb", [128, H0], F32)
        bb = sb("bb", [128, H0], F32)
        lngb = sb("lngb", [128, 1024], F32) if kind == 0 else None
        lnbb = sb("lnbb", [128, 1024], F32) if kind == 0 else None
        xs = sb("xs", [128, 1024], F32)
        ub = sb("ub", [128, 1024], BF16)
        ar1 = sb("ar1", [128, 2048], F32)
        ar2 = sb("ar2", [128, 2048], F32)
        hb = sb("hb", [128, 2048], BF16)
        hT = sb("hT", [128, 2048], BF16)
        bnst = sb("bnst", [128, 24], F32)
        bnmv = sb("bnmv", [128, 2], F32)
        sd = sb("sd", [128, 1], F32)
        rstd = sb("rstd", [128, 1], F32)
        nmr = sb("nmr", [128, 1], F32)
        m1 = sb("m1", [128, 1], F32)
        m2 = sb("m2", [128, 1], F32)
        gsum = sb("gsum", [128, 1], F32)
        gates = sb("gates", [128, 8], F32)
        msk = sb("msk", [128, 8], F32)
        gm = sb("gm", [128, 8], F32)
        BNB = dict(bnst=bnst, bnmv=bnmv, sd=sd, rstd=rstd, nmr=nmr)

        wbf1 = nc.dram_tensor(f"{tag}_wbf1", [E * 1024, H0], BF16).ap()
        w2rows = 2048 if kind == 0 else 1024
        wbf2 = nc.dram_tensor(f"{tag}_wbf2", [E * w2rows, 1024], BF16).ap()

        def preconv(src, dst, rows, cols):
            for r in range(0, rows, 128):
                s.add("sync", lambda e, r=r: [e.dma_start(out=ar1[:, 0:cols], in_=src[r:r + 128, :])], dma_n=1)
                s.add("vector", lambda e: e.tensor_copy(out=hb[:, 0:cols], in_=ar1[:, 0:cols]))
                s.add("gpsimd", lambda e, r=r: [e.dma_start(out=dst[r:r + 128, :], in_=hb[:, 0:cols])], dma_n=1)
        preconv(dr["w1"], wbf1, E * 1024, H0)
        preconv(dr["w2"], wbf2, E * w2rows, 1024)

        s.add("gpsimd", lambda e: [e.dma_start(
            out=gwrow.ap().rearrange("p (c j) -> p c j", j=8),
            in_=dr["gate_w"].rearrange("(c p) j -> p c j", p=128))], dma_n=1)
        if kind == 0:
            s.add("sync", lambda e: [
                e.dma_start(out=grow_a[:, 0:1024], in_=dr["ln_g"]),
                e.dma_start(out=grow_b[:, 0:1024], in_=dr["ln_b"])], dma_n=2)
            emit_row_broadcast(ctx, lambda nb, w: grow_a[0:1, nb:nb + w], lngb, 1024)
            emit_row_broadcast(ctx, lambda nb, w: grow_b[0:1, nb:nb + w], lnbb, 1024)
        else:
            s.add("gpsimd", lambda e: [e.dma_start(out=grbias[:, :], in_=dr["router_bias"])], dma_n=1)
        s.barrier()

        for b in range(NB):
            t0 = b * TB
            for t in range(TB):
                r0 = (t0 + t) * 128
                s.add("sync", lambda e, r0=r0: [e.dma_start(out=xs[:, :], in_=xin[r0:r0 + 128, :])], dma_n=1)
                emit_ln_bn(ctx, lambda lo, w: xs[:, lo:lo + w], 1024, BNB, "prev")
                s.add("vector", lambda e: e.tensor_scalar(
                    out=ub[:, :], in0=xs[:, :], scalar1=bnmv[:, 0:1], scalar2=rstd[:, :],
                    op0=ALU.subtract, op1=ALU.mult))
                emit_transposes(ctx, ub, 0, 8, uT, [c * TB * 128 + t * 128 for c in range(8)])
                s.add("vector", lambda e, t=t: e.memset(acc[:, t * 1024:(t + 1) * 1024], 0.0))

                def glog(e, t=t):
                    ins = None
                    for c in range(8):
                        ins = e.matmul(ctx.pb1[:, 0:8],
                                       uT[:, c * TB * 128 + t * 128: c * TB * 128 + (t + 1) * 128],
                                       gwrow[:, c * 8:(c + 1) * 8],
                                       start=(c == 0), stop=(c == 7 and kind == 0))
                    if kind == 1:
                        ins = e.matmul(ctx.pb1[:, 0:8], ctx.onesb[:, :], grbias[:, :],
                                       start=False, stop=True)
                    return ins
                s.add("tensor", glog)
                s.add("vector", lambda e: e.reduce_max(m1[:, :], ctx.pb1[:, 0:8], axis=AX))
                s.add("vector", lambda e: e.tensor_scalar_mul(m2[:, :], m1[:, :], -1.0))
                s.add("scalar", lambda e: e.activation(gates[:, :], ctx.pb1[:, 0:8], AF.Exp,
                                                       bias=m2[:, :], accum_out=gsum[:, :]))
                s.add("vector", lambda e: e.reciprocal(gsum[:, :], gsum[:, :]))
                s.add("vector", lambda e: e.tensor_scalar_mul(gates[:, :], gates[:, :], gsum[:, :]))
                s.add("vector", lambda e: e.reduce_max(m1[:, :], gates[:, :], axis=AX))
                s.add("vector", lambda e: e.tensor_scalar(
                    out=msk[:, :], in0=gates[:, :], scalar1=m1[:, :], scalar2=None, op0=ALU.is_ge))
                s.add("vector", lambda e: e.scalar_tensor_tensor(
                    out=gm[:, :], in0=msk[:, :], scalar=-1e30, in1=gates[:, :],
                    op0=ALU.mult, op1=ALU.add))
                s.add("vector", lambda e: e.reduce_max(m2[:, :], gm[:, :], axis=AX))
                s.add("vector", lambda e: e.tensor_scalar(
                    out=msk[:, :], in0=gates[:, :], scalar1=m2[:, :], scalar2=None, op0=ALU.is_ge))
                s.add("vector", lambda e, t=t: e.tensor_mul(cw[:, t * 8:(t + 1) * 8], gates[:, :], msk[:, :]))
                if kind == 0:
                    s.add("vector", lambda e: e.tensor_add(m1[:, :], m1[:, :], m2[:, :]))
                    s.add("vector", lambda e: e.reciprocal(m1[:, :], m1[:, :]))
                    s.add("vector", lambda e, t=t: e.tensor_scalar_mul(
                        cw[:, t * 8:(t + 1) * 8], cw[:, t * 8:(t + 1) * 8], m1[:, :]))
            m_ph1 = s.barrier()

            m_wld = s.add("sync", lambda e: [
                e.dma_start(out=w1_sb.ap().rearrange("p (c j) -> p c j", j=H0),
                            in_=wbf1.rearrange("(ee c p) j -> ee p c j", ee=E, p=128)[0]),
                e.dma_start(out=w2_sb.ap().rearrange("p (c j) -> p c j", j=1024),
                            in_=wbf2.rearrange("(ee c p) j -> ee p c j", ee=E, p=128)[0]),
            ], dma_n=2, after=[m_ph1], stream="d_wld")
            m_brow = s.add("gpsimd", lambda e: [
                e.dma_start(out=b1e[:, :], in_=dr["b1"][0:1, :]),
                e.dma_start(out=b2e[:, :], in_=dr["b2"][0:1, :]),
            ], dma_n=2, after=[m_ph1], stream="d_brow")
            if kind == 0:
                m_grow = s.add("sync", lambda e: [
                    e.dma_start(out=grow_a[:, :], in_=dr["ln1_g"][0:1, :]),
                    e.dma_start(out=grow_b[:, :], in_=dr["ln1_b"][0:1, :])], dma_n=2,
                    after=[m_ph1], stream="d_grow")
            m_gbb = None
            mp = {"w1": None, "w2": None, "apply": None, "mulgb": None, "addbb": None,
                  "gelu": None, "tfn": None, "cfn": None, "acc": None, "ev": None,
                  "tsmin": None, "tsclip": None, "sig": None, "mulglu": None, "stt": None,
                  "ex": 0, "b2dep": None}
            for ex in range(E):
                if kind == 0:
                    gdeps = [m_grow] + [m for m in (mp["mulgb"], mp["addbb"], m_gbb) if m]
                    last = None
                    for nb in range(0, H0, 512):
                        mmm = s.add("tensor", lambda e, nb=nb: e.matmul(
                            ctx.pb1[:, 0:512], ctx.onesf[:, :], grow_a[0:1, nb:nb + 512],
                            start=True, stop=True), after=gdeps if last is None else [last])
                        last = s.add("scalar", lambda e, nb=nb: e.activation(
                            gb[:, nb:nb + 512], ctx.pb1[:, 0:512], AF.Identity), after=[mmm])
                    for nb in range(0, H0, 512):
                        mmm = s.add("tensor", lambda e, nb=nb: e.matmul(
                            ctx.pb1[:, 0:512], ctx.onesf[:, :], grow_b[0:1, nb:nb + 512],
                            start=True, stop=True), after=[last])
                        last = s.add("scalar", lambda e, nb=nb: e.activation(
                            bb[:, nb:nb + 512], ctx.pb1[:, 0:512], AF.Identity), after=[mmm])
                    m_gbb = last
                def emit_tr_w2_acc(tp):
                    nch = 16 if kind == 0 else 8
                    tfn, cfn = emit_transposes2(
                        ctx, hb, 0, nch, hT, [c * 128 for c in range(nch)],
                        [mp["gelu"]], [mp["w2"]] if mp["w2"] else [], mp["cfn"])
                    mp["tfn"], mp["cfn"] = tfn, cfn

                    def w2fn(e):
                        ins = None
                        for c in range(NC2):
                            for nb in range(2):
                                ins = e.matmul(
                                    ctx.pa[:, nb * 512:(nb + 1) * 512],
                                    hT[:, c * 128:(c + 1) * 128],
                                    w2_sb[:, c * 1024 + nb * 512: c * 1024 + (nb + 1) * 512],
                                    start=(c == 0), stop=False)
                        for nb in range(2):
                            ins = e.matmul(ctx.pa[:, nb * 512:(nb + 1) * 512],
                                           ctx.onesb[:, :], b2e[0:1, nb * 512:(nb + 1) * 512],
                                           start=False, stop=True)
                        return ins
                    mp["w2"] = s.add("tensor", w2fn,
                                     after=[mp["cfn"], mp["b2dep"]] + ([mp["acc"]] if mp["acc"] else []))
                    mp["acc"] = s.add("vector", lambda e, tp=tp, exp=mp["ex"]: e.scalar_tensor_tensor(
                        out=acc[:, tp * 1024:(tp + 1) * 1024], in0=ctx.pa[:, :],
                        scalar=cw[:, tp * 8 + exp: tp * 8 + exp + 1],
                        in1=acc[:, tp * 1024:(tp + 1) * 1024], op0=ALU.mult, op1=ALU.add),
                        after=[mp["w2"]])

                mp["ex"] = ex
                mp["b2dep"] = m_brow
                for t in range(TB):
                    def w1fn(e, t=t):
                        ins = None
                        for c in range(8):
                            for nb in range(4):
                                ins = e.matmul(
                                    ctx.p2048[:, nb * 512:(nb + 1) * 512],
                                    uT[:, c * TB * 128 + t * 128: c * TB * 128 + (t + 1) * 128],
                                    w1_sb[:, c * H0 + nb * 512: c * H0 + (nb + 1) * 512],
                                    start=(c == 0), stop=False)
                        for nb in range(4):
                            ins = e.matmul(ctx.p2048[:, nb * 512:(nb + 1) * 512],
                                           ctx.onesb[:, :], b1e[0:1, nb * 512:(nb + 1) * 512],
                                           start=False, stop=True)
                        return ins
                    w1deps = [m_wld, m_brow]
                    if kind == 0 and mp["ev"]:
                        w1deps.append(mp["ev"])
                    if kind == 1:
                        w1deps += [m for m in (mp["tsmin"], mp["tsclip"]) if m]
                    mp["w1"] = s.add("tensor", w1fn, after=w1deps)
                    if t > 0:
                        emit_tr_w2_acc(t - 1)
                    if kind == 0:
                        mp["ev"] = s.add("scalar", lambda e: e.activation(
                            ar1[:, :], ctx.p2048[:, :], AF.Identity),
                            after=[mp["w1"]] + ([mp["addbb"]] if mp["addbb"] else []))
                        mln = emit_ln_bn(ctx, lambda lo, w: ar1[:, lo:lo + w], 2048,
                                         BNB, [mp["ev"]])
                        mp["apply"] = s.add("vector", lambda e: e.tensor_scalar(
                            out=ar2[:, :], in0=ar1[:, :], scalar1=bnmv[:, 0:1],
                            scalar2=rstd[:, :], op0=ALU.subtract, op1=ALU.mult),
                            after=[mln] + ([mp["gelu"]] if mp["gelu"] else []))
                        mp["mulgb"] = s.add("vector", lambda e: e.tensor_mul(
                            ar1[:, :], ar2[:, :], gb[:, :]),
                            after=[mp["apply"], m_gbb])
                        mp["addbb"] = s.add("gpsimd", lambda e: e.tensor_add(
                            ar2[:, :], ar1[:, :], bb[:, :]), after=[mp["mulgb"]])
                        mp["gelu"] = s.add("scalar", lambda e: e.activation(
                            hb[:, :], ar2[:, :], AF.Gelu),
                            after=[mp["addbb"]] + ([mp["tfn"]] if mp["tfn"] else []))
                    else:
                        mp["tsmin"] = s.add("vector", lambda e: e.tensor_scalar(
                            out=ar1[:, 0:1024], in0=ctx.p2048[:, 0:1024], scalar1=7.0,
                            scalar2=None, op0=ALU.min),
                            after=[mp["w1"]] + ([mp["sig"]] if mp["sig"] else []))
                        mp["tsclip"] = s.add("vector", lambda e: e.tensor_scalar(
                            out=ar1[:, 1024:2048], in0=ctx.p2048[:, 1024:2048], scalar1=-7.0,
                            scalar2=7.0, op0=ALU.max, op1=ALU.min), after=[mp["tsmin"]])
                        mp["sig"] = s.add("scalar", lambda e: e.activation(
                            ar2[:, 0:1024], ar1[:, 0:1024], AF.Sigmoid, scale=1.702),
                            after=[mp["tsmin"]] + ([mp["mulglu"]] if mp["mulglu"] else []))
                        mp["mulglu"] = s.add("vector", lambda e: e.tensor_mul(
                            ar2[:, 1024:2048], ar1[:, 0:1024], ar2[:, 0:1024]),
                            after=[mp["sig"]])
                        mp["gelu"] = s.add("vector", lambda e: e.scalar_tensor_tensor(
                            out=hb[:, 0:1024], in0=ar1[:, 1024:2048], scalar=1.0,
                            in1=ar2[:, 1024:2048], op0=ALU.add, op1=ALU.mult),
                            after=[mp["mulglu"], mp["tsclip"]] + ([mp["tfn"]] if mp["tfn"] else []))
                emit_tr_w2_acc(TB - 1)
                if ex + 1 < E:
                    m_wld = s.add("sync", lambda e, ex1=ex + 1: [
                        e.dma_start(out=w1_sb.ap().rearrange("p (c j) -> p c j", j=H0),
                                    in_=wbf1.rearrange("(ee c p) j -> ee p c j", ee=E, p=128)[ex1]),
                        e.dma_start(out=w2_sb.ap().rearrange("p (c j) -> p c j", j=1024),
                                    in_=wbf2.rearrange("(ee c p) j -> ee p c j", ee=E, p=128)[ex1]),
                    ], dma_n=2, after=[mp["w1"], mp["w2"]], stream="d_wld")
                    m_brow = s.add("gpsimd", lambda e, ex1=ex + 1: [
                        e.dma_start(out=b1e[:, :], in_=dr["b1"][ex1:ex1 + 1, :]),
                        e.dma_start(out=b2e[:, :], in_=dr["b2"][ex1:ex1 + 1, :]),
                    ], dma_n=2, after=[mp["w1"], mp["w2"]], stream="d_brow")
                    if kind == 0:
                        m_grow = s.add("sync", lambda e, ex1=ex + 1: [
                            e.dma_start(out=grow_a[:, :], in_=dr["ln1_g"][ex1:ex1 + 1, :]),
                            e.dma_start(out=grow_b[:, :], in_=dr["ln1_b"][ex1:ex1 + 1, :])],
                            dma_n=2, after=[m_gbb], stream="d_grow")
            s.barrier()

            for t in range(TB):
                r0 = (t0 + t) * 128
                s.add("sync", lambda e, r0=r0: [e.dma_start(out=xs[:, :], in_=xin[r0:r0 + 128, :])], dma_n=1)
                if kind == 0:
                    emit_ln_bn(ctx, lambda lo, w: xs[:, lo:lo + w], 1024, BNB, "prev")
                    s.add("vector", lambda e: e.tensor_scalar(
                        out=ar1[:, 0:1024], in0=xs[:, :], scalar1=bnmv[:, 0:1], scalar2=rstd[:, :],
                        op0=ALU.subtract, op1=ALU.mult))
                    s.add("vector", lambda e, t=t: e.tensor_add(
                        ar1[:, 1024:2048], acc[:, t * 1024:(t + 1) * 1024], ar1[:, 0:1024]))
                    emit_ln_bn(ctx, lambda lo, w: ar1[:, 1024 + lo:1024 + lo + w], 1024, BNB, "prev")
                    s.add("scalar", lambda e: e.activation(ar2[:, 0:1024], ar1[:, 1024:2048], AF.Identity,
                                                           bias=nmr[:, :], scale=rstd[:, :]))
                    s.add("vector", lambda e: e.tensor_mul(ar2[:, 1024:2048], ar2[:, 0:1024], lngb[:, :]))
                    s.add("gpsimd", lambda e: e.tensor_add(ar1[:, 0:1024], ar2[:, 1024:2048], lnbb[:, :]))
                    s.add("vector", lambda e: e.tensor_add(ar1[:, 1024:2048], ar1[:, 0:1024], xs[:, :]))
                else:
                    s.add("vector", lambda e, t=t: e.tensor_add(
                        ar1[:, 1024:2048], acc[:, t * 1024:(t + 1) * 1024], xs[:, :]))
                s.add("gpsimd", lambda e, r0=r0: [e.dma_start(out=xout[r0:r0 + 128, :], in_=ar1[:, 1024:2048])], dma_n=1)
            s.barrier()
        s.flush()



def host_tables():
    pos = np.arange(S, dtype=np.float64)[:, None]
    i = np.arange(32, dtype=np.float64)
    f0 = 10000.0 ** (2.0 * i / 64.0)
    a0 = pos / f0[None, :]
    ch0 = np.tile(np.cos(a0), (1, 16)).astype(np.float32)
    sh0 = np.tile(np.sin(a0), (1, 16)).astype(np.float32)
    cs, ss = [], []
    for sc in (1.0, 4.0, 16.0):
        f = 1.0 / (10000.0 * sc) ** (2.0 * i / 64.0)
        a = pos * f[None, :]
        cs.append(np.cos(a))
        ss.append(np.sin(a))
    ch1 = np.tile(np.mean(cs, axis=0), (1, 16)).astype(np.float32)
    sh1 = np.tile(np.mean(ss, axis=0), (1, 16)).astype(np.float32)
    return ch0, sh0, ch1, sh1


def build_stage_program(which):
    nc = bass.Bass("TRN2", target_bir_lowering=False)
    xin = nc.dram_tensor("xin", [ROWS, D], F32, kind="ExternalInput").ap()
    xout = nc.dram_tensor("xout", [ROWS, D], F32, kind="ExternalOutput").ap()
    specs = [("idn", (128, 128), BF16)]
    if which in ("a0", "all"):
        specs += [(n, (D, D), F32) for n in ("a0_wq", "a0_wk", "a0_wv", "a0_wo")]
        specs += [(n, (1, D), F32) for n in ("a0_bq", "a0_bk", "a0_bv", "a0_bo")]
        specs += [("ch0", (S, 512), F32), ("sh0", (S, 512), F32)]
    if which in ("f0", "all"):
        specs += [("f0_gate_w", (D, E), F32), ("f0_w1", (E * D, H0), F32),
                  ("f0_b1", (E, H0), F32), ("f0_ln1_g", (E, H0), F32),
                  ("f0_ln1_b", (E, H0), F32), ("f0_w2", (E * H0, D), F32),
                  ("f0_b2", (E, D), F32), ("f0_ln_g", (1, D), F32), ("f0_ln_b", (1, D), F32)]
    if which in ("a1", "all"):
        specs += [(n, (D, D), F32) for n in ("a1_wq", "a1_wk", "a1_wv", "a1_wo")]
        specs += [("ch1", (S, 512), F32), ("sh1", (S, 512), F32)]
    if which in ("f1", "all"):
        specs += [("f1_gate_w", (D, E), F32), ("f1_rbias", (1, E), F32),
                  ("f1_w1", (E * D, H0), F32), ("f1_b1", (E, H0), F32),
                  ("f1_w2", (E * I1, D), F32), ("f1_b2", (E, D), F32)]
    dr = {}
    for name, shape, dt in specs:
        dr[name] = nc.dram_tensor(name, list(shape), dt, kind="ExternalInput").ap()

    with ExitStack() as stack:
        ctx = Ctx(nc, stack)
        ctx.init_items(dr["idn"])
        ctx.s.flush()
        if which == "all":
            x1 = nc.dram_tensor("x1", [ROWS, D], F32).ap()
            x2 = nc.dram_tensor("x2", [ROWS, D], F32).ap()
            x3 = nc.dram_tensor("x3", [ROWS, D], F32).ap()
            io = {"a0": (xin, x1), "f0": (x1, x2), "a1": (x2, x3), "f1": (x3, xout)}
        else:
            io = {which: (xin, xout)}
        if which in ("a0", "all"):
            stage_attn(ctx, io["a0"][0], io["a0"][1],
                       dict(wq=dr["a0_wq"], wk=dr["a0_wk"], wv=dr["a0_wv"], wo=dr["a0_wo"],
                            bq=dr["a0_bq"], bk=dr["a0_bk"], bv=dr["a0_bv"], bo=dr["a0_bo"]),
                       True, dr["ch0"], dr["sh0"], "a0")
        if which in ("f0", "all"):
            stage_ffn(ctx, io["f0"][0], io["f0"][1],
                      dict(gate_w=dr["f0_gate_w"], w1=dr["f0_w1"], b1=dr["f0_b1"],
                           ln1_g=dr["f0_ln1_g"], ln1_b=dr["f0_ln1_b"], w2=dr["f0_w2"],
                           b2=dr["f0_b2"], ln_g=dr["f0_ln_g"], ln_b=dr["f0_ln_b"]),
                      0, "f0")
        if which in ("a1", "all"):
            stage_attn(ctx, io["a1"][0], io["a1"][1],
                       dict(wq=dr["a1_wq"], wk=dr["a1_wk"], wv=dr["a1_wv"], wo=dr["a1_wo"]),
                       False, dr["ch1"], dr["sh1"], "a1")
        if which in ("f1", "all"):
            stage_ffn(ctx, io["f1"][0], io["f1"][1],
                      dict(gate_w=dr["f1_gate_w"], router_bias=dr["f1_rbias"],
                           w1=dr["f1_w1"], b1=dr["f1_b1"], w2=dr["f1_w2"], b2=dr["f1_b2"]),
                      1, "f1")
        ctx.s.add("vector", lambda e: e.engine_nop())
        ctx.s.flush()
    return nc


def _prep_host(inputs):
    ch0, sh0, ch1, sh1 = host_tables()
    c = lambda a: np.ascontiguousarray(np.asarray(a), dtype=np.float32) if not isinstance(a, np.ndarray) or a.dtype != np.float32 or not a.flags.c_contiguous else a
    def cc(a):
        return np.ascontiguousarray(np.asarray(a, dtype=np.float32))
    g = {
        "idn": np.eye(128, dtype=ml_dtypes.bfloat16),
        "ch0": ch0, "sh0": sh0, "ch1": ch1, "sh1": sh1,
        "a0_wq": cc(np.asarray(inputs["L0_attn_wq"]).T),
        "a0_wk": cc(np.asarray(inputs["L0_attn_wk"]).T),
        "a0_wv": cc(np.asarray(inputs["L0_attn_wv"]).T),
        "a0_wo": cc(np.asarray(inputs["L0_attn_wo"]).T),
        "a0_bq": cc(inputs["L0_attn_bq"]).reshape(1, D),
        "a0_bk": cc(inputs["L0_attn_bk"]).reshape(1, D),
        "a0_bv": cc(inputs["L0_attn_bv"]).reshape(1, D),
        "a0_bo": cc(inputs["L0_attn_bo"]).reshape(1, D),
        "f0_gate_w": cc(inputs["L0_ffn_gate_w"]),
        "f0_w1": cc(inputs["L0_ffn_w1"]).reshape(E * D, H0),
        "f0_b1": cc(inputs["L0_ffn_b1"]),
        "f0_ln1_g": cc(inputs["L0_ffn_ln1_g"]),
        "f0_ln1_b": cc(inputs["L0_ffn_ln1_b"]),
        "f0_w2": cc(inputs["L0_ffn_w2"]).reshape(E * H0, D),
        "f0_b2": cc(inputs["L0_ffn_b2"]),
        "f0_ln_g": cc(inputs["L0_ffn_ln_g"]).reshape(1, D),
        "f0_ln_b": cc(inputs["L0_ffn_ln_b"]).reshape(1, D),
        "a1_wq": cc(inputs["L1_attn_Wq"]),
        "a1_wk": cc(inputs["L1_attn_Wk"]),
        "a1_wv": cc(inputs["L1_attn_Wv"]),
        "a1_wo": cc(inputs["L1_attn_Wo"]),
        "f1_gate_w": cc(np.asarray(inputs["L1_ffn_router_weight"]).T),
        "f1_rbias": cc(inputs["L1_ffn_router_bias"]).reshape(1, E),
        "f1_w1": cc(inputs["L1_ffn_gate_up_proj"]).reshape(E * D, 2 * I1),
        "f1_b1": cc(inputs["L1_ffn_gate_up_bias"]),
        "f1_w2": cc(inputs["L1_ffn_down_proj"]).reshape(E * I1, D),
        "f1_b2": cc(inputs["L1_ffn_down_bias"]),
    }
    return g


_STAGE_KEYS = {
    "a0": ["idn", "a0_wq", "a0_wk", "a0_wv", "a0_wo", "a0_bq", "a0_bk", "a0_bv",
           "a0_bo", "ch0", "sh0"],
    "f0": ["idn", "f0_gate_w", "f0_w1", "f0_b1", "f0_ln1_g", "f0_ln1_b", "f0_w2",
           "f0_b2", "f0_ln_g", "f0_ln_b"],
    "a1": ["idn", "a1_wq", "a1_wk", "a1_wv", "a1_wo", "ch1", "sh1"],
    "f1": ["idn", "f1_gate_w", "f1_rbias", "f1_w1", "f1_b1", "f1_w2", "f1_b2"],
}


FUSED = True


def kernel(**inputs):
    g = _prep_host(inputs)
    x = np.ascontiguousarray(np.asarray(inputs["L0_attn_x"], dtype=np.float32)
                             ).reshape(BATCH * S, D)
    shards = [np.ascontiguousarray(x[c * ROWS:(c + 1) * ROWS]) for c in range(NCORES)]

    stages = ("all",) if FUSED else ("a0", "f0", "a1", "f1")
    for which in stages:
        nc = build_stage_program(which)
        keys = (set().union(*_STAGE_KEYS.values()) if which == "all"
                else set(_STAGE_KEYS[which]))
        in_maps = []
        for c in range(NCORES):
            m = {k: g[k] for k in keys}
            m["xin"] = shards[c]
            in_maps.append(m)
        res = run_bass_kernel_spmd(nc, in_maps, core_ids=list(range(NCORES)))
        shards = [np.ascontiguousarray(np.asarray(r["xout"], dtype=np.float32))
                  for r in res.results]

    out = np.concatenate(shards, axis=0).reshape(BATCH, S, D)
    return out.astype(np.float32)
